# version 13
# baseline (speedup 1.0000x reference)
import os as _os
import numpy as np
import ml_dtypes
_P3BL = int(_os.environ.get("P3BL", "9"))
_TRC = _os.environ.get("TRC", "dve")
from contextlib import ExitStack

import concourse.bass as bass
import concourse.mybir as mybir
from concourse.bass_utils import run_bass_kernel_spmd

F32 = mybir.dt.float32
F16 = mybir.dt.float16
BF16 = mybir.dt.bfloat16
AF = mybir.ActivationFunctionType
ALU = mybir.AluOpType
AX = mybir.AxisListType

SCALE = float(128) ** -0.5
EPS = 1e-5
NE = 8
RK = 16
NR = 4

CFG_FULL = dict(NDCH=16, NQB=16)
CFG_MINI = dict(NDCH=4, NQB=4)


def shapes(cfg):
    NDCH, NQB = cfg["NDCH"], cfg["NQB"]
    D = NDCH * 128
    T = NQB * 128
    NH = NDCH
    NKC = NR * NQB
    NQG = max(1, T // 512)
    NOB = max(1, D // 512)
    return NDCH, NQB, D, T, NH, NKC, NQG, NOB



class FakeInstr:
    def then_inc(self, *a, **k):
        return self


class FakeEngine:
    def __getattr__(self, name):
        def f(*a, **k):
            return FakeInstr()
        return f


class Ctx:

    def __init__(self):
        self.emit = False
        self.marks = {}
        self.cnt = {}
        self.handles = {}

    def inc(self, sem, instr=None, key=None, k=1):
        n = self.cnt.get(sem, 0) + k
        self.cnt[sem] = n
        if key is not None:
            kk = (sem, key)
            if self.emit:
                assert self.marks.get(kk) == n, f"mark drift {kk}: {self.marks.get(kk)} vs {n}"
            else:
                self.marks[kk] = n
        if self.emit and instr is not None and not isinstance(instr, FakeInstr):
            instr.then_inc(self.handles[sem], k)
        return n

    def mark(self, sem, key):
        n = self.cnt.get(sem, 0)
        kk = (sem, key)
        if self.emit:
            assert self.marks.get(kk) == n, f"mark drift {kk}"
        else:
            self.marks[kk] = n
        return n

    def wait(self, eng, sem, key):
        kk = (sem, key)
        if not self.emit:
            return
        if kk not in self.marks:
            raise KeyError(f"wait on unknown mark {kk}")
        eng.wait_ge(self.handles[sem], self.marks[kk])

    def wait_val(self, eng, sem, val):
        if self.emit:
            eng.wait_ge(self.handles[sem], val)


def AP3(t, off, *dims):
    return bass.AP(t, off, [list(d) for d in dims])



def build(cfg, phases=99):
    NDCH, NQB, D, T, NH, NKC, NQG, NOB = shapes(cfg)
    HPO = min(4, NH)
    OBW = HPO * 128
    QPG = 512 if T >= 512 else T
    qpo = NDCH // NOB

    nc = bass.Bass("TRN2", target_bir_lowering=False)

    xfm_d = nc.dram_tensor("xfm", [D, T], F32, kind="ExternalInput")
    xtm_d = nc.dram_tensor("xtm", [T, D], F32, kind="ExternalInput")
    wqT_d = nc.dram_tensor("wqT", [D, D], F32, kind="ExternalInput")
    wkT_d = nc.dram_tensor("wkT", [D, D], F32, kind="ExternalInput")
    NOBW = max(1, D // 512) * 128
    wvT_d = nc.dram_tensor("wvT", [NOBW, NDCH * 512], F32, kind="ExternalInput")
    woT_d = nc.dram_tensor("woT", [NOBW, NDCH * 512], F32, kind="ExternalInput")
    wupT_d = nc.dram_tensor("wupT", [NOBW, NDCH * 512], F32, kind="ExternalInput")
    wdnT_d = nc.dram_tensor("wdnT", [NOBW, NDCH * 512], F32, kind="ExternalInput")
    wex_d = nc.dram_tensor("wex", [NOBW, NDCH * 512], F32, kind="ExternalInput")
    wgT_d = nc.dram_tensor("wgT", [D, NE], F32, kind="ExternalInput")
    ballT_d = nc.dram_tensor("ballT", [D, 80], F32, kind="ExternalInput")
    aqT_d = nc.dram_tensor("aqT", [RK, D], F32, kind="ExternalInput")
    akT_d = nc.dram_tensor("akT", [RK, D], F32, kind="ExternalInput")
    avT_d = nc.dram_tensor("avT", [RK, D], F32, kind="ExternalInput")
    aoT_d = nc.dram_tensor("aoT", [RK, D], F32, kind="ExternalInput")
    boT_d = nc.dram_tensor("boT", [D, RK], F32, kind="ExternalInput")
    bqr_d = nc.dram_tensor("bqr", [128, NDCH], F32, kind="ExternalInput")
    bkr_d = nc.dram_tensor("bkr", [128, NDCH], F32, kind="ExternalInput")
    bor_d = nc.dram_tensor("bor", [1, D], F32, kind="ExternalInput")
    bvr_d = nc.dram_tensor("bvr", [1, D], F32, kind="ExternalInput")
    idb_d = nc.dram_tensor("idb", [128, 128], BF16, kind="ExternalInput")
    out_d = nc.dram_tensor("out", [T, D], F32, kind="ExternalOutput")
    Kc_d = nc.dram_tensor("Kc", [D, T], BF16)
    Vc_d = nc.dram_tensor("Vc", [128, NH * T], BF16)
    Qc_d = nc.dram_tensor("Qc", [D, T], BF16)
    Kg_d = nc.dram_tensor("Kg", [NR * D, T], BF16)
    Vg_d = nc.dram_tensor("Vg", [NR * 128, NH * T], BF16)
    opre_d = nc.dram_tensor("opre", [T, D], F16)
    z1tm_d = nc.dram_tensor("z1tm", [T, D], F16)
    s2pre_d = nc.dram_tensor("s2pre", [T, D], F16)

    C = Ctx()

    def run_phase(fns):
        C.emit = False
        snap = dict(C.cnt)
        for f in fns.values():
            f(FakeEngine(), C)
        C.emit = True
        C.cnt = snap
        with nc.Block() as blk:
            if "sync" in fns:
                @blk.sync
                def _(sync, f=fns["sync"]):
                    f(sync, C)
            if "pe" in fns:
                @blk.tensor
                def _(tensor, f=fns["pe"]):
                    f(tensor, C)
            if "act" in fns:
                @blk.scalar
                def _(scalar, f=fns["act"]):
                    f(scalar, C)
            if "dve" in fns:
                @blk.vector
                def _(vector, f=fns["dve"]):
                    f(vector, C)
            if "pool" in fns:
                @blk.gpsimd
                def _(gpsimd, f=fns["pool"]):
                    f(gpsimd, C)
        C.emit = False

    with ExitStack() as ST:
        DMASEMS = ["ms", "cst", "ast", "xs0", "xs1", "wc0", "wc1", "wh1",
                   "ko0", "ko1", "vo0", "vo1", "qo0", "qo1",
                   "qh0", "qh1", "kh0", "kh1", "vh0", "vh1",
                   "wh3", "oo0", "oo1", "op0", "op1", "xr0", "xr1", "z1o",
                   "wh4", "wh1b", "wh3b", "wh4b", "sg0", "sg1", "ss0", "ss1", "zs0", "zs1", "fo0", "fo1"]
        for s in ["pe", "act", "dve", "pool", "ccK", "ccV"] + DMASEMS:
            C.handles[s] = ST.enter_context(nc.semaphore(s))

        idb = ST.enter_context(nc.sbuf_tensor("idb_s", [128, 128], BF16))
        onescol = ST.enter_context(nc.sbuf_tensor("onescol", [128, 1], F32))
        onesrow = ST.enter_context(nc.sbuf_tensor("onesrow", [1, 128], F32))
        bqr = ST.enter_context(nc.sbuf_tensor("bqr_s", [128, NDCH], F32))
        bkr = ST.enter_context(nc.sbuf_tensor("bkr_s", [128, NDCH], F32))
        wgT = ST.enter_context(nc.sbuf_tensor("wgT_s", [128, NDCH * NE], BF16))
        s_all = ST.enter_context(nc.sbuf_tensor("s_all", [128, NQB], F32))
        glogt = ST.enter_context(nc.sbuf_tensor("glogt", [128, NQB * NE], F32))
        bna = ST.enter_context(nc.sbuf_tensor("bna", [128, 2], F32))
        NLC = max(1, D // 512)
        bns = ST.enter_context(nc.sbuf_tensor("bns", [128, 6 * NLC], F32))
        sd1 = ST.enter_context(nc.sbuf_tensor("sd1", [128, 1], F32))
        rstd = ST.enter_context(nc.sbuf_tensor("rstd", [128, 1], F32))
        l1t = ST.enter_context(nc.sbuf_tensor("l1t", [128, 1], F32))
        l2t = ST.enter_context(nc.sbuf_tensor("l2t", [128, 1], F32))
        ddt = ST.enter_context(nc.sbuf_tensor("ddt", [128, 1], F32))
        sg1 = ST.enter_context(nc.sbuf_tensor("sg1", [128, 1], F32))
        sg2 = ST.enter_context(nc.sbuf_tensor("sg2", [128, 1], F32))
        eqm = ST.enter_context(nc.sbuf_tensor("eqm", [128, NE], F32))
        epst = ST.enter_context(nc.sbuf_tensor("epst", [128, 1], F32))
        negbig = ST.enter_context(nc.sbuf_tensor("negbig", [128, 1], F32))
        mskt = ST.enter_context(nc.sbuf_tensor("mskt", [128, NE], F32))

        def wslice(buf, i, w):
            return buf[:, i * w:(i + 1) * w]

        with ExitStack() as SB:
            bvb = SB.enter_context(nc.sbuf_tensor("bvb", [128, D], F32))
            bob = SB.enter_context(nc.sbuf_tensor("bob", [128, D], F32))

            with ExitStack() as SA:
                xstg = [SA.enter_context(nc.sbuf_tensor(f"xstg{i}", [128, T], F32)) for i in range(2)]
                xfm = SA.enter_context(nc.sbuf_tensor("xfm_bf", [128, NDCH * T], BF16))
                wcf = [SA.enter_context(nc.sbuf_tensor(f"wcf{i}", [128, NDCH * 128], F32)) for i in range(2)]
                wcb = [SA.enter_context(nc.sbuf_tensor(f"wcb{i}", [128, NDCH * 128], BF16)) for i in range(2)]
                whs = [SA.enter_context(nc.sbuf_tensor(f"whs{i}", [128, NDCH * 128], F32)) for i in range(2)]
                wsb = SA.enter_context(nc.sbuf_tensor("wsb", [128, NDCH * 512], BF16))
                kstg = [SA.enter_context(nc.sbuf_tensor(f"kstg{i}", [128, T], BF16)) for i in range(2)]
                vstg = [SA.enter_context(nc.sbuf_tensor(f"vstg{i}", [128, 512], BF16)) for i in range(2)]
                cst = SA.enter_context(nc.sbuf_tensor("cst", [128, max(NDCH * 80, D)], F32))
                ast = SA.enter_context(nc.sbuf_tensor("ast", [RK, D], F32))
                ballT = SA.enter_context(nc.sbuf_tensor("ballT_s", [128, NDCH * 80], BF16))
                aT = {}
                for nm in ("aq", "ak", "av"):
                    aT[nm] = SA.enter_context(nc.sbuf_tensor(nm + "T_s", [RK, D], BF16))
                u_q = SA.enter_context(nc.sbuf_tensor("u_q", [RK, T], BF16))
                u_k = SA.enter_context(nc.sbuf_tensor("u_k", [RK, T], BF16))
                u_v = SA.enter_context(nc.sbuf_tensor("u_v", [RK, T], BF16))
                pp = [SA.enter_context(nc.psum_tensor(f"pp{i}", [128, 512], F32)) for i in range(2)]
                kp = [SA.enter_context(nc.psum_tensor(f"kp{i}", [128, 512], F32)) for i in range(2)]
                vp = [SA.enter_context(nc.psum_tensor(f"vp{i}", [128, 512], F32)) for i in range(2)]

                def p0_sync(e, C):
                    CW = max(NDCH * 80, D)
                    for (dst, srct, key) in ((idb, idb_d, "idb"), (bqr, bqr_d, "bqr"),
                                             (bkr, bkr_d, "bkr")):
                        C.inc("ms", e.dma_start(dst[:, :], srct[:, :]), key, 16)
                    C.mark("ms", "ms5")
                    C.inc("cst", e.dma_start(
                        AP3(cst, 0, (CW, 128), (80, NDCH), (1, 80)),
                        AP3(ballT_d, 0, (80, 128), (128 * 80, NDCH), (1, 80))), "ball", 16)
                    for i, nm in enumerate(("aq", "ak", "av")):
                        if i > 0:
                            C.wait(e, "pool", ("acast", i - 1))
                        C.inc("ast", e.dma_start(ast[:, :], {"aq": aqT_d, "ak": akT_d,
                                                             "av": avT_d}[nm][:, :]),
                              ("adma", i), 16)
                    C.wait(e, "pool", "ballcast")
                    C.inc("cst", e.dma_start(
                        AP3(cst, 0, (CW, 128), (NE, NDCH), (1, NE)),
                        AP3(wgT_d, 0, (NE, 128), (128 * NE, NDCH), (1, NE))), "wgdma", 16)
                    C.wait(e, "pool", "wgcast")
                    C.inc("cst", e.dma_start(cst[0:1, 0:D], bvr_d[0:1, :]), "bvrow", 16)
                    C.wait(e, "pe", ("bvmm", NOB - 1))
                    C.inc("cst", e.dma_start(cst[0:1, 0:D], bor_d[0:1, :]), "borow", 16)
                    for i in range(NDCH):
                        if i >= 2:
                            C.wait(e, "pool", ("xcast", i - 2))
                        C.inc("xs%d" % (i % 2), e.dma_start(xstg[i % 2][:, :],
                                                            xfm_d[i * 128:(i + 1) * 128, :]),
                              ("xdma", i), 16)
                    for s in ("ms", "cst", "ast", "xs0", "xs1"):
                        C.mark(s, "p0all")
                        C.wait(e, s, "p0all")

                def p0_pool(e, C):
                    CW = max(NDCH * 80, D)
                    e.memset(onescol[:, :], 1.0)
                    e.memset(epst[:, :], EPS)
                    e.memset(negbig[:, :], -1e30)
                    C.inc("pool", e.memset(onesrow[:, :], 1.0), "ones")
                    C.wait(e, "cst", "ball")
                    C.inc("pool", e.tensor_copy(
                        AP3(ballT, 0, (NDCH * 80, 128), (80, NDCH), (1, 80)),
                        AP3(cst, 0, (CW, 128), (80, NDCH), (1, 80))), "ballcast")
                    for i, nm in enumerate(("aq", "ak", "av")):
                        C.wait(e, "ast", ("adma", i))
                        C.inc("pool", e.tensor_copy(aT[nm][:, :], ast[:, :]), ("acast", i))
                    C.wait(e, "cst", "wgdma")
                    C.inc("pool", e.tensor_copy(
                        AP3(wgT, 0, (NDCH * NE, 128), (NE, NDCH), (1, NE)),
                        AP3(cst, 0, (CW, 128), (NE, NDCH), (1, NE))), "wgcast")
                    for i in range(NDCH):
                        C.wait(e, "xs%d" % (i % 2), ("xdma", i))
                        C.inc("pool", e.tensor_copy(xfm[:, i * T:(i + 1) * T], xstg[i % 2][:, :]),
                              ("xcast", i))

                def p0_pe(e, C):
                    C.wait(e, "pool", "ones")
                    C.wait(e, "ms", "ms5")
                    w = min(512, D)
                    C.wait(e, "cst", "bvrow")
                    for j in range(NOB):
                        C.inc("pe", e.matmul(pp[0][:, 0:w], onesrow[:, :],
                                             cst[0:1, j * w:(j + 1) * w], start=True, stop=True),
                              ("bvmm", j))
                        C.wait(e, "dve", ("bvc", j))
                    C.wait(e, "cst", "borow")
                    for j in range(NOB):
                        C.inc("pe", e.matmul(pp[1][:, 0:w], onesrow[:, :],
                                             cst[0:1, j * w:(j + 1) * w], start=True, stop=True),
                              ("bomm", j))
                        C.wait(e, "dve", ("boc", j))

                def p0_dve(e, C):
                    w = min(512, D)
                    for j in range(NOB):
                        C.wait(e, "pe", ("bvmm", j))
                        C.inc("dve", e.tensor_copy(bvb[:, j * w:(j + 1) * w], pp[0][:, 0:w]), ("bvc", j))
                    for j in range(NOB):
                        C.wait(e, "pe", ("bomm", j))
                        C.inc("dve", e.tensor_copy(bob[:, j * w:(j + 1) * w], pp[1][:, 0:w]), ("boc", j))

                if phases >= 1:
                    run_phase(dict(sync=p0_sync, pool=p0_pool, pe=p0_pe, dve=p0_dve))

                def p1_sync(e, C):
                    for oc in range(NDCH):
                        if oc >= 2:
                            C.wait(e, "pool", ("wkcast", oc - 2))
                        C.inc("wc%d" % (oc % 2), e.dma_start(
                            wcf[oc % 2][:, :], wkT_d[oc * 128:(oc + 1) * 128, :]), ("wkdma", oc), 16)
                        if oc > 0:
                            C.wait(e, "act", ("kevac", oc - 1, NQG - 1))
                            C.inc("ko%d" % ((oc - 1) % 2), e.dma_start(Kc_d[(oc - 1) * 128:oc * 128, :],
                                                      kstg[(oc - 1) % 2][:, :]), ("kout", oc - 1), 16)
                    C.wait(e, "act", ("kevac", NDCH - 1, NQG - 1))
                    C.inc("ko%d" % ((NDCH - 1) % 2), e.dma_start(Kc_d[(NDCH - 1) * 128:NDCH * 128, :],
                                              kstg[(NDCH - 1) % 2][:, :]), ("kout", NDCH - 1), 16)
                    for ob in range(NOB):
                        if ob > 0:
                            for qb in range(NQB):
                                C.wait(e, "dve", ("vevac", ob - 1, qb))
                                h0 = (ob - 1) * HPO
                                C.inc("vo%d" % (((ob - 1) * NQB + qb) % 2), e.dma_start(
                                    AP3(Vc_d, h0 * T + qb * 128, (NH * T, 128), (T, HPO), (1, 128)),
                                    AP3(vstg[((ob - 1) * NQB + qb) % 2], 0, (512, 128), (128, HPO), (1, 128))),
                                    ("vout", ob - 1, qb), 16)
                        for hh in range(4):
                            gi = ob * 4 + hh
                            if gi >= 2:
                                C.wait(e, "dve", ("wvcast", (gi - 2) // 4, (gi - 2) % 4))
                            PW = NDCH * 128
                            C.inc("wh1" if gi % 2 == 0 else "wh1b", e.dma_start(
                                whs[gi % 2][:, :],
                                wvT_d[ob * 128:(ob + 1) * 128, hh * PW:(hh + 1) * PW]),
                                ("wvdma", ob, hh), 16)
                    for qb in range(NQB):
                        C.wait(e, "dve", ("vevac", NOB - 1, qb))
                        h0 = (NOB - 1) * HPO
                        C.inc("vo%d" % (((NOB - 1) * NQB + qb) % 2), e.dma_start(
                            AP3(Vc_d, h0 * T + qb * 128, (NH * T, 128), (T, HPO), (1, 128)),
                            AP3(vstg[((NOB - 1) * NQB + qb) % 2], 0, (512, 128), (128, HPO), (1, 128))),
                            ("vout", NOB - 1, qb), 16)
                    for oc in range(NDCH):
                        if oc < 2:
                            C.wait(e, "pool", ("wkcast", NDCH - 1))
                        else:
                            C.wait(e, "dve", ("wqcast", oc - 2))
                        C.inc("wc%d" % (oc % 2), e.dma_start(
                            wcf[oc % 2][:, :], wqT_d[oc * 128:(oc + 1) * 128, :]), ("wqdma", oc), 16)
                        if oc > 0:
                            C.wait(e, "act", ("qevac", oc - 1, NQG - 1))
                            C.inc("qo%d" % ((oc - 1) % 2), e.dma_start(Qc_d[(oc - 1) * 128:oc * 128, :],
                                                                       kstg[(oc - 1) % 2][:, :]),
                                  ("qout", oc - 1), 16)
                    C.wait(e, "act", ("qevac", NDCH - 1, NQG - 1))
                    C.inc("qo%d" % ((NDCH - 1) % 2), e.dma_start(Qc_d[(NDCH - 1) * 128:NDCH * 128, :],
                                                                 kstg[(NDCH - 1) % 2][:, :]),
                          ("qout", NDCH - 1), 16)
                    for s in ("wc0", "wc1", "wh1", "wh1b", "ko0", "ko1", "vo0", "vo1", "qo0", "qo1"):
                        C.mark(s, "p1all")
                        C.wait(e, s, "p1all")

                def p1_pool(e, C):
                    for oc in range(NDCH):
                        C.wait(e, "wc%d" % (oc % 2), ("wkdma", oc))
                        if oc >= 2:
                            C.wait(e, "pe", ("kmm", oc - 2, NQG - 1))
                        C.inc("pool", e.tensor_copy(wcb[oc % 2][:, :], wcf[oc % 2][:, :]), ("wkcast", oc))
                    C.wait(e, "ko0", ("kout", NDCH - 2))
                    C.wait(e, "ko1", ("kout", NDCH - 1))
                    C.inc("ccK", e.collective_compute(
                        "AllGather", ALU.bypass,
                        replica_groups=[[0, 1, 2, 3], [4, 5, 6, 7]],
                        ins=[Kc_d.ap()], outs=[Kg_d.ap()]), "done", 1)
                    iv0 = (NOB - 1) * NQB + NQB - 1
                    if iv0 >= 1:
                        C.wait(e, "vo%d" % ((iv0 - 1) % 2), ("vout", (iv0 - 1) // NQB, (iv0 - 1) % NQB))
                    C.wait(e, "vo%d" % (iv0 % 2), ("vout", NOB - 1, NQB - 1))
                    C.inc("ccV", e.collective_compute(
                        "AllGather", ALU.bypass,
                        replica_groups=[[0, 1, 2, 3], [4, 5, 6, 7]],
                        ins=[Vc_d.ap()], outs=[Vg_d.ap()]), "done", 1)

                def p1_pe(e, C):
                    for qg in range(NQG):
                        if qg >= 2:
                            C.wait(e, "act", ("uevac", qg - 2))
                        for i in range(NDCH):
                            ii = e.matmul(pp[qg % 2][0:80, 0:QPG], ballT[:, i * 80:(i + 1) * 80],
                                          xfm[:, i * T + qg * QPG: i * T + (qg + 1) * QPG],
                                          start=(i == 0), stop=(i == NDCH - 1))
                        C.inc("pe", ii, ("umm", qg))
                    for oc in range(NDCH):
                        C.wait(e, "pool", ("wkcast", oc))
                        for qg in range(NQG):
                            idx = oc * NQG + qg
                            if idx >= 2:
                                C.wait(e, "act", ("kevac", (idx - 2) // NQG, (idx - 2) % NQG))
                            elif NQG == 1 and oc == 1:
                                pass
                            if idx < 2:
                                C.wait(e, "act", ("uevac", NQG - 1))
                            for i in range(NDCH):
                                e.matmul(kp[idx % 2][:, 0:QPG], wslice(wcb[oc % 2], i, 128),
                                         xfm[:, i * T + qg * QPG: i * T + (qg + 1) * QPG],
                                         start=(i == 0), stop=False)
                            ii = e.matmul(kp[idx % 2][:, 0:QPG], aT["ak"][:, oc * 128:(oc + 1) * 128],
                                          u_k[:, qg * QPG:(qg + 1) * QPG], start=False, stop=True)
                            C.inc("pe", ii, ("kmm", oc, qg))
                    for ob in range(NOB):
                        C.wait(e, "dve", ("wvcast", ob, 3))
                        for qb in range(NQB):
                            idx = ob * NQB + qb
                            if idx >= 2:
                                C.wait(e, "dve", ("vevac", (idx - 2) // NQB, (idx - 2) % NQB))
                            for i in range(NDCH):
                                e.matmul(vp[idx % 2][:, 0:512], xfm[:, i * T + qb * 128: i * T + (qb + 1) * 128],
                                         wslice(wsb, i, 512), start=(i == 0), stop=False)
                            ii = e.matmul(vp[idx % 2][:, 0:512], u_v[:, qb * 128:(qb + 1) * 128],
                                          aT["av"][:, ob * OBW:(ob + 1) * OBW], start=False, stop=True)
                            C.inc("pe", ii, ("vmm", ob, qb))
                    for oc in range(NDCH):
                        C.wait(e, "dve", ("wqcast", oc))
                        for qg in range(NQG):
                            idx = oc * NQG + qg
                            if idx >= 2:
                                C.wait(e, "act", ("qevac", (idx - 2) // NQG, (idx - 2) % NQG))
                            else:
                                C.wait(e, "act", ("kevac", NDCH - 1, NQG - 1))
                            for i in range(NDCH):
                                e.matmul(kp[idx % 2][:, 0:QPG], wslice(wcb[oc % 2], i, 128),
                                         xfm[:, i * T + qg * QPG: i * T + (qg + 1) * QPG],
                                         start=(i == 0), stop=False)
                            ii = e.matmul(kp[idx % 2][:, 0:QPG], aT["aq"][:, oc * 128:(oc + 1) * 128],
                                          u_q[:, qg * QPG:(qg + 1) * QPG], start=False, stop=True)
                            C.inc("pe", ii, ("qmm", oc, qg))

                def p1_act(e, C):
                    for qg in range(NQG):
                        C.wait(e, "pe", ("umm", qg))
                        e.activation(u_q[:, qg * QPG:(qg + 1) * QPG], pp[qg % 2][0:16, 0:QPG], AF.Copy, scale=0.5)
                        e.activation(u_k[:, qg * QPG:(qg + 1) * QPG], pp[qg % 2][32:48, 0:QPG], AF.Copy, scale=0.5)
                        ii = e.activation(u_v[:, qg * QPG:(qg + 1) * QPG], pp[qg % 2][64:80, 0:QPG], AF.Copy, scale=0.5)
                        C.inc("act", ii, ("uevac", qg))
                    C.mark("act", ("uevac", NQG - 1))
                    for oc in range(NDCH):
                        for qg in range(NQG):
                            C.wait(e, "pe", ("kmm", oc, qg))
                            if qg == 0 and oc >= 2:
                                C.wait(e, "ko%d" % (oc % 2), ("kout", oc - 2))
                            ii = e.activation(kstg[oc % 2][:, qg * QPG:(qg + 1) * QPG],
                                              kp[(oc * NQG + qg) % 2][:, 0:QPG], AF.Identity,
                                              bias=bkr[:, oc:oc + 1])
                            C.inc("act", ii, ("kevac", oc, qg))
                    for oc in range(NDCH):
                        for qg in range(NQG):
                            C.wait(e, "pe", ("qmm", oc, qg))
                            if qg == 0:
                                if oc >= 2:
                                    C.wait(e, "qo%d" % (oc % 2), ("qout", oc - 2))
                                else:
                                    C.wait(e, "ko%d" % (oc % 2), ("kout", NDCH - 2 + (oc % 2)))
                            ii = e.activation(kstg[oc % 2][:, qg * QPG:(qg + 1) * QPG],
                                              kp[(oc * NQG + qg) % 2][:, 0:QPG], AF.Identity,
                                              bias=bqr[:, oc:oc + 1])
                            C.inc("act", ii, ("qevac", oc, qg))

                def p1_dve(e, C):
                    for ob in range(NOB):
                        for hh in range(4):
                            gi = ob * 4 + hh
                            C.wait(e, "wh1" if gi % 2 == 0 else "wh1b", ("wvdma", ob, hh))
                            if ob > 0 and hh == 0:
                                C.wait(e, "pe", ("vmm", ob - 1, NQB - 1))
                            PW = NDCH * 128
                            C.inc("dve", e.tensor_copy(
                                wsb[:, hh * PW:(hh + 1) * PW], whs[gi % 2][:, :]),
                                ("wvcast", ob, hh))
                        for qb in range(NQB):
                            idx = ob * NQB + qb
                            C.wait(e, "pe", ("vmm", ob, qb))
                            if idx >= 2:
                                C.wait(e, "vo%d" % (idx % 2), ("vout", (idx - 2) // NQB, (idx - 2) % NQB))
                            ii = e.tensor_tensor(vstg[idx % 2][:, :], vp[idx % 2][:, 0:512],
                                                 bvb[:, ob * OBW:(ob + 1) * OBW], ALU.add)
                            C.inc("dve", ii, ("vevac", ob, qb))
                    for oc in range(NDCH):
                        C.wait(e, "wc%d" % (oc % 2), ("wqdma", oc))
                        if oc >= 2:
                            C.wait(e, "pe", ("qmm", oc - 2, NQG - 1))
                        else:
                            C.wait(e, "pe", ("kmm", NDCH - 2 + oc, NQG - 1))
                        C.inc("dve", e.tensor_copy(wcb[oc % 2][:, :], wcf[oc % 2][:, :]), ("wqcast", oc))

                if phases >= 2:
                    run_phase(dict(sync=p1_sync, pool=p1_pool, pe=p1_pe, act=p1_act, dve=p1_dve))

            with ExitStack() as SC:
                attn = SC.enter_context(nc.sbuf_tensor("attn", [128, NH * T], BF16))

                with ExitStack() as SD:
                    qh = [SD.enter_context(nc.sbuf_tensor(f"qh{i}", [128, T], BF16)) for i in range(2)]
                    kh = [SD.enter_context(nc.sbuf_tensor(f"kh{i}", [128, NR * T], BF16)) for i in range(2)]
                    vh = [SD.enter_context(nc.sbuf_tensor(f"vh{i}", [128, NR * T], BF16)) for i in range(2)]
                    pt = [SD.enter_context(nc.sbuf_tensor(f"pt{i}", [128, T], BF16)) for i in range(4)]
                    acc = SD.enter_context(nc.sbuf_tensor("acc", [128, T], F32))
                    pairb = SD.enter_context(nc.sbuf_tensor("pairb", [128, T], BF16))
                    rb = SD.enter_context(nc.sbuf_tensor("rb", [128, T], F32))
                    ssum = SD.enter_context(nc.sbuf_tensor("ssum", [1, T], F32))
                    srec = SD.enter_context(nc.sbuf_tensor("srec", [1, T], F32))
                    GRPW = min(1024, T)
                    stp = [SD.enter_context(nc.psum_tensor(f"stp{i}", [128, GRPW], F32)) for i in range(2)]
                    avp = SD.enter_context(nc.psum_tensor("avp", [128, T], F32))

                    GROUPS = []
                    for g in range((NQG + 1) // 2):
                        GROUPS.append((g, [q for q in (2 * g, 2 * g + 1) if q < NQG]))
                    LASTG = GROUPS[-1][0]
                    SPAN = NQG * QPG

                    def p2_sync(e, C):
                        for h in range(NH):
                            if h == 0:
                                C.wait_val(e, "ccK", 1)
                                C.wait_val(e, "ccV", 1)
                            if h >= 2:
                                C.wait(e, "pe", ("hpe", h - 2))
                            C.inc("qh%d" % (h % 2), e.dma_start(qh[h % 2][:, :],
                                                                 Qc_d[h * 128:(h + 1) * 128, :]), ("q", h), 16)
                            for r in range(NR):
                                C.inc("kh%d" % (h % 2), e.dma_start(kh[h % 2][:, r * T:(r + 1) * T],
                                                         Kg_d[r * D + h * 128: r * D + (h + 1) * 128, :]),
                                      ("k", h) if r == NR - 1 else None, 16)
                            for r in range(NR):
                                C.inc("vh%d" % (h % 2), e.dma_start(vh[h % 2][:, r * T:(r + 1) * T],
                                                         Vg_d[r * 128:(r + 1) * 128, h * T:(h + 1) * T]),
                                      ("v", h) if r == NR - 1 else None, 16)
                        for s in ("qh0", "qh1", "kh0", "kh1", "vh0", "vh1"):
                            C.mark(s, "p2all")
                            C.wait(e, s, "p2all")

                    def p2_pe(e, C):
                        for h in range(NH):
                            C.wait(e, "qh%d" % (h % 2), ("q", h))
                            C.wait(e, "kh%d" % (h % 2), ("k", h))
                            C.wait(e, "vh%d" % (h % 2), ("v", h))
                            if h > 0:
                                C.wait(e, "dve", ("rb", h - 1))
                                C.wait(e, "dve", ("norm", h - 1))
                            for kc in range(NKC):
                                for (g, qgs) in GROUPS:
                                    if kc > 0:
                                        C.wait(e, "act", ("exp", h, kc - 1, g))
                                    for j, qg in enumerate(qgs):
                                        ii = e.matmul(stp[g][:, j * QPG:(j + 1) * QPG],
                                                      kh[h % 2][:, kc * 128:(kc + 1) * 128],
                                                      qh[h % 2][:, qg * QPG:(qg + 1) * QPG],
                                                      start=True, stop=True)
                                    C.inc("pe", ii, ("qk", h, kc, g))
                                if kc > 0:
                                    kcp = kc - 1
                                    for qg in range(NQG):
                                        ii = e.matmul(avp[:, qg * QPG:(qg + 1) * QPG],
                                                      vh[h % 2][:, kcp * 128:(kcp + 1) * 128],
                                                      pt[kcp % 4][:, qg * QPG:(qg + 1) * QPG],
                                                      start=(kcp == 0), stop=False)
                                    C.inc("pe", ii, ("av", h, kcp))
                            kcp = NKC - 1
                            C.wait(e, "act", ("exp", h, kcp, LASTG))
                            for qg in range(NQG):
                                ii = e.matmul(avp[:, qg * QPG:(qg + 1) * QPG],
                                              vh[h % 2][:, kcp * 128:(kcp + 1) * 128],
                                              pt[kcp % 4][:, qg * QPG:(qg + 1) * QPG],
                                              start=(kcp == 0), stop=True)
                            C.inc("pe", ii, ("av", h, kcp))
                            C.wait(e, "dve", ("sum", h, NKC - 1))
                            for qg in range(NQG):
                                g, j = qg // 2, qg % 2
                                ii = e.matmul(stp[g][0:1, j * QPG:(j + 1) * QPG], onescol[:, :],
                                              acc[:, qg * QPG:(qg + 1) * QPG], start=True, stop=True)
                            C.inc("pe", ii, ("summ", h))
                            C.wait(e, "act", ("recip", h))
                            for qg in range(NQG):
                                g, j = qg // 2, qg % 2
                                ii = e.matmul(stp[g][:, j * QPG:(j + 1) * QPG], onesrow[:, :],
                                              srec[0:1, qg * QPG:(qg + 1) * QPG], start=True, stop=True)
                            C.inc("pe", ii, ("bcmm", h))
                            C.mark("pe", ("hpe", h))

                    def p2_act(e, C):
                        for h in range(NH):
                            for kc in range(NKC):
                                if kc >= 4:
                                    C.wait(e, "pe", ("av", h, kc - 4))
                                    j = kc - 3 if (kc - 3) % 2 == 1 else kc - 4
                                    if j >= 1:
                                        C.wait(e, "dve", ("sum", h, j))
                                elif h > 0:
                                    C.wait(e, "pe", ("av", h - 1, NKC - 1))
                                    C.wait(e, "dve", ("sum", h - 1, NKC - 1))
                                for (g, qgs) in GROUPS:
                                    C.wait(e, "pe", ("qk", h, kc, g))
                                    span = len(qgs) * QPG
                                    ii = e.activation(pt[kc % 4][:, 2 * g * QPG: 2 * g * QPG + span],
                                                      stp[g][:, 0:span], AF.Exp, scale=SCALE)
                                    C.inc("act", ii, ("exp", h, kc, g))
                            C.wait(e, "pe", ("summ", h))
                            for qg in range(NQG):
                                g, j = qg // 2, qg % 2
                                ii = e.copy(ssum[0:1, qg * QPG:(qg + 1) * QPG],
                                            stp[g][0:1, j * QPG:(j + 1) * QPG])
                            C.inc("act", ii, ("ssum", h))
                            C.wait(e, "act", ("ssum", h))
                            ii = e.activation(srec[0:1, :], ssum[0:1, :], AF.Ln)
                            C.inc("act", ii, ("lns", h))
                            C.wait(e, "act", ("lns", h))
                            ii = e.activation(srec[0:1, :], srec[0:1, :], AF.Exp, scale=-1.0)
                            C.inc("act", ii, ("recip", h))

                    def p2_dve(e, C):
                        sl = slice(0, SPAN)
                        for h in range(NH):
                            for kc in range(1, NKC, 2):
                                C.wait(e, "act", ("exp", h, kc, LASTG))
                                if kc >= 3:
                                    C.wait(e, "dve", ("sum", h, kc - 2))
                                ii = e.tensor_tensor(pairb[:, sl], pt[(kc - 1) % 4][:, sl],
                                                     pt[kc % 4][:, sl], ALU.add)
                                C.inc("dve", ii, ("pair", h, kc))
                                C.wait(e, "dve", ("pair", h, kc))
                                if kc == 1:
                                    ii = e.tensor_copy(acc[:, sl], pairb[:, sl])
                                else:
                                    ii = e.tensor_tensor(acc[:, sl], acc[:, sl], pairb[:, sl], ALU.add)
                                C.inc("dve", ii, ("sum", h, kc))
                            C.wait(e, "pe", ("bcmm", h))
                            for qg in range(NQG):
                                g, j = qg // 2, qg % 2
                                ii = e.tensor_copy(rb[:, qg * QPG:(qg + 1) * QPG],
                                                   stp[g][:, j * QPG:(j + 1) * QPG])
                            C.inc("dve", ii, ("rb", h))
                            C.wait(e, "dve", ("rb", h))
                            C.wait(e, "pe", ("av", h, NKC - 1))
                            for qg in range(NQG):
                                sl = slice(qg * QPG, (qg + 1) * QPG)
                                ii = e.tensor_tensor(attn[:, h * T + qg * QPG: h * T + (qg + 1) * QPG],
                                                     avp[:, sl], rb[:, sl], ALU.mult)
                            C.inc("dve", ii, ("norm", h))

                    if phases >= 3:
                        run_phase(dict(sync=p2_sync, pe=p2_pe, act=p2_act, dve=p2_dve))

                with ExitStack() as SE:
                    whs3 = [SE.enter_context(nc.sbuf_tensor(f"whs3{i}", [128, NDCH * 128], F32)) for i in range(2)]
                    wsb3 = SE.enter_context(nc.sbuf_tensor("wsb3", [128, NDCH * 512], BF16))
                    aoT = SE.enter_context(nc.sbuf_tensor("aoT_s", [RK, D], BF16))
                    boT = SE.enter_context(nc.sbuf_tensor("boT_s", [128, NDCH * RK], BF16))
                    u_o = SE.enter_context(nc.sbuf_tensor("u_o", [RK, T], BF16))
                    ostg = [SE.enter_context(nc.sbuf_tensor(f"ostg{i}", [128, 512], F16)) for i in range(2)]
                    po = [SE.enter_context(nc.psum_tensor(f"po{i}", [128, 512], F32)) for i in range(2)]
                    puo = SE.enter_context(nc.psum_tensor("puo", [128, 512], F32))

                    def p3a_sync(e, C):
                        C.inc("wh3", e.dma_start(whs3[0][0:RK, 0:D], aoT_d[:, :]), "aodma", 16)
                        C.wait(e, "pool", "aocast")
                        C.inc("wh3", e.dma_start(
                            AP3(whs3[0], 0, (NDCH * 128, 128), (128, NDCH), (1, RK)),
                            AP3(boT_d, 0, (RK, 128), (128 * RK, NDCH), (1, RK))), "bodma", 16)
                        for ob in range(NOB):
                            if ob > 0:
                                for qb in range(NQB):
                                    C.wait(e, "dve", ("oevac", ob - 1, qb))
                                    C.inc("oo%d" % (((ob - 1) * NQB + qb) % 2), e.dma_start(
                                        opre_d[qb * 128:(qb + 1) * 128, (ob - 1) * OBW: ob * OBW],
                                        ostg[((ob - 1) * NQB + qb) % 2][:, 0:OBW]), ("oout", ob - 1, qb), 16)
                            for hh in range(4):
                                gi = ob * 4 + hh
                                if gi >= 2:
                                    C.wait(e, "dve", ("wocast", (gi - 2) // 4, (gi - 2) % 4))
                                elif gi == 0:
                                    C.wait(e, "pool", "bocast")
                                PW = NDCH * 128
                                C.inc("wh3" if gi % 2 == 0 else "wh3b", e.dma_start(
                                    whs3[gi % 2][:, :],
                                    woT_d[ob * 128:(ob + 1) * 128, hh * PW:(hh + 1) * PW]),
                                    ("wodma", ob, hh), 16)
                        for qb in range(NQB):
                            C.wait(e, "dve", ("oevac", NOB - 1, qb))
                            C.inc("oo%d" % (((NOB - 1) * NQB + qb) % 2), e.dma_start(
                                opre_d[qb * 128:(qb + 1) * 128, (NOB - 1) * OBW: NOB * OBW],
                                ostg[((NOB - 1) * NQB + qb) % 2][:, 0:OBW]), ("oout", NOB - 1, qb), 16)
                        for s in ("wh3", "wh3b", "oo0", "oo1"):
                            C.mark(s, "p3aall")
                            C.wait(e, s, "p3aall")

                    def p3a_pool(e, C):
                        C.wait(e, "wh3", "aodma")
                        C.inc("pool", e.tensor_copy(aoT[:, :], whs3[0][0:RK, 0:D]), "aocast")
                        C.wait(e, "wh3", "bodma")
                        C.inc("pool", e.tensor_copy(
                            AP3(boT, 0, (NDCH * RK, 128), (RK, NDCH), (1, RK)),
                            AP3(whs3[0], 0, (NDCH * 128, 128), (128, NDCH), (1, RK))), "bocast")

                    def p3a_pe(e, C):
                        C.wait(e, "pool", "bocast")
                        for qg in range(NQG):
                            if qg >= 1:
                                C.wait(e, "act", ("uoevac", qg - 1))
                            for i in range(NDCH):
                                ii = e.matmul(puo[0:RK, 0:QPG], boT[:, i * RK:(i + 1) * RK],
                                              attn[:, i * T + qg * QPG: i * T + (qg + 1) * QPG],
                                              start=(i == 0), stop=(i == NDCH - 1))
                            C.inc("pe", ii, ("uomm", qg))
                        for ob in range(NOB):
                            C.wait(e, "dve", ("wocast", ob, 3))
                            for qb in range(NQB):
                                idx = ob * NQB + qb
                                if idx >= 2:
                                    C.wait(e, "dve", ("oevac", (idx - 2) // NQB, (idx - 2) % NQB))
                                elif idx == 0:
                                    C.wait(e, "act", ("uoevac", NQG - 1))
                                for i in range(NDCH):
                                    e.matmul(po[idx % 2][:, 0:512],
                                             attn[:, i * T + qb * 128: i * T + (qb + 1) * 128],
                                             wslice(wsb3, i, 512), start=(i == 0), stop=False)
                                ii = e.matmul(po[idx % 2][:, 0:512], u_o[:, qb * 128:(qb + 1) * 128],
                                              aoT[:, ob * OBW:(ob + 1) * OBW], start=False, stop=True)
                                C.inc("pe", ii, ("omm", ob, qb))

                    def p3a_act(e, C):
                        for qg in range(NQG):
                            C.wait(e, "pe", ("uomm", qg))
                            ii = e.activation(u_o[:, qg * QPG:(qg + 1) * QPG], puo[0:RK, 0:QPG],
                                              AF.Copy, scale=0.5)
                            C.inc("act", ii, ("uoevac", qg))

                    def p3a_dve(e, C):
                        for ob in range(NOB):
                            for hh in range(4):
                                gi = ob * 4 + hh
                                C.wait(e, "wh3" if gi % 2 == 0 else "wh3b", ("wodma", ob, hh))
                                if ob > 0 and hh == 0:
                                    C.wait(e, "pe", ("omm", ob - 1, NQB - 1))
                                PW = NDCH * 128
                                C.inc("dve", e.tensor_copy(
                                    wsb3[:, hh * PW:(hh + 1) * PW], whs3[gi % 2][:, :]),
                                    ("wocast", ob, hh))
                            for qb in range(NQB):
                                idx = ob * NQB + qb
                                C.wait(e, "pe", ("omm", ob, qb))
                                if idx >= 2:
                                    C.wait(e, "oo%d" % (idx % 2), ("oout", (idx - 2) // NQB, (idx - 2) % NQB))
                                ii = e.tensor_tensor(ostg[idx % 2][:, 0:OBW], po[idx % 2][:, 0:512],
                                                     bob[:, ob * OBW:(ob + 1) * OBW], ALU.add)
                                C.inc("dve", ii, ("oevac", ob, qb))

                    if phases >= 4:
                        run_phase(dict(sync=p3a_sync, pool=p3a_pool, pe=p3a_pe, act=p3a_act, dve=p3a_dve))

        with ExitStack() as SF:
            z1fm = SF.enter_context(nc.sbuf_tensor("z1fm", [128, NDCH * T], BF16))
            ROT = 8 if NDCH >= 8 else NDCH

            with ExitStack() as SG:
                opstr = [SG.enter_context(nc.sbuf_tensor(f"opstr{i}", [128, D], F16)) for i in range(2)]
                xstr = [SG.enter_context(nc.sbuf_tensor(f"xstr{i}", [128, D], F32)) for i in range(2)]
                s1 = SG.enter_context(nc.sbuf_tensor("s1", [128, D], F32))
                z1h = SG.enter_context(nc.sbuf_tensor("z1h", [128, D], F16))
                z1b = SG.enter_context(nc.sbuf_tensor("z1b", [128, D], BF16))
                trp = [SG.enter_context(nc.psum_tensor(f"trp{i}", [128, 512], BF16)) for i in range(2)]

                NG4 = NDCH // 4 if NDCH >= 4 else 1

                def tr_slot(qb, i):
                    g = qb * NG4 + i // 4
                    return trp[g % 2][:, (i % 4) * 128:((i % 4) + 1) * 128]

                def p3b_sync(e, C):
                    for qb in range(NQB):
                        if qb >= 2 and _P3BL >= 2:
                            C.wait(e, "dve", ("s1c", qb - 2))
                        C.inc("op%d" % (qb % 2), e.dma_start(opstr[qb % 2][:, :],
                                                              opre_d[qb * 128:(qb + 1) * 128, :]),
                              ("opin", qb), 16)
                        C.inc("xr%d" % (qb % 2), e.dma_start(xstr[qb % 2][:, :],
                                                             xtm_d[qb * 128:(qb + 1) * 128, :]),
                              ("xin", qb), 16)
                        if _P3BL >= 3:
                            C.wait(e, "dve", ("z1hc", qb))
                            C.inc("z1o", e.dma_start(z1tm_d[qb * 128:(qb + 1) * 128, :], z1h[:, :]),
                                  ("z1out", qb), 16)
                    for s in (("op0", "op1", "xr0", "xr1", "z1o") if _P3BL >= 3 else ("op0", "op1", "xr0", "xr1")):
                        C.mark(s, "p3ball")
                        C.wait(e, s, "p3ball")

                def p3b_act(e, C):
                    if _P3BL < 2:
                        return
                    for qb in range(NQB):
                        C.wait(e, "dve", ("bnagg", qb))
                        C.inc("act", e.activation(sd1[:, :], bna[:, 1:2], AF.Sqrt, bias=epst[:, :]), ("sqrt", qb))
                        C.wait(e, "dve", ("z1hc", qb))
                        if qb > 0 and _P3BL >= 4:
                            C.wait(e, "pe", ("trg", qb - 1, NDCH // 4 - 1 if NDCH >= 4 else 0))
                        C.inc("act", e.activation(z1b[:, :], z1h[:, :], AF.Copy), ("z1bc", qb))
                        if _P3BL < 5:
                            continue
                        for i in range(NDCH):
                            if i % 2 == 0 and _TRC != "act":
                                continue
                            if i % 2 == 1 and _TRC == "dve":
                                continue
                            C.wait(e, "pe", ("trg", qb, i // 4))
                            C.inc("act", e.copy(z1fm[:, i * T + qb * 128: i * T + (qb + 1) * 128],
                                                tr_slot(qb, i)), ("trc", qb, i))

                def p3b_dve(e, C):
                    if _P3BL < 2:
                        return
                    for qb in range(NQB):
                        C.wait(e, "op%d" % (qb % 2), ("opin", qb))
                        C.wait(e, "xr%d" % (qb % 2), ("xin", qb))
                        if qb > 0:
                            C.wait(e, "dve", ("z1hc", qb - 1))
                        C.inc("dve", e.tensor_tensor(s1[:, :], opstr[qb % 2][:, :], xstr[qb % 2][:, :],
                                                     ALU.add), ("s1c", qb))
                        C.wait(e, "dve", ("s1c", qb))
                        for j in range(NLC):
                            ii = e.bn_stats(bns[:, 6 * j:6 * (j + 1)], s1[:, 512 * j:512 * (j + 1)])
                        C.inc("dve", ii, ("bnst", qb))
                        C.wait(e, "dve", ("bnst", qb))
                        C.inc("dve", e.bn_aggr(bna[:, :], bns[:, :]), ("bnagg", qb))
                        C.wait(e, "act", ("sqrt", qb))
                        C.inc("dve", e.reciprocal(rstd[:, :], sd1[:, :]), ("rstd", qb))
                        C.wait(e, "dve", ("rstd", qb))
                        if qb > 0 and _P3BL >= 3:
                            C.wait(e, "z1o", ("z1out", qb - 1))
                        if qb > 0:
                            C.wait(e, "act", ("z1bc", qb - 1))
                        C.inc("dve", e.tensor_scalar(z1h[:, :], s1[:, :], bna[:, 0:1], rstd[:, :],
                                                     ALU.subtract, ALU.mult), ("z1hc", qb))
                        if _P3BL < 5:
                            continue
                        for i in range(NDCH):
                            if i % 2 == 1 and _TRC != "dve":
                                continue
                            if i % 2 == 0 and _TRC == "act":
                                continue
                            C.wait(e, "pe", ("trg", qb, i // 4))
                            C.inc("dve", e.tensor_copy(z1fm[:, i * T + qb * 128: i * T + (qb + 1) * 128],
                                                       tr_slot(qb, i)), ("trc", qb, i))

                def p3b_pe(e, C):
                    if _P3BL < 4:
                        return
                    for qb in range(NQB):
                        C.wait(e, "act", ("z1bc", qb))
                        for g4 in range(NG4):
                            pg = qb * NG4 + g4 - 1
                            if pg >= 0 and _P3BL >= 5:
                                pqb, pg4 = divmod(pg, NG4)
                                for i in range(pg4 * 4, pg4 * 4 + 4):
                                    eng = "act" if (_TRC == "act" or (_TRC == "mix" and i % 2 == 1)) else "dve"
                                    C.wait(e, eng, ("trc", pqb, i))
                            for i in range(g4 * 4, g4 * 4 + 4):
                                ii = e.transpose(tr_slot(qb, i), z1b[:, i * 128:(i + 1) * 128], idb[:, :])
                            C.inc("pe", ii, ("trg", qb, g4))

                if phases >= 5:
                    run_phase(dict(sync=p3b_sync, act=p3b_act, dve=p3b_dve, pe=p3b_pe))

            with ExitStack() as SH:
                hfm = SH.enter_context(nc.sbuf_tensor("hfm", [128, NDCH * T], BF16))
                whs4 = [SH.enter_context(nc.sbuf_tensor(f"whs4{i}", [128, NDCH * 64], F32)) for i in range(2)]
                web = SH.enter_context(nc.sbuf_tensor("web", [128, NDCH * 512], BF16))
                wdb = SH.enter_context(nc.sbuf_tensor("wdb", [128, NDCH * 512], BF16))
                wub = web
                glog = SH.enter_context(nc.sbuf_tensor("glog", [NE, T], BF16))
                t1 = SH.enter_context(nc.sbuf_tensor("t1", [128, 512], F32))
                s2stg = [SH.enter_context(nc.sbuf_tensor(f"s2stg{i}", [128, 512], F16)) for i in range(2)]
                s2str = SH.enter_context(nc.sbuf_tensor("s2str", [128, D], F16))
                z1str = SH.enter_context(nc.sbuf_tensor("z1str", [128, D], F16))
                s2f = SH.enter_context(nc.sbuf_tensor("s2f", [128, D], F32))
                ostg4 = SH.enter_context(nc.sbuf_tensor("ostg4", [128, D], F32))
                up_ps = [SH.enter_context(nc.psum_tensor(f"up_ps{i}", [128, 512], F32)) for i in range(2)]
                ep_ps = [SH.enter_context(nc.psum_tensor(f"ep_ps{i}", [128, 512], F32)) for i in range(2)]
                dp_ps = [SH.enter_context(nc.psum_tensor(f"dp_ps{i}", [128, 512], F32)) for i in range(2)]
                gt_ps = SH.enter_context(nc.psum_tensor("gt_ps", [128, NE * 4], BF16))

                SEQ4 = [("wu", ob, hh) for ob in range(NOB) for hh in range(8)] + \
                       [(nm, ob, hh) for ob in range(NOB) for nm in ("we", "wd") for hh in range(8)]
                SRC4 = {"wu": wupT_d, "we": wex_d, "wd": wdnT_d}
                DST4 = {"wu": wub, "we": web, "wd": wdb}
                IDX4 = {k: i for i, k in enumerate(SEQ4)}

                def p4_sync(e, C):
                    def wpiece(key):
                        i = IDX4[key]
                        nm, ob, hh = key
                        if i >= 2:
                            C.wait(e, "pool", ("wcast4", i - 2))
                        PW = NDCH * 64
                        C.inc("wh4" if i % 2 == 0 else "wh4b", e.dma_start(
                            whs4[i % 2][:, :],
                            SRC4[nm][ob * 128:(ob + 1) * 128, hh * PW:(hh + 1) * PW]),
                            ("wdma4", i), 16)
                    for ob in range(NOB):
                        for hh in range(8):
                            wpiece(("wu", ob, hh))
                    for ob in range(NOB):
                        if ob > 0:
                            for qb in range(NQB):
                                C.wait(e, "dve", ("s2evac", ob - 1, qb))
                                C.inc("sg%d" % (((ob - 1) * NQB + qb) % 2), e.dma_start(
                                    s2pre_d[qb * 128:(qb + 1) * 128, (ob - 1) * OBW: ob * OBW],
                                    s2stg[((ob - 1) * NQB + qb) % 2][:, 0:OBW]), ("s2out", ob - 1, qb), 16)
                        for nm in ("we", "wd"):
                            for hh in range(8):
                                wpiece((nm, ob, hh))
                    for qb in range(NQB):
                        C.wait(e, "dve", ("s2evac", NOB - 1, qb))
                        C.inc("sg%d" % (((NOB - 1) * NQB + qb) % 2), e.dma_start(
                            s2pre_d[qb * 128:(qb + 1) * 128, (NOB - 1) * OBW: NOB * OBW],
                            s2stg[((NOB - 1) * NQB + qb) % 2][:, 0:OBW]), ("s2out", NOB - 1, qb), 16)
                        if qb >= 1:
                            C.wait(e, "dve", ("s2f", qb - 1))
                        C.wait(e, "sg%d" % (((NOB - 1) * NQB + qb) % 2), ("s2out", NOB - 1, qb))
                        C.inc("ss0", e.dma_start(s2str[:, :],
                                                 s2pre_d[qb * 128:(qb + 1) * 128, :]),
                              ("s2in", qb), 16)
                        C.inc("zs0", e.dma_start(z1str[:, :],
                                                 z1tm_d[qb * 128:(qb + 1) * 128, :]),
                              ("z1in", qb), 16)
                        if qb >= 1:
                            C.wait(e, "dve", ("outc", qb - 1))
                            C.inc("fo0", e.dma_start(out_d[(qb - 1) * 128:qb * 128, :],
                                                     ostg4[:, :]), ("final", qb - 1), 16)
                    C.wait(e, "dve", ("outc", NQB - 1))
                    C.inc("fo0", e.dma_start(out_d[(NQB - 1) * 128:NQB * 128, :],
                                             ostg4[:, :]), ("final", NQB - 1), 16)
                    for s in ("wh4", "wh4b", "ss0", "zs0", "fo0"):
                        C.mark(s, "p4all")
                        C.wait(e, s, "p4all")

                def p4_wcast(e, C, key):
                    i = IDX4[key]
                    nm, ob, hh = key
                    C.wait(e, "wh4" if i % 2 == 0 else "wh4b", ("wdma4", i))
                    if nm == "wu" and ob > 0 and hh == 0:
                        C.wait(e, "pe", ("upmm", ob - 1, qpo - 1, NQG - 1))
                    if nm == "we" and ob == 0 and hh == 0:
                        C.wait(e, "pe", ("upmm", NOB - 1, qpo - 1, NQG - 1))
                    if nm == "we" and ob > 0 and hh == 0:
                        C.wait(e, "pe", ("edmm", ob - 1, NQB - 1))
                    if nm == "wd" and ob > 0 and hh == 0:
                        C.wait(e, "pe", ("edmm", ob - 1, NQB - 1))
                    PW = NDCH * 64
                    C.inc("pool", e.tensor_copy(
                        DST4[nm][:, hh * PW:(hh + 1) * PW], whs4[i % 2][:, :]),
                        ("wcast4", i))

                def p4_pe(e, C):
                    for ob in range(NOB):
                        C.wait(e, "pool", ("wcast4", IDX4[("wu", ob, 7)]))
                        for j in range(qpo):
                            oc = ob * qpo + j
                            for qg in range(NQG):
                                idx = oc * NQG + qg
                                if idx >= 2:
                                    C.wait(e, "act", ("upevac", ((idx - 2) // NQG), (idx - 2) % NQG))
                                for i in range(NDCH):
                                    ii = e.matmul(up_ps[idx % 2][:, 0:QPG],
                                                  wub[:, i * 512 + (j % 4) * 128: i * 512 + ((j % 4) + 1) * 128],
                                                  z1fm[:, i * T + qg * QPG: i * T + (qg + 1) * QPG],
                                                  start=(i == 0), stop=(i == NDCH - 1))
                                C.inc("pe", ii, ("upmm", ob, j, qg))
                    for qg in range(NQG):
                        if qg == 0:
                            C.wait(e, "act", ("upevac", NDCH - 1, NQG - 1))
                        else:
                            C.wait(e, "act", ("gevac", qg - 1))
                        for i in range(NDCH):
                            ii = e.matmul(up_ps[qg % 2][0:NE, 0:QPG], wgT[:, i * NE:(i + 1) * NE],
                                          z1fm[:, i * T + qg * QPG: i * T + (qg + 1) * QPG],
                                          start=(i == 0), stop=(i == NDCH - 1))
                        C.inc("pe", ii, ("gmm", qg))
                    for qgr in range(NQB // 4):
                        C.wait(e, "act", ("gevac", ((qgr * 4 + 3) * 128) // QPG))
                        if qgr >= 1:
                            C.wait(e, "dve", ("gtc", qgr - 1))
                        for qb in range(qgr * 4, qgr * 4 + 4):
                            ii = e.transpose(gt_ps[:, (qb % 4) * NE:((qb % 4) + 1) * NE],
                                             glog[0:NE, qb * 128:(qb + 1) * 128], idb[0:NE, 0:NE])
                        C.inc("pe", ii, ("gt", qgr))
                    for ob in range(NOB):
                        C.wait(e, "pool", ("wcast4", IDX4[("wd", ob, 7)]))
                        for qb in range(NQB):
                            idx = ob * NQB + qb
                            if idx >= 2:
                                C.wait(e, "dve", ("eevac", (idx - 2) // NQB, (idx - 2) % NQB))
                                C.wait(e, "dve", ("s2evac", (idx - 2) // NQB, (idx - 2) % NQB))
                            for i in range(NDCH):
                                ii = e.matmul(ep_ps[idx % 2][:, 0:512],
                                              z1fm[:, i * T + qb * 128: i * T + (qb + 1) * 128],
                                              wslice(web, i, 512), start=(i == 0), stop=(i == NDCH - 1))
                            C.inc("pe", ii, ("emm", ob, qb))
                            for i in range(NDCH):
                                ii = e.matmul(dp_ps[idx % 2][:, 0:512],
                                              hfm[:, i * T + qb * 128: i * T + (qb + 1) * 128],
                                              wslice(wdb, i, 512), start=(i == 0), stop=(i == NDCH - 1))
                            C.inc("pe", ii, ("edmm", ob, qb))

                def p4_act(e, C):
                    for ob in range(NOB):
                        for j in range(qpo):
                            oc = ob * qpo + j
                            for qg in range(NQG):
                                C.wait(e, "pe", ("upmm", ob, j, qg))
                                ii = e.activation(hfm[:, oc * T + qg * QPG: oc * T + (qg + 1) * QPG],
                                                  up_ps[(oc * NQG + qg) % 2][:, 0:QPG], AF.Relu)
                                C.inc("act", ii, ("upevac", oc, qg))
                    for qg in range(NQG):
                        C.wait(e, "pe", ("gmm", qg))
                        ii = e.activation(glog[0:NE, qg * QPG:(qg + 1) * QPG],
                                          up_ps[qg % 2][0:NE, 0:QPG], AF.Copy)
                        C.inc("act", ii, ("gevac", qg))
                    for qb in range(NQB):
                        C.wait(e, "dve", ("dd", qb))
                        C.inc("act", e.activation(sg1[:, :], ddt[:, :], AF.Sigmoid), ("sg1", qb))
                        C.inc("act", e.activation(sg2[:, :], ddt[:, :], AF.Sigmoid, scale=-1.0), ("sg2", qb))
                    for qb in range(NQB):
                        C.wait(e, "dve", ("bnagg4", qb))
                        C.inc("act", e.activation(sd1[:, :], bna[:, 1:2], AF.Sqrt, bias=epst[:, :]), ("sqrt4", qb))

                def p4_pool(e, C):
                    for ob in range(NOB):
                        for hh in range(8):
                            p4_wcast(e, C, ("wu", ob, hh))
                    for ob in range(NOB):
                        for nm in ("we", "wd"):
                            for hh in range(8):
                                p4_wcast(e, C, (nm, ob, hh))

                def p4_dve(e, C):
                    for qgr in range(NQB // 4):
                        C.wait(e, "pe", ("gt", qgr))
                        C.inc("dve", e.tensor_copy(glogt[:, qgr * 4 * NE:(qgr + 1) * 4 * NE],
                                                   gt_ps[:, :]), ("gtc", qgr))
                    for qb in range(NQB):
                        sl = glogt[:, qb * NE:(qb + 1) * NE]
                        C.wait(e, "dve", ("gtc", qb // 4))
                        if qb > 0:
                            C.wait(e, "dve", ("sfin", qb - 1))
                        C.inc("dve", e.tensor_reduce(l1t[:, :], sl, AX.X, ALU.max), ("l1", qb))
                        C.wait(e, "dve", ("l1", qb))
                        C.inc("dve", e.tensor_scalar(eqm[:, :], sl, l1t[:, :], negbig[:, :],
                                                     ALU.is_equal, ALU.mult), ("eqm", qb))
                        C.wait(e, "dve", ("eqm", qb))
                        C.inc("dve", e.tensor_tensor(mskt[:, :], eqm[:, :], sl, ALU.add), ("msk", qb))
                        C.wait(e, "dve", ("msk", qb))
                        C.inc("dve", e.tensor_reduce(l2t[:, :], mskt[:, :], AX.X, ALU.max), ("l2", qb))
                        C.wait(e, "dve", ("l2", qb))
                        C.inc("dve", e.tensor_tensor(ddt[:, :], l1t[:, :], l2t[:, :], ALU.subtract), ("dd", qb))
                        C.wait(e, "act", ("sg1", qb))
                        C.wait(e, "act", ("sg2", qb))
                        C.inc("dve", e.tensor_tensor(s_all[:, qb:qb + 1], sg1[:, :], sg2[:, :], ALU.add),
                              ("sfin", qb))
                    for ob in range(NOB):
                        for qb in range(NQB):
                            idx = ob * NQB + qb
                            C.wait(e, "pe", ("emm", ob, qb))
                            C.wait(e, "dve", ("sfin", NQB - 1))
                            if idx > 0:
                                C.wait(e, "dve", ("s2evac", (idx - 1) // NQB, (idx - 1) % NQB))
                            C.inc("dve", e.tensor_scalar(t1[:, 0:512], ep_ps[idx % 2][:, 0:512],
                                                         s_all[:, qb:qb + 1], 0.0, ALU.mult, ALU.add),
                                  ("eevac", ob, qb))
                            C.wait(e, "pe", ("edmm", ob, qb))
                            C.wait(e, "dve", ("eevac", ob, qb))
                            if idx >= 2:
                                C.wait(e, "sg%d" % (idx % 2), ("s2out", (idx - 2) // NQB, (idx - 2) % NQB))
                            C.inc("dve", e.tensor_tensor(s2stg[idx % 2][:, 0:OBW], t1[:, 0:512],
                                                         dp_ps[idx % 2][:, 0:512], ALU.add),
                                  ("s2evac", ob, qb))
                            if ob == NOB - 1:
                                C.wait(e, "ss0", ("s2in", qb))
                                C.wait(e, "zs0", ("z1in", qb))
                                if qb > 0:
                                    C.wait(e, "dve", ("outc", qb - 1))
                                C.inc("dve", e.tensor_tensor(s2f[:, :], s2str[:, :],
                                                             z1str[:, :], ALU.add), ("s2f", qb))
                                C.wait(e, "dve", ("s2f", qb))
                                for j in range(NLC):
                                    ii = e.bn_stats(bns[:, 6 * j:6 * (j + 1)], s2f[:, 512 * j:512 * (j + 1)])
                                C.inc("dve", ii, ("bnst4", qb))
                                C.wait(e, "dve", ("bnst4", qb))
                                C.inc("dve", e.bn_aggr(bna[:, :], bns[:, :]), ("bnagg4", qb))
                                C.wait(e, "act", ("sqrt4", qb))
                                C.inc("dve", e.reciprocal(rstd[:, :], sd1[:, :]), ("rstd4", qb))
                                C.wait(e, "dve", ("rstd4", qb))
                                if qb >= 1:
                                    C.wait(e, "fo0", ("final", qb - 1))
                                C.inc("dve", e.tensor_scalar(ostg4[:, :], s2f[:, :], bna[:, 0:1],
                                                             rstd[:, :], ALU.subtract, ALU.mult), ("outc", qb))

                if phases >= 6:
                    run_phase(dict(sync=p4_sync, pool=p4_pool, pe=p4_pe, act=p4_act, dve=p4_dve))

    if phases < 6:
        with nc.sbuf_tensor("stub", [128, D], F32) as stub, nc.semaphore("stubs") as sts, nc.Block() as blk:
            @blk.gpsimd
            def _(gpsimd):
                gpsimd.memset(stub[:, :], 0.0).then_inc(sts, 1)
            @blk.sync
            def _(sync):
                sync.wait_ge(sts, 1)
                for qb in range(NQB):
                    sync.dma_start(out_d[qb * 128:(qb + 1) * 128, :], stub[:, :]).then_inc(sts, 16)
                sync.wait_ge(sts, 1 + 16 * NQB)
    nc.finalize()
    return nc



_BUILD_CACHE = {}


def _get_program(key, cfg):
    if key not in _BUILD_CACHE:
        _BUILD_CACHE[key] = build(cfg)
    return _BUILD_CACHE[key]


def make_inputs(cfg, x, Wq, bq, Bq, Aq, Wk, bk, Bk, Ak, Wv, bv, Bv, Av,
                Wo, bo, Bo, Ao, W_up, W_down, W_gate, W_expert):
    NDCH, NQB, D, T, NH, NKC, NQG, NOB = shapes(cfg)
    f32 = lambda a: np.ascontiguousarray(np.asarray(a), dtype=np.float32)
    ballT = np.zeros((D, 80), np.float32)
    ballT[:, 0:16] = np.asarray(Bq).T
    ballT[:, 32:48] = np.asarray(Bk).T
    ballT[:, 64:80] = np.asarray(Bv).T
    def tile_col(WT):
        return np.ascontiguousarray(
            np.asarray(WT).reshape(NDCH, 128, NDCH, 128).transpose(2, 1, 0, 3).reshape(D, D))

    def tile_slab(Wdo):
        return np.ascontiguousarray(
            np.asarray(Wdo).reshape(NDCH, 128, NOB, 512 if D >= 512 else D)
            .transpose(2, 1, 0, 3).reshape(NOB * 128, NDCH * (512 if D >= 512 else D)))

    shared = dict(
        wqT=tile_col(np.asarray(Wq).T), wkT=tile_col(np.asarray(Wk).T),
        wvT=tile_slab(np.asarray(Wv).T), woT=tile_slab(np.asarray(Wo).T),
        wupT=tile_slab(np.asarray(W_up).T), wdnT=tile_slab(np.asarray(W_down).T),
        wex=tile_slab(W_expert), wgT=f32(np.asarray(W_gate).T), ballT=f32(ballT),
        aqT=f32(np.asarray(Aq).T), akT=f32(np.asarray(Ak).T), avT=f32(np.asarray(Av).T),
        aoT=f32(np.asarray(Ao).T), boT=f32(np.asarray(Bo).T),
        bqr=f32(np.asarray(bq).reshape(NDCH, 128).T), bkr=f32(np.asarray(bk).reshape(NDCH, 128).T),
        bor=f32(np.asarray(bo).reshape(1, D)), bvr=f32(np.asarray(bv).reshape(1, D)),
        idb=np.eye(128, dtype=ml_dtypes.bfloat16),
    )
    in_maps = []
    for c in range(8):
        b, s = c // NR, (c % NR) * T
        xs = np.asarray(x)[b, s:s + T, :]
        m = dict(shared)
        m["xtm"] = f32(xs)
        m["xfm"] = f32(xs.T)
        in_maps.append(m)
    return in_maps


def kernel(**inputs):
    cfg = CFG_FULL
    NDCH, NQB, D, T, NH, NKC, NQG, NOB = shapes(cfg)
    x = np.asarray(inputs["L0_attn_x"], np.float32)
    args = [x]
    for p in ("Wq", "bq", "Bq", "Aq", "Wk", "bk", "Bk", "Ak", "Wv", "bv", "Bv", "Av",
              "Wo", "bo", "Bo", "Ao"):
        args.append(np.asarray(inputs["L0_attn_" + p], np.float32))
    for p in ("W_up", "W_down", "W_gate", "W_expert"):
        args.append(np.asarray(inputs["L0_ffn_" + p], np.float32))
    in_maps = make_inputs(cfg, *args)
    nc = _get_program("full", cfg)
    res = run_bass_kernel_spmd(nc, in_maps, core_ids=list(range(8)))
    out = np.empty((2, 8192, 2048), np.float32)
    for c in range(8):
        b, s = c // NR, (c % NR) * T
        out[b, s:s + T, :] = res.results[c]["out"]
    return out


# version 15
# speedup vs baseline: 1.0163x; 1.0163x over previous
import os as _os
import numpy as np
import ml_dtypes
_P3BL = int(_os.environ.get("P3BL", "9"))
_TRC = _os.environ.get("TRC", "dve")
from contextlib import ExitStack

import concourse.bass as bass
import concourse.mybir as mybir
from concourse.bass_utils import run_bass_kernel_spmd

F32 = mybir.dt.float32
F16 = mybir.dt.float16
BF16 = mybir.dt.bfloat16
AF = mybir.ActivationFunctionType
ALU = mybir.AluOpType
AX = mybir.AxisListType

SCALE = float(128) ** -0.5
EPS = 1e-5
NE = 8
RK = 16
NR = 4

CFG_FULL = dict(NDCH=16, NQB=16)
CFG_MINI = dict(NDCH=4, NQB=4)


def shapes(cfg):
    NDCH, NQB = cfg["NDCH"], cfg["NQB"]
    D = NDCH * 128
    T = NQB * 128
    NH = NDCH
    NKC = NR * NQB
    NQG = max(1, T // 512)
    NOB = max(1, D // 512)
    return NDCH, NQB, D, T, NH, NKC, NQG, NOB



class FakeInstr:
    def then_inc(self, *a, **k):
        return self


class FakeEngine:
    def __getattr__(self, name):
        def f(*a, **k):
            return FakeInstr()
        return f


class Ctx:

    def __init__(self):
        self.emit = False
        self.marks = {}
        self.cnt = {}
        self.handles = {}

    def inc(self, sem, instr=None, key=None, k=1):
        n = self.cnt.get(sem, 0) + k
        self.cnt[sem] = n
        if key is not None:
            kk = (sem, key)
            if self.emit:
                assert self.marks.get(kk) == n, f"mark drift {kk}: {self.marks.get(kk)} vs {n}"
            else:
                self.marks[kk] = n
        if self.emit and instr is not None and not isinstance(instr, FakeInstr):
            instr.then_inc(self.handles[sem], k)
        return n

    def mark(self, sem, key):
        n = self.cnt.get(sem, 0)
        kk = (sem, key)
        if self.emit:
            assert self.marks.get(kk) == n, f"mark drift {kk}"
        else:
            self.marks[kk] = n
        return n

    def wait(self, eng, sem, key):
        kk = (sem, key)
        if not self.emit:
            return
        if kk not in self.marks:
            raise KeyError(f"wait on unknown mark {kk}")
        eng.wait_ge(self.handles[sem], self.marks[kk])

    def wait_val(self, eng, sem, val):
        if self.emit:
            eng.wait_ge(self.handles[sem], val)


def AP3(t, off, *dims):
    return bass.AP(t, off, [list(d) for d in dims])



def build(cfg, phases=99):
    NDCH, NQB, D, T, NH, NKC, NQG, NOB = shapes(cfg)
    HPO = min(4, NH)
    OBW = HPO * 128
    QPG = 512 if T >= 512 else T
    qpo = NDCH // NOB

    nc = bass.Bass("TRN2", target_bir_lowering=False)

    xfm_d = nc.dram_tensor("xfm", [D, T], F32, kind="ExternalInput")
    xtm_d = nc.dram_tensor("xtm", [T, D], F32, kind="ExternalInput")
    wqT_d = nc.dram_tensor("wqT", [D, D], F32, kind="ExternalInput")
    wkT_d = nc.dram_tensor("wkT", [D, D], F32, kind="ExternalInput")
    NOBW = max(1, D // 512) * 128
    wvT_d = nc.dram_tensor("wvT", [NOBW, NDCH * 512], F32, kind="ExternalInput")
    woT_d = nc.dram_tensor("woT", [NOBW, NDCH * 512], F32, kind="ExternalInput")
    wupT_d = nc.dram_tensor("wupT", [NOBW, NDCH * 512], F32, kind="ExternalInput")
    wdnT_d = nc.dram_tensor("wdnT", [NOBW, NDCH * 512], F32, kind="ExternalInput")
    wex_d = nc.dram_tensor("wex", [NOBW, NDCH * 512], F32, kind="ExternalInput")
    wgT_d = nc.dram_tensor("wgT", [D, NE], F32, kind="ExternalInput")
    ballT_d = nc.dram_tensor("ballT", [D, 80], F32, kind="ExternalInput")
    aqT_d = nc.dram_tensor("aqT", [RK, D], F32, kind="ExternalInput")
    akT_d = nc.dram_tensor("akT", [RK, D], F32, kind="ExternalInput")
    avT_d = nc.dram_tensor("avT", [RK, D], F32, kind="ExternalInput")
    aoT_d = nc.dram_tensor("aoT", [RK, D], F32, kind="ExternalInput")
    boT_d = nc.dram_tensor("boT", [D, RK], F32, kind="ExternalInput")
    bqr_d = nc.dram_tensor("bqr", [128, NDCH], F32, kind="ExternalInput")
    bkr_d = nc.dram_tensor("bkr", [128, NDCH], F32, kind="ExternalInput")
    bor_d = nc.dram_tensor("bor", [1, D], F32, kind="ExternalInput")
    bvr_d = nc.dram_tensor("bvr", [1, D], F32, kind="ExternalInput")
    idb_d = nc.dram_tensor("idb", [128, 128], BF16, kind="ExternalInput")
    out_d = nc.dram_tensor("out", [T, D], F32, kind="ExternalOutput")
    Kc_d = nc.dram_tensor("Kc", [D, T], BF16)
    Vc_d = nc.dram_tensor("Vc", [128, NH * T], BF16)
    Qc_d = nc.dram_tensor("Qc", [D, T], BF16)
    Kg_d = nc.dram_tensor("Kg", [NR * D, T], BF16)
    Vg_d = nc.dram_tensor("Vg", [NR * 128, NH * T], BF16)
    opre_d = nc.dram_tensor("opre", [T, D], F16)
    z1tm_d = nc.dram_tensor("z1tm", [T, D], F16)
    s2pre_d = nc.dram_tensor("s2pre", [T, D], F16)

    C = Ctx()

    def run_phase(fns):
        C.emit = False
        snap = dict(C.cnt)
        for f in fns.values():
            f(FakeEngine(), C)
        C.emit = True
        C.cnt = snap
        with nc.Block() as blk:
            if "sync" in fns:
                @blk.sync
                def _(sync, f=fns["sync"]):
                    f(sync, C)
            if "pe" in fns:
                @blk.tensor
                def _(tensor, f=fns["pe"]):
                    f(tensor, C)
            if "act" in fns:
                @blk.scalar
                def _(scalar, f=fns["act"]):
                    f(scalar, C)
            if "dve" in fns:
                @blk.vector
                def _(vector, f=fns["dve"]):
                    f(vector, C)
            if "pool" in fns:
                @blk.gpsimd
                def _(gpsimd, f=fns["pool"]):
                    f(gpsimd, C)
        C.emit = False

    with ExitStack() as ST:
        DMASEMS = ["ms", "cst", "ast", "xs0", "xs1", "wc0", "wc1", "wh1",
                   "ko0", "ko1", "vo0", "vo1", "qo0", "qo1",
                   "qh0", "qh1", "kh0", "kh1", "vh0", "vh1",
                   "wh3", "oo0", "oo1", "op0", "op1", "xr0", "xr1", "z1o",
                   "wh4", "wh1b", "wh3b", "wh4b", "z1ob", "sg0", "sg1", "ss0", "ss1", "zs0", "zs1", "fo0", "fo1"]
        for s in ["pe", "act", "dve", "pool", "ccK", "ccV"] + DMASEMS:
            C.handles[s] = ST.enter_context(nc.semaphore(s))

        idb = ST.enter_context(nc.sbuf_tensor("idb_s", [128, 128], BF16))
        onescol = ST.enter_context(nc.sbuf_tensor("onescol", [128, 1], F32))
        onesrow = ST.enter_context(nc.sbuf_tensor("onesrow", [1, 128], F32))
        bqr = ST.enter_context(nc.sbuf_tensor("bqr_s", [128, NDCH], F32))
        bkr = ST.enter_context(nc.sbuf_tensor("bkr_s", [128, NDCH], F32))
        wgT = ST.enter_context(nc.sbuf_tensor("wgT_s", [128, NDCH * NE], BF16))
        s_all = ST.enter_context(nc.sbuf_tensor("s_all", [128, NQB], F32))
        glogt = ST.enter_context(nc.sbuf_tensor("glogt", [128, NQB * NE], F32))
        bna = ST.enter_context(nc.sbuf_tensor("bna", [128, 2], F32))
        NLC = max(1, D // 512)
        bns = ST.enter_context(nc.sbuf_tensor("bns", [128, 6 * NLC], F32))
        sd1 = ST.enter_context(nc.sbuf_tensor("sd1", [128, 1], F32))
        rstd = ST.enter_context(nc.sbuf_tensor("rstd", [128, 1], F32))
        l1t = ST.enter_context(nc.sbuf_tensor("l1t", [128, 1], F32))
        l2t = ST.enter_context(nc.sbuf_tensor("l2t", [128, 1], F32))
        ddt = ST.enter_context(nc.sbuf_tensor("ddt", [128, 1], F32))
        sg1 = ST.enter_context(nc.sbuf_tensor("sg1", [128, 1], F32))
        sg2 = ST.enter_context(nc.sbuf_tensor("sg2", [128, 1], F32))
        eqm = ST.enter_context(nc.sbuf_tensor("eqm", [128, NE], F32))
        epst = ST.enter_context(nc.sbuf_tensor("epst", [128, 1], F32))
        negbig = ST.enter_context(nc.sbuf_tensor("negbig", [128, 1], F32))
        mskt = ST.enter_context(nc.sbuf_tensor("mskt", [128, NE], F32))

        def wslice(buf, i, w):
            return buf[:, i * w:(i + 1) * w]

        with ExitStack() as SB:
            bvb = SB.enter_context(nc.sbuf_tensor("bvb", [128, D], F32))
            bob = SB.enter_context(nc.sbuf_tensor("bob", [128, D], F32))

            with ExitStack() as SA:
                xstg = [SA.enter_context(nc.sbuf_tensor(f"xstg{i}", [128, T], F32)) for i in range(2)]
                xfm = SA.enter_context(nc.sbuf_tensor("xfm_bf", [128, NDCH * T], BF16))
                wcf = [SA.enter_context(nc.sbuf_tensor(f"wcf{i}", [128, NDCH * 128], F32)) for i in range(2)]
                wcb = [SA.enter_context(nc.sbuf_tensor(f"wcb{i}", [128, NDCH * 128], BF16)) for i in range(2)]
                whs = [SA.enter_context(nc.sbuf_tensor(f"whs{i}", [128, NDCH * 128], F32)) for i in range(2)]
                wsb = SA.enter_context(nc.sbuf_tensor("wsb", [128, NDCH * 512], BF16))
                kstg = [SA.enter_context(nc.sbuf_tensor(f"kstg{i}", [128, T], BF16)) for i in range(2)]
                vstg = [SA.enter_context(nc.sbuf_tensor(f"vstg{i}", [128, 512], BF16)) for i in range(2)]
                cst = SA.enter_context(nc.sbuf_tensor("cst", [128, max(NDCH * 80, D)], F32))
                ast = SA.enter_context(nc.sbuf_tensor("ast", [RK, D], F32))
                ballT = SA.enter_context(nc.sbuf_tensor("ballT_s", [128, NDCH * 80], BF16))
                aT = {}
                for nm in ("aq", "ak", "av"):
                    aT[nm] = SA.enter_context(nc.sbuf_tensor(nm + "T_s", [RK, D], BF16))
                u_q = SA.enter_context(nc.sbuf_tensor("u_q", [RK, T], BF16))
                u_k = SA.enter_context(nc.sbuf_tensor("u_k", [RK, T], BF16))
                u_v = SA.enter_context(nc.sbuf_tensor("u_v", [RK, T], BF16))
                pp = [SA.enter_context(nc.psum_tensor(f"pp{i}", [128, 512], F32)) for i in range(2)]
                kp = [SA.enter_context(nc.psum_tensor(f"kp{i}", [128, 512], F32)) for i in range(2)]
                vp = [SA.enter_context(nc.psum_tensor(f"vp{i}", [128, 512], F32)) for i in range(2)]

                def p0_sync(e, C):
                    CW = max(NDCH * 80, D)
                    for (dst, srct, key) in ((idb, idb_d, "idb"), (bqr, bqr_d, "bqr"),
                                             (bkr, bkr_d, "bkr")):
                        C.inc("ms", e.dma_start(dst[:, :], srct[:, :]), key, 16)
                    C.mark("ms", "ms5")
                    C.inc("cst", e.dma_start(
                        AP3(cst, 0, (CW, 128), (80, NDCH), (1, 80)),
                        AP3(ballT_d, 0, (80, 128), (128 * 80, NDCH), (1, 80))), "ball", 16)
                    for i, nm in enumerate(("aq", "ak", "av")):
                        if i > 0:
                            C.wait(e, "pool", ("acast", i - 1))
                        C.inc("ast", e.dma_start(ast[:, :], {"aq": aqT_d, "ak": akT_d,
                                                             "av": avT_d}[nm][:, :]),
                              ("adma", i), 16)
                    C.wait(e, "pool", "ballcast")
                    C.inc("cst", e.dma_start(
                        AP3(cst, 0, (CW, 128), (NE, NDCH), (1, NE)),
                        AP3(wgT_d, 0, (NE, 128), (128 * NE, NDCH), (1, NE))), "wgdma", 16)
                    C.wait(e, "pool", "wgcast")
                    C.inc("cst", e.dma_start(cst[0:1, 0:D], bvr_d[0:1, :]), "bvrow", 16)
                    C.wait(e, "pe", ("bvmm", NOB - 1))
                    C.inc("cst", e.dma_start(cst[0:1, 0:D], bor_d[0:1, :]), "borow", 16)
                    for i in range(NDCH):
                        if i >= 2:
                            C.wait(e, "pool", ("xcast", i - 2))
                        C.inc("xs%d" % (i % 2), e.dma_start(xstg[i % 2][:, :],
                                                            xfm_d[i * 128:(i + 1) * 128, :]),
                              ("xdma", i), 16)
                    for s in ("ms", "cst", "ast", "xs0", "xs1"):
                        C.mark(s, "p0all")
                        C.wait(e, s, "p0all")

                def p0_pool(e, C):
                    CW = max(NDCH * 80, D)
                    e.memset(onescol[:, :], 1.0)
                    e.memset(epst[:, :], EPS)
                    e.memset(negbig[:, :], -1e30)
                    C.inc("pool", e.memset(onesrow[:, :], 1.0), "ones")
                    C.wait(e, "cst", "ball")
                    C.inc("pool", e.tensor_copy(
                        AP3(ballT, 0, (NDCH * 80, 128), (80, NDCH), (1, 80)),
                        AP3(cst, 0, (CW, 128), (80, NDCH), (1, 80))), "ballcast")
                    for i, nm in enumerate(("aq", "ak", "av")):
                        C.wait(e, "ast", ("adma", i))
                        C.inc("pool", e.tensor_copy(aT[nm][:, :], ast[:, :]), ("acast", i))
                    C.wait(e, "cst", "wgdma")
                    C.inc("pool", e.tensor_copy(
                        AP3(wgT, 0, (NDCH * NE, 128), (NE, NDCH), (1, NE)),
                        AP3(cst, 0, (CW, 128), (NE, NDCH), (1, NE))), "wgcast")
                    for i in range(NDCH):
                        C.wait(e, "xs%d" % (i % 2), ("xdma", i))
                        C.inc("pool", e.tensor_copy(xfm[:, i * T:(i + 1) * T], xstg[i % 2][:, :]),
                              ("xcast", i))

                def p0_pe(e, C):
                    C.wait(e, "pool", "ones")
                    C.wait(e, "ms", "ms5")
                    w = min(512, D)
                    C.wait(e, "cst", "bvrow")
                    for j in range(NOB):
                        C.inc("pe", e.matmul(pp[0][:, 0:w], onesrow[:, :],
                                             cst[0:1, j * w:(j + 1) * w], start=True, stop=True),
                              ("bvmm", j))
                        C.wait(e, "dve", ("bvc", j))
                    C.wait(e, "cst", "borow")
                    for j in range(NOB):
                        C.inc("pe", e.matmul(pp[1][:, 0:w], onesrow[:, :],
                                             cst[0:1, j * w:(j + 1) * w], start=True, stop=True),
                              ("bomm", j))
                        C.wait(e, "dve", ("boc", j))

                def p0_dve(e, C):
                    w = min(512, D)
                    for j in range(NOB):
                        C.wait(e, "pe", ("bvmm", j))
                        C.inc("dve", e.tensor_copy(bvb[:, j * w:(j + 1) * w], pp[0][:, 0:w]), ("bvc", j))
                    for j in range(NOB):
                        C.wait(e, "pe", ("bomm", j))
                        C.inc("dve", e.tensor_copy(bob[:, j * w:(j + 1) * w], pp[1][:, 0:w]), ("boc", j))

                if phases >= 1:
                    run_phase(dict(sync=p0_sync, pool=p0_pool, pe=p0_pe, dve=p0_dve))

                def p1_sync(e, C):
                    for oc in range(NDCH):
                        if oc >= 2:
                            C.wait(e, "pool", ("wkcast", oc - 2))
                        C.inc("wc%d" % (oc % 2), e.dma_start(
                            wcf[oc % 2][:, :], wkT_d[oc * 128:(oc + 1) * 128, :]), ("wkdma", oc), 16)
                        if oc > 0:
                            C.wait(e, "act", ("kevac", oc - 1, NQG - 1))
                            C.inc("ko%d" % ((oc - 1) % 2), e.dma_start(Kc_d[(oc - 1) * 128:oc * 128, :],
                                                      kstg[(oc - 1) % 2][:, :]), ("kout", oc - 1), 16)
                    C.wait(e, "act", ("kevac", NDCH - 1, NQG - 1))
                    C.inc("ko%d" % ((NDCH - 1) % 2), e.dma_start(Kc_d[(NDCH - 1) * 128:NDCH * 128, :],
                                              kstg[(NDCH - 1) % 2][:, :]), ("kout", NDCH - 1), 16)
                    for ob in range(NOB):
                        if ob > 0:
                            for qb in range(NQB):
                                C.wait(e, "dve", ("vevac", ob - 1, qb))
                                h0 = (ob - 1) * HPO
                                C.inc("vo%d" % (((ob - 1) * NQB + qb) % 2), e.dma_start(
                                    AP3(Vc_d, h0 * T + qb * 128, (NH * T, 128), (T, HPO), (1, 128)),
                                    AP3(vstg[((ob - 1) * NQB + qb) % 2], 0, (512, 128), (128, HPO), (1, 128))),
                                    ("vout", ob - 1, qb), 16)
                        for hh in range(4):
                            gi = ob * 4 + hh
                            if gi >= 2:
                                C.wait(e, "dve", ("wvcast", (gi - 2) // 4, (gi - 2) % 4))
                            PW = NDCH * 128
                            C.inc("wh1" if gi % 2 == 0 else "wh1b", e.dma_start(
                                whs[gi % 2][:, :],
                                wvT_d[ob * 128:(ob + 1) * 128, hh * PW:(hh + 1) * PW]),
                                ("wvdma", ob, hh), 16)
                    for qb in range(NQB):
                        C.wait(e, "dve", ("vevac", NOB - 1, qb))
                        h0 = (NOB - 1) * HPO
                        C.inc("vo%d" % (((NOB - 1) * NQB + qb) % 2), e.dma_start(
                            AP3(Vc_d, h0 * T + qb * 128, (NH * T, 128), (T, HPO), (1, 128)),
                            AP3(vstg[((NOB - 1) * NQB + qb) % 2], 0, (512, 128), (128, HPO), (1, 128))),
                            ("vout", NOB - 1, qb), 16)
                    for oc in range(NDCH):
                        if oc < 2:
                            C.wait(e, "pool", ("wkcast", NDCH - 1))
                        else:
                            C.wait(e, "dve", ("wqcast", oc - 2))
                        C.inc("wc%d" % (oc % 2), e.dma_start(
                            wcf[oc % 2][:, :], wqT_d[oc * 128:(oc + 1) * 128, :]), ("wqdma", oc), 16)
                        if oc > 0:
                            C.wait(e, "act", ("qevac", oc - 1, NQG - 1))
                            C.inc("qo%d" % ((oc - 1) % 2), e.dma_start(Qc_d[(oc - 1) * 128:oc * 128, :],
                                                                       kstg[(oc - 1) % 2][:, :]),
                                  ("qout", oc - 1), 16)
                    C.wait(e, "act", ("qevac", NDCH - 1, NQG - 1))
                    C.inc("qo%d" % ((NDCH - 1) % 2), e.dma_start(Qc_d[(NDCH - 1) * 128:NDCH * 128, :],
                                                                 kstg[(NDCH - 1) % 2][:, :]),
                          ("qout", NDCH - 1), 16)
                    for s in ("wc0", "wc1", "wh1", "wh1b", "ko0", "ko1", "vo0", "vo1", "qo0", "qo1"):
                        C.mark(s, "p1all")
                        C.wait(e, s, "p1all")

                def p1_pool(e, C):
                    for oc in range(NDCH):
                        C.wait(e, "wc%d" % (oc % 2), ("wkdma", oc))
                        if oc >= 2:
                            C.wait(e, "pe", ("kmm", oc - 2, NQG - 1))
                        C.inc("pool", e.tensor_copy(wcb[oc % 2][:, :], wcf[oc % 2][:, :]), ("wkcast", oc))
                    C.wait(e, "ko0", ("kout", NDCH - 2))
                    C.wait(e, "ko1", ("kout", NDCH - 1))
                    C.inc("ccK", e.collective_compute(
                        "AllGather", ALU.bypass,
                        replica_groups=[[0, 1, 2, 3], [4, 5, 6, 7]],
                        ins=[Kc_d.ap()], outs=[Kg_d.ap()]), "done", 1)
                    iv0 = (NOB - 1) * NQB + NQB - 1
                    if iv0 >= 1:
                        C.wait(e, "vo%d" % ((iv0 - 1) % 2), ("vout", (iv0 - 1) // NQB, (iv0 - 1) % NQB))
                    C.wait(e, "vo%d" % (iv0 % 2), ("vout", NOB - 1, NQB - 1))
                    C.inc("ccV", e.collective_compute(
                        "AllGather", ALU.bypass,
                        replica_groups=[[0, 1, 2, 3], [4, 5, 6, 7]],
                        ins=[Vc_d.ap()], outs=[Vg_d.ap()]), "done", 1)

                def p1_pe(e, C):
                    for qg in range(NQG):
                        if qg >= 2:
                            C.wait(e, "act", ("uevac", qg - 2))
                        for i in range(NDCH):
                            ii = e.matmul(pp[qg % 2][0:80, 0:QPG], ballT[:, i * 80:(i + 1) * 80],
                                          xfm[:, i * T + qg * QPG: i * T + (qg + 1) * QPG],
                                          start=(i == 0), stop=(i == NDCH - 1))
                        C.inc("pe", ii, ("umm", qg))
                    for oc in range(NDCH):
                        C.wait(e, "pool", ("wkcast", oc))
                        for qg in range(NQG):
                            idx = oc * NQG + qg
                            if idx >= 2:
                                C.wait(e, "act", ("kevac", (idx - 2) // NQG, (idx - 2) % NQG))
                            elif NQG == 1 and oc == 1:
                                pass
                            if idx < 2:
                                C.wait(e, "act", ("uevac", NQG - 1))
                            for i in range(NDCH):
                                e.matmul(kp[idx % 2][:, 0:QPG], wslice(wcb[oc % 2], i, 128),
                                         xfm[:, i * T + qg * QPG: i * T + (qg + 1) * QPG],
                                         start=(i == 0), stop=False)
                            ii = e.matmul(kp[idx % 2][:, 0:QPG], aT["ak"][:, oc * 128:(oc + 1) * 128],
                                          u_k[:, qg * QPG:(qg + 1) * QPG], start=False, stop=True)
                            C.inc("pe", ii, ("kmm", oc, qg))
                    for ob in range(NOB):
                        C.wait(e, "dve", ("wvcast", ob, 3))
                        for qb in range(NQB):
                            idx = ob * NQB + qb
                            if idx >= 2:
                                C.wait(e, "dve", ("vevac", (idx - 2) // NQB, (idx - 2) % NQB))
                            for i in range(NDCH):
                                e.matmul(vp[idx % 2][:, 0:512], xfm[:, i * T + qb * 128: i * T + (qb + 1) * 128],
                                         wslice(wsb, i, 512), start=(i == 0), stop=False)
                            ii = e.matmul(vp[idx % 2][:, 0:512], u_v[:, qb * 128:(qb + 1) * 128],
                                          aT["av"][:, ob * OBW:(ob + 1) * OBW], start=False, stop=True)
                            C.inc("pe", ii, ("vmm", ob, qb))
                    for oc in range(NDCH):
                        C.wait(e, "dve", ("wqcast", oc))
                        for qg in range(NQG):
                            idx = oc * NQG + qg
                            if idx >= 2:
                                C.wait(e, "act", ("qevac", (idx - 2) // NQG, (idx - 2) % NQG))
                            else:
                                C.wait(e, "act", ("kevac", NDCH - 1, NQG - 1))
                            for i in range(NDCH):
                                e.matmul(kp[idx % 2][:, 0:QPG], wslice(wcb[oc % 2], i, 128),
                                         xfm[:, i * T + qg * QPG: i * T + (qg + 1) * QPG],
                                         start=(i == 0), stop=False)
                            ii = e.matmul(kp[idx % 2][:, 0:QPG], aT["aq"][:, oc * 128:(oc + 1) * 128],
                                          u_q[:, qg * QPG:(qg + 1) * QPG], start=False, stop=True)
                            C.inc("pe", ii, ("qmm", oc, qg))

                def p1_act(e, C):
                    for qg in range(NQG):
                        C.wait(e, "pe", ("umm", qg))
                        e.activation(u_q[:, qg * QPG:(qg + 1) * QPG], pp[qg % 2][0:16, 0:QPG], AF.Copy, scale=0.5)
                        e.activation(u_k[:, qg * QPG:(qg + 1) * QPG], pp[qg % 2][32:48, 0:QPG], AF.Copy, scale=0.5)
                        ii = e.activation(u_v[:, qg * QPG:(qg + 1) * QPG], pp[qg % 2][64:80, 0:QPG], AF.Copy, scale=0.5)
                        C.inc("act", ii, ("uevac", qg))
                    C.mark("act", ("uevac", NQG - 1))
                    for oc in range(NDCH):
                        for qg in range(NQG):
                            C.wait(e, "pe", ("kmm", oc, qg))
                            if qg == 0 and oc >= 2:
                                C.wait(e, "ko%d" % (oc % 2), ("kout", oc - 2))
                            ii = e.activation(kstg[oc % 2][:, qg * QPG:(qg + 1) * QPG],
                                              kp[(oc * NQG + qg) % 2][:, 0:QPG], AF.Identity,
                                              bias=bkr[:, oc:oc + 1])
                            C.inc("act", ii, ("kevac", oc, qg))
                    for oc in range(NDCH):
                        for qg in range(NQG):
                            C.wait(e, "pe", ("qmm", oc, qg))
                            if qg == 0:
                                if oc >= 2:
                                    C.wait(e, "qo%d" % (oc % 2), ("qout", oc - 2))
                                else:
                                    C.wait(e, "ko%d" % (oc % 2), ("kout", NDCH - 2 + (oc % 2)))
                            ii = e.activation(kstg[oc % 2][:, qg * QPG:(qg + 1) * QPG],
                                              kp[(oc * NQG + qg) % 2][:, 0:QPG], AF.Identity,
                                              bias=bqr[:, oc:oc + 1])
                            C.inc("act", ii, ("qevac", oc, qg))

                def p1_dve(e, C):
                    for ob in range(NOB):
                        for hh in range(4):
                            gi = ob * 4 + hh
                            C.wait(e, "wh1" if gi % 2 == 0 else "wh1b", ("wvdma", ob, hh))
                            if ob > 0 and hh == 0:
                                C.wait(e, "pe", ("vmm", ob - 1, NQB - 1))
                            PW = NDCH * 128
                            C.inc("dve", e.tensor_copy(
                                wsb[:, hh * PW:(hh + 1) * PW], whs[gi % 2][:, :]),
                                ("wvcast", ob, hh))
                        for qb in range(NQB):
                            idx = ob * NQB + qb
                            C.wait(e, "pe", ("vmm", ob, qb))
                            if idx >= 2:
                                C.wait(e, "vo%d" % (idx % 2), ("vout", (idx - 2) // NQB, (idx - 2) % NQB))
                            ii = e.tensor_tensor(vstg[idx % 2][:, :], vp[idx % 2][:, 0:512],
                                                 bvb[:, ob * OBW:(ob + 1) * OBW], ALU.add)
                            C.inc("dve", ii, ("vevac", ob, qb))
                    for oc in range(NDCH):
                        C.wait(e, "wc%d" % (oc % 2), ("wqdma", oc))
                        if oc >= 2:
                            C.wait(e, "pe", ("qmm", oc - 2, NQG - 1))
                        else:
                            C.wait(e, "pe", ("kmm", NDCH - 2 + oc, NQG - 1))
                        C.inc("dve", e.tensor_copy(wcb[oc % 2][:, :], wcf[oc % 2][:, :]), ("wqcast", oc))

                if phases >= 2:
                    run_phase(dict(sync=p1_sync, pool=p1_pool, pe=p1_pe, act=p1_act, dve=p1_dve))

            with ExitStack() as SC:
                attn = SC.enter_context(nc.sbuf_tensor("attn", [128, NH * T], BF16))

                with ExitStack() as SD:
                    qh = [SD.enter_context(nc.sbuf_tensor(f"qh{i}", [128, T], BF16)) for i in range(2)]
                    kh = [SD.enter_context(nc.sbuf_tensor(f"kh{i}", [128, NR * T], BF16)) for i in range(2)]
                    vh = [SD.enter_context(nc.sbuf_tensor(f"vh{i}", [128, NR * T], BF16)) for i in range(2)]
                    pt = [SD.enter_context(nc.sbuf_tensor(f"pt{i}", [128, T], BF16)) for i in range(4)]
                    acc = SD.enter_context(nc.sbuf_tensor("acc", [128, T], F32))
                    pairb = SD.enter_context(nc.sbuf_tensor("pairb", [128, T], BF16))
                    rb = SD.enter_context(nc.sbuf_tensor("rb", [128, T], F32))
                    ssum = SD.enter_context(nc.sbuf_tensor("ssum", [1, T], F32))
                    srec = SD.enter_context(nc.sbuf_tensor("srec", [1, T], F32))
                    GRPW = min(1024, T)
                    stp = [SD.enter_context(nc.psum_tensor(f"stp{i}", [128, GRPW], F32)) for i in range(2)]
                    avp = SD.enter_context(nc.psum_tensor("avp", [128, T], F32))

                    GROUPS = []
                    for g in range((NQG + 1) // 2):
                        GROUPS.append((g, [q for q in (2 * g, 2 * g + 1) if q < NQG]))
                    LASTG = GROUPS[-1][0]
                    SPAN = NQG * QPG

                    def p2_sync(e, C):
                        for h in range(NH):
                            if h == 0:
                                C.wait_val(e, "ccK", 1)
                                C.wait_val(e, "ccV", 1)
                            if h >= 2:
                                C.wait(e, "pe", ("hpe", h - 2))
                            C.inc("qh%d" % (h % 2), e.dma_start(qh[h % 2][:, :],
                                                                 Qc_d[h * 128:(h + 1) * 128, :]), ("q", h), 16)
                            for r in range(NR):
                                C.inc("kh%d" % (h % 2), e.dma_start(kh[h % 2][:, r * T:(r + 1) * T],
                                                         Kg_d[r * D + h * 128: r * D + (h + 1) * 128, :]),
                                      ("k", h) if r == NR - 1 else None, 16)
                            for r in range(NR):
                                C.inc("vh%d" % (h % 2), e.dma_start(vh[h % 2][:, r * T:(r + 1) * T],
                                                         Vg_d[r * 128:(r + 1) * 128, h * T:(h + 1) * T]),
                                      ("v", h) if r == NR - 1 else None, 16)
                        for s in ("qh0", "qh1", "kh0", "kh1", "vh0", "vh1"):
                            C.mark(s, "p2all")
                            C.wait(e, s, "p2all")

                    def p2_pe(e, C):
                        for h in range(NH):
                            C.wait(e, "qh%d" % (h % 2), ("q", h))
                            C.wait(e, "kh%d" % (h % 2), ("k", h))
                            C.wait(e, "vh%d" % (h % 2), ("v", h))
                            if h > 0:
                                C.wait(e, "dve", ("rb", h - 1))
                                C.wait(e, "dve", ("norm", h - 1))
                            for kc in range(NKC):
                                for (g, qgs) in GROUPS:
                                    if kc > 0:
                                        C.wait(e, "act", ("exp", h, kc - 1, g))
                                    for j, qg in enumerate(qgs):
                                        ii = e.matmul(stp[g][:, j * QPG:(j + 1) * QPG],
                                                      kh[h % 2][:, kc * 128:(kc + 1) * 128],
                                                      qh[h % 2][:, qg * QPG:(qg + 1) * QPG],
                                                      start=True, stop=True)
                                    C.inc("pe", ii, ("qk", h, kc, g))
                                if kc > 0:
                                    kcp = kc - 1
                                    for qg in range(NQG):
                                        ii = e.matmul(avp[:, qg * QPG:(qg + 1) * QPG],
                                                      vh[h % 2][:, kcp * 128:(kcp + 1) * 128],
                                                      pt[kcp % 4][:, qg * QPG:(qg + 1) * QPG],
                                                      start=(kcp == 0), stop=False)
                                    C.inc("pe", ii, ("av", h, kcp))
                            kcp = NKC - 1
                            C.wait(e, "act", ("exp", h, kcp, LASTG))
                            for qg in range(NQG):
                                ii = e.matmul(avp[:, qg * QPG:(qg + 1) * QPG],
                                              vh[h % 2][:, kcp * 128:(kcp + 1) * 128],
                                              pt[kcp % 4][:, qg * QPG:(qg + 1) * QPG],
                                              start=(kcp == 0), stop=True)
                            C.inc("pe", ii, ("av", h, kcp))
                            C.wait(e, "dve", ("sum", h, NKC - 1))
                            for qg in range(NQG):
                                g, j = qg // 2, qg % 2
                                ii = e.matmul(stp[g][0:1, j * QPG:(j + 1) * QPG], onescol[:, :],
                                              acc[:, qg * QPG:(qg + 1) * QPG], start=True, stop=True)
                            C.inc("pe", ii, ("summ", h))
                            C.wait(e, "act", ("recip", h))
                            for qg in range(NQG):
                                g, j = qg // 2, qg % 2
                                ii = e.matmul(stp[g][:, j * QPG:(j + 1) * QPG], onesrow[:, :],
                                              srec[0:1, qg * QPG:(qg + 1) * QPG], start=True, stop=True)
                            C.inc("pe", ii, ("bcmm", h))
                            C.mark("pe", ("hpe", h))

                    def p2_act(e, C):
                        for h in range(NH):
                            for kc in range(NKC):
                                if kc >= 4:
                                    C.wait(e, "pe", ("av", h, kc - 4))
                                    j = kc - 3 if (kc - 3) % 2 == 1 else kc - 4
                                    if j >= 1:
                                        C.wait(e, "dve", ("sum", h, j))
                                elif h > 0:
                                    C.wait(e, "pe", ("av", h - 1, NKC - 1))
                                    C.wait(e, "dve", ("sum", h - 1, NKC - 1))
                                for (g, qgs) in GROUPS:
                                    C.wait(e, "pe", ("qk", h, kc, g))
                                    span = len(qgs) * QPG
                                    ii = e.activation(pt[kc % 4][:, 2 * g * QPG: 2 * g * QPG + span],
                                                      stp[g][:, 0:span], AF.Exp, scale=SCALE)
                                    C.inc("act", ii, ("exp", h, kc, g))
                            C.wait(e, "pe", ("summ", h))
                            for qg in range(NQG):
                                g, j = qg // 2, qg % 2
                                ii = e.copy(ssum[0:1, qg * QPG:(qg + 1) * QPG],
                                            stp[g][0:1, j * QPG:(j + 1) * QPG])
                            C.inc("act", ii, ("ssum", h))
                            C.wait(e, "act", ("ssum", h))
                            ii = e.activation(srec[0:1, :], ssum[0:1, :], AF.Ln)
                            C.inc("act", ii, ("lns", h))
                            C.wait(e, "act", ("lns", h))
                            ii = e.activation(srec[0:1, :], srec[0:1, :], AF.Exp, scale=-1.0)
                            C.inc("act", ii, ("recip", h))

                    def p2_dve(e, C):
                        sl = slice(0, SPAN)
                        for h in range(NH):
                            for kc in range(1, NKC, 2):
                                C.wait(e, "act", ("exp", h, kc, LASTG))
                                if kc >= 3:
                                    C.wait(e, "dve", ("sum", h, kc - 2))
                                ii = e.tensor_tensor(pairb[:, sl], pt[(kc - 1) % 4][:, sl],
                                                     pt[kc % 4][:, sl], ALU.add)
                                C.inc("dve", ii, ("pair", h, kc))
                                C.wait(e, "dve", ("pair", h, kc))
                                if kc == 1:
                                    ii = e.tensor_copy(acc[:, sl], pairb[:, sl])
                                else:
                                    ii = e.tensor_tensor(acc[:, sl], acc[:, sl], pairb[:, sl], ALU.add)
                                C.inc("dve", ii, ("sum", h, kc))
                            C.wait(e, "pe", ("bcmm", h))
                            for qg in range(NQG):
                                g, j = qg // 2, qg % 2
                                ii = e.tensor_copy(rb[:, qg * QPG:(qg + 1) * QPG],
                                                   stp[g][:, j * QPG:(j + 1) * QPG])
                            C.inc("dve", ii, ("rb", h))
                            C.wait(e, "dve", ("rb", h))
                            C.wait(e, "pe", ("av", h, NKC - 1))
                            for qg in range(NQG):
                                sl = slice(qg * QPG, (qg + 1) * QPG)
                                ii = e.tensor_tensor(attn[:, h * T + qg * QPG: h * T + (qg + 1) * QPG],
                                                     avp[:, sl], rb[:, sl], ALU.mult)
                            C.inc("dve", ii, ("norm", h))

                    if phases >= 3:
                        run_phase(dict(sync=p2_sync, pe=p2_pe, act=p2_act, dve=p2_dve))

                with ExitStack() as SE:
                    whs3 = [SE.enter_context(nc.sbuf_tensor(f"whs3{i}", [128, NDCH * 128], F32)) for i in range(2)]
                    wsb3 = SE.enter_context(nc.sbuf_tensor("wsb3", [128, NDCH * 512], BF16))
                    aoT = SE.enter_context(nc.sbuf_tensor("aoT_s", [RK, D], BF16))
                    boT = SE.enter_context(nc.sbuf_tensor("boT_s", [128, NDCH * RK], BF16))
                    u_o = SE.enter_context(nc.sbuf_tensor("u_o", [RK, T], BF16))
                    ostg = [SE.enter_context(nc.sbuf_tensor(f"ostg{i}", [128, 512], F16)) for i in range(2)]
                    po = [SE.enter_context(nc.psum_tensor(f"po{i}", [128, 512], F32)) for i in range(2)]
                    puo = SE.enter_context(nc.psum_tensor("puo", [128, 512], F32))

                    def p3a_sync(e, C):
                        C.inc("wh3", e.dma_start(whs3[0][0:RK, 0:D], aoT_d[:, :]), "aodma", 16)
                        C.wait(e, "pool", "aocast")
                        C.inc("wh3", e.dma_start(
                            AP3(whs3[0], 0, (NDCH * 128, 128), (128, NDCH), (1, RK)),
                            AP3(boT_d, 0, (RK, 128), (128 * RK, NDCH), (1, RK))), "bodma", 16)
                        for ob in range(NOB):
                            if ob > 0:
                                for qb in range(NQB):
                                    C.wait(e, "dve", ("oevac", ob - 1, qb))
                                    C.inc("oo%d" % (((ob - 1) * NQB + qb) % 2), e.dma_start(
                                        opre_d[qb * 128:(qb + 1) * 128, (ob - 1) * OBW: ob * OBW],
                                        ostg[((ob - 1) * NQB + qb) % 2][:, 0:OBW]), ("oout", ob - 1, qb), 16)
                            for hh in range(4):
                                gi = ob * 4 + hh
                                if gi >= 2:
                                    C.wait(e, "dve", ("wocast", (gi - 2) // 4, (gi - 2) % 4))
                                elif gi == 0:
                                    C.wait(e, "pool", "bocast")
                                PW = NDCH * 128
                                C.inc("wh3" if gi % 2 == 0 else "wh3b", e.dma_start(
                                    whs3[gi % 2][:, :],
                                    woT_d[ob * 128:(ob + 1) * 128, hh * PW:(hh + 1) * PW]),
                                    ("wodma", ob, hh), 16)
                        for qb in range(NQB):
                            C.wait(e, "dve", ("oevac", NOB - 1, qb))
                            C.inc("oo%d" % (((NOB - 1) * NQB + qb) % 2), e.dma_start(
                                opre_d[qb * 128:(qb + 1) * 128, (NOB - 1) * OBW: NOB * OBW],
                                ostg[((NOB - 1) * NQB + qb) % 2][:, 0:OBW]), ("oout", NOB - 1, qb), 16)
                        for s in ("wh3", "wh3b", "oo0", "oo1"):
                            C.mark(s, "p3aall")
                            C.wait(e, s, "p3aall")

                    def p3a_pool(e, C):
                        C.wait(e, "wh3", "aodma")
                        C.inc("pool", e.tensor_copy(aoT[:, :], whs3[0][0:RK, 0:D]), "aocast")
                        C.wait(e, "wh3", "bodma")
                        C.inc("pool", e.tensor_copy(
                            AP3(boT, 0, (NDCH * RK, 128), (RK, NDCH), (1, RK)),
                            AP3(whs3[0], 0, (NDCH * 128, 128), (128, NDCH), (1, RK))), "bocast")

                    def p3a_pe(e, C):
                        C.wait(e, "pool", "bocast")
                        for qg in range(NQG):
                            if qg >= 1:
                                C.wait(e, "act", ("uoevac", qg - 1))
                            for i in range(NDCH):
                                ii = e.matmul(puo[0:RK, 0:QPG], boT[:, i * RK:(i + 1) * RK],
                                              attn[:, i * T + qg * QPG: i * T + (qg + 1) * QPG],
                                              start=(i == 0), stop=(i == NDCH - 1))
                            C.inc("pe", ii, ("uomm", qg))
                        for ob in range(NOB):
                            C.wait(e, "dve", ("wocast", ob, 3))
                            for qb in range(NQB):
                                idx = ob * NQB + qb
                                if idx >= 2:
                                    C.wait(e, "dve", ("oevac", (idx - 2) // NQB, (idx - 2) % NQB))
                                elif idx == 0:
                                    C.wait(e, "act", ("uoevac", NQG - 1))
                                for i in range(NDCH):
                                    e.matmul(po[idx % 2][:, 0:512],
                                             attn[:, i * T + qb * 128: i * T + (qb + 1) * 128],
                                             wslice(wsb3, i, 512), start=(i == 0), stop=False)
                                ii = e.matmul(po[idx % 2][:, 0:512], u_o[:, qb * 128:(qb + 1) * 128],
                                              aoT[:, ob * OBW:(ob + 1) * OBW], start=False, stop=True)
                                C.inc("pe", ii, ("omm", ob, qb))

                    def p3a_act(e, C):
                        for qg in range(NQG):
                            C.wait(e, "pe", ("uomm", qg))
                            ii = e.activation(u_o[:, qg * QPG:(qg + 1) * QPG], puo[0:RK, 0:QPG],
                                              AF.Copy, scale=0.5)
                            C.inc("act", ii, ("uoevac", qg))

                    def p3a_dve(e, C):
                        for ob in range(NOB):
                            for hh in range(4):
                                gi = ob * 4 + hh
                                C.wait(e, "wh3" if gi % 2 == 0 else "wh3b", ("wodma", ob, hh))
                                if ob > 0 and hh == 0:
                                    C.wait(e, "pe", ("omm", ob - 1, NQB - 1))
                                PW = NDCH * 128
                                C.inc("dve", e.tensor_copy(
                                    wsb3[:, hh * PW:(hh + 1) * PW], whs3[gi % 2][:, :]),
                                    ("wocast", ob, hh))
                            for qb in range(NQB):
                                idx = ob * NQB + qb
                                C.wait(e, "pe", ("omm", ob, qb))
                                if idx >= 2:
                                    C.wait(e, "oo%d" % (idx % 2), ("oout", (idx - 2) // NQB, (idx - 2) % NQB))
                                ii = e.tensor_tensor(ostg[idx % 2][:, 0:OBW], po[idx % 2][:, 0:512],
                                                     bob[:, ob * OBW:(ob + 1) * OBW], ALU.add)
                                C.inc("dve", ii, ("oevac", ob, qb))

                    if phases >= 4:
                        run_phase(dict(sync=p3a_sync, pool=p3a_pool, pe=p3a_pe, act=p3a_act, dve=p3a_dve))

        with ExitStack() as SF:
            z1fm = SF.enter_context(nc.sbuf_tensor("z1fm", [128, NDCH * T], BF16))
            ROT = 8 if NDCH >= 8 else NDCH

            with ExitStack() as SG:
                opstr = [SG.enter_context(nc.sbuf_tensor(f"opstr{i}", [128, D], F16)) for i in range(2)]
                xstr = [SG.enter_context(nc.sbuf_tensor(f"xstr{i}", [128, D], F32)) for i in range(2)]
                s1 = [SG.enter_context(nc.sbuf_tensor(f"s1_{i}", [128, D], F32)) for i in range(2)]
                z1h = [SG.enter_context(nc.sbuf_tensor(f"z1h{i}", [128, D], F16)) for i in range(2)]
                z1b = [SG.enter_context(nc.sbuf_tensor(f"z1b{i}", [128, D], BF16)) for i in range(2)]
                bns3 = [SG.enter_context(nc.sbuf_tensor(f"bns3_{i}", [128, 6 * NLC], F32)) for i in range(2)]
                bna3 = [SG.enter_context(nc.sbuf_tensor(f"bna3_{i}", [128, 2], F32)) for i in range(2)]
                sd3 = [SG.enter_context(nc.sbuf_tensor(f"sd3_{i}", [128, 1], F32)) for i in range(2)]
                rstd3 = [SG.enter_context(nc.sbuf_tensor(f"rstd3_{i}", [128, 1], F32)) for i in range(2)]
                trp = [SG.enter_context(nc.psum_tensor(f"trp{i}", [128, 512], BF16)) for i in range(2)]

                NG4 = NDCH // 4 if NDCH >= 4 else 1

                def tr_slot(qb, i):
                    g = qb * NG4 + i // 4
                    return trp[g % 2][:, (i % 4) * 128:((i % 4) + 1) * 128]

                def p3b_sync(e, C):
                    for qb in range(NQB):
                        if qb >= 2 and _P3BL >= 2:
                            C.wait(e, "dve", ("s1c", qb - 2))
                        C.inc("op%d" % (qb % 2), e.dma_start(opstr[qb % 2][:, :],
                                                              opre_d[qb * 128:(qb + 1) * 128, :]),
                              ("opin", qb), 16)
                        C.inc("xr%d" % (qb % 2), e.dma_start(xstr[qb % 2][:, :],
                                                             xtm_d[qb * 128:(qb + 1) * 128, :]),
                              ("xin", qb), 16)
                        if _P3BL >= 3:
                            C.wait(e, "dve", ("z1hc", qb))
                            C.inc("z1o" if qb % 2 == 0 else "z1ob",
                                  e.dma_start(z1tm_d[qb * 128:(qb + 1) * 128, :],
                                              z1h[qb % 2][:, :]), ("z1out", qb), 16)
                    for s in (("op0", "op1", "xr0", "xr1", "z1o", "z1ob") if _P3BL >= 3 else ("op0", "op1", "xr0", "xr1")):
                        C.mark(s, "p3ball")
                        C.wait(e, s, "p3ball")

                def p3b_act(e, C):
                    if _P3BL < 2:
                        return
                    for qb in range(NQB):
                        p = qb % 2
                        C.wait(e, "dve", ("bnagg", qb))
                        C.inc("act", e.activation(sd3[p][:, :], bna3[p][:, 1:2], AF.Sqrt,
                                                  bias=epst[:, :]), ("sqrt", qb))
                        C.wait(e, "dve", ("z1hc", qb))
                        if qb > 1 and _P3BL >= 4:
                            C.wait(e, "pe", ("trg", qb - 2, NDCH // 4 - 1 if NDCH >= 4 else 0))
                        C.inc("act", e.activation(z1b[p][:, :], z1h[p][:, :], AF.Copy), ("z1bc", qb))
                        if _P3BL < 5:
                            continue
                        for i in range(NDCH):
                            if i % 2 == 0 and _TRC != "act":
                                continue
                            if i % 2 == 1 and _TRC == "dve":
                                continue
                            C.wait(e, "pe", ("trg", qb, i // 4))
                            C.inc("act", e.copy(z1fm[:, i * T + qb * 128: i * T + (qb + 1) * 128],
                                                tr_slot(qb, i)), ("trc", qb, i))

                def p3b_dve(e, C):
                    if _P3BL < 2:
                        return
                    for qb in range(NQB):
                        p = qb % 2
                        C.wait(e, "op%d" % p, ("opin", qb))
                        C.wait(e, "xr%d" % p, ("xin", qb))
                        if qb > 1:
                            C.wait(e, "dve", ("z1hc", qb - 2))
                        C.inc("dve", e.tensor_tensor(s1[p][:, :], opstr[p][:, :], xstr[p][:, :],
                                                     ALU.add), ("s1c", qb))
                        C.wait(e, "dve", ("s1c", qb))
                        for j in range(NLC):
                            ii = e.bn_stats(bns3[p][:, 6 * j:6 * (j + 1)], s1[p][:, 512 * j:512 * (j + 1)])
                        C.inc("dve", ii, ("bnst", qb))
                        C.wait(e, "dve", ("bnst", qb))
                        C.inc("dve", e.bn_aggr(bna3[p][:, :], bns3[p][:, :]), ("bnagg", qb))
                        C.wait(e, "act", ("sqrt", qb))
                        C.inc("dve", e.reciprocal(rstd3[p][:, :], sd3[p][:, :]), ("rstd", qb))
                        C.wait(e, "dve", ("rstd", qb))
                        if qb > 1 and _P3BL >= 3:
                            C.wait(e, "z1o" if qb % 2 == 0 else "z1ob", ("z1out", qb - 2))
                        if qb > 1:
                            C.wait(e, "act", ("z1bc", qb - 2))
                        C.inc("dve", e.tensor_scalar(z1h[p][:, :], s1[p][:, :], bna3[p][:, 0:1],
                                                     rstd3[p][:, :], ALU.subtract, ALU.mult), ("z1hc", qb))
                        if _P3BL < 5:
                            continue
                        for i in range(NDCH):
                            if i % 2 == 1 and _TRC != "dve":
                                continue
                            if i % 2 == 0 and _TRC == "act":
                                continue
                            C.wait(e, "pe", ("trg", qb, i // 4))
                            C.inc("dve", e.tensor_copy(z1fm[:, i * T + qb * 128: i * T + (qb + 1) * 128],
                                                       tr_slot(qb, i)), ("trc", qb, i))

                def p3b_pe(e, C):
                    if _P3BL < 4:
                        return
                    for qb in range(NQB):
                        C.wait(e, "act", ("z1bc", qb))
                        for g4 in range(NG4):
                            pg = qb * NG4 + g4 - 1
                            if pg >= 0 and _P3BL >= 5:
                                pqb, pg4 = divmod(pg, NG4)
                                for i in range(pg4 * 4, pg4 * 4 + 4):
                                    eng = "act" if (_TRC == "act" or (_TRC == "mix" and i % 2 == 1)) else "dve"
                                    C.wait(e, eng, ("trc", pqb, i))
                            for i in range(g4 * 4, g4 * 4 + 4):
                                ii = e.transpose(tr_slot(qb, i), z1b[qb % 2][:, i * 128:(i + 1) * 128],
                                                 idb[:, :])
                            C.inc("pe", ii, ("trg", qb, g4))

                if phases >= 5:
                    run_phase(dict(sync=p3b_sync, act=p3b_act, dve=p3b_dve, pe=p3b_pe))

            with ExitStack() as SH:
                hfm = SH.enter_context(nc.sbuf_tensor("hfm", [128, NDCH * T], BF16))
                whs4 = [SH.enter_context(nc.sbuf_tensor(f"whs4{i}", [128, NDCH * 64], F32)) for i in range(2)]
                web = SH.enter_context(nc.sbuf_tensor("web", [128, NDCH * 512], BF16))
                wdb = SH.enter_context(nc.sbuf_tensor("wdb", [128, NDCH * 512], BF16))
                wub = web
                glog = SH.enter_context(nc.sbuf_tensor("glog", [NE, T], BF16))
                t1 = SH.enter_context(nc.sbuf_tensor("t1", [128, 512], F32))
                s2stg = [SH.enter_context(nc.sbuf_tensor(f"s2stg{i}", [128, 512], F16)) for i in range(2)]
                s2str = SH.enter_context(nc.sbuf_tensor("s2str", [128, D], F16))
                z1str = SH.enter_context(nc.sbuf_tensor("z1str", [128, D], F16))
                s2f = SH.enter_context(nc.sbuf_tensor("s2f", [128, D], F32))
                ostg4 = SH.enter_context(nc.sbuf_tensor("ostg4", [128, D], F32))
                up_ps = [SH.enter_context(nc.psum_tensor(f"up_ps{i}", [128, 512], F32)) for i in range(2)]
                ep_ps = [SH.enter_context(nc.psum_tensor(f"ep_ps{i}", [128, 512], F32)) for i in range(2)]
                dp_ps = [SH.enter_context(nc.psum_tensor(f"dp_ps{i}", [128, 512], F32)) for i in range(2)]
                gt_ps = SH.enter_context(nc.psum_tensor("gt_ps", [128, NE * 4], BF16))

                SEQ4 = [("wu", ob, hh) for ob in range(NOB) for hh in range(8)] + \
                       [(nm, ob, hh) for ob in range(NOB) for nm in ("we", "wd") for hh in range(8)]
                SRC4 = {"wu": wupT_d, "we": wex_d, "wd": wdnT_d}
                DST4 = {"wu": wub, "we": web, "wd": wdb}
                IDX4 = {k: i for i, k in enumerate(SEQ4)}

                def p4_sync(e, C):
                    def wpiece(key):
                        i = IDX4[key]
                        nm, ob, hh = key
                        if i >= 2:
                            C.wait(e, "dve", ("wcast4", i - 2))
                        PW = NDCH * 64
                        C.inc("wh4" if i % 2 == 0 else "wh4b", e.dma_start(
                            whs4[i % 2][:, :],
                            SRC4[nm][ob * 128:(ob + 1) * 128, hh * PW:(hh + 1) * PW]),
                            ("wdma4", i), 16)
                    for ob in range(NOB):
                        for hh in range(8):
                            wpiece(("wu", ob, hh))
                    for ob in range(NOB):
                        if ob > 0:
                            for qb in range(NQB):
                                C.wait(e, "dve", ("s2evac", ob - 1, qb))
                                C.inc("sg%d" % (((ob - 1) * NQB + qb) % 2), e.dma_start(
                                    s2pre_d[qb * 128:(qb + 1) * 128, (ob - 1) * OBW: ob * OBW],
                                    s2stg[((ob - 1) * NQB + qb) % 2][:, 0:OBW]), ("s2out", ob - 1, qb), 16)
                        for nm in ("we", "wd"):
                            for hh in range(8):
                                wpiece((nm, ob, hh))
                    for qb in range(NQB):
                        C.wait(e, "dve", ("s2evac", NOB - 1, qb))
                        C.inc("sg%d" % (((NOB - 1) * NQB + qb) % 2), e.dma_start(
                            s2pre_d[qb * 128:(qb + 1) * 128, (NOB - 1) * OBW: NOB * OBW],
                            s2stg[((NOB - 1) * NQB + qb) % 2][:, 0:OBW]), ("s2out", NOB - 1, qb), 16)
                        if qb >= 1:
                            C.wait(e, "dve", ("s2f", qb - 1))
                        C.wait(e, "sg%d" % (((NOB - 1) * NQB + qb) % 2), ("s2out", NOB - 1, qb))
                        C.inc("ss0", e.dma_start(s2str[:, :],
                                                 s2pre_d[qb * 128:(qb + 1) * 128, :]),
                              ("s2in", qb), 16)
                        C.inc("zs0", e.dma_start(z1str[:, :],
                                                 z1tm_d[qb * 128:(qb + 1) * 128, :]),
                              ("z1in", qb), 16)
                        if qb >= 1:
                            C.wait(e, "dve", ("outc", qb - 1))
                            C.inc("fo0", e.dma_start(out_d[(qb - 1) * 128:qb * 128, :],
                                                     ostg4[:, :]), ("final", qb - 1), 16)
                    C.wait(e, "dve", ("outc", NQB - 1))
                    C.inc("fo0", e.dma_start(out_d[(NQB - 1) * 128:NQB * 128, :],
                                             ostg4[:, :]), ("final", NQB - 1), 16)
                    for s in ("wh4", "wh4b", "ss0", "zs0", "fo0"):
                        C.mark(s, "p4all")
                        C.wait(e, s, "p4all")

                def p4_wcast(e, C, key):
                    i = IDX4[key]
                    nm, ob, hh = key
                    C.wait(e, "wh4" if i % 2 == 0 else "wh4b", ("wdma4", i))
                    if nm == "wu" and ob > 0 and hh == 0:
                        C.wait(e, "pe", ("upmm", ob - 1, qpo - 1, NQG - 1))
                    if nm == "we" and ob == 0 and hh == 0:
                        C.wait(e, "pe", ("upmm", NOB - 1, qpo - 1, NQG - 1))
                    if nm == "we" and ob > 0 and hh == 0:
                        C.wait(e, "pe", ("edmm", ob - 1, NQB - 1))
                    if nm == "wd" and ob > 0 and hh == 0:
                        C.wait(e, "pe", ("edmm", ob - 1, NQB - 1))
                    PW = NDCH * 64
                    C.inc("dve", e.tensor_copy(
                        DST4[nm][:, hh * PW:(hh + 1) * PW], whs4[i % 2][:, :]),
                        ("wcast4", i))

                def p4_pe(e, C):
                    for ob in range(NOB):
                        C.wait(e, "dve", ("wcast4", IDX4[("wu", ob, 7)]))
                        for j in range(qpo):
                            oc = ob * qpo + j
                            for qg in range(NQG):
                                idx = oc * NQG + qg
                                if idx >= 2:
                                    C.wait(e, "act", ("upevac", ((idx - 2) // NQG), (idx - 2) % NQG))
                                for i in range(NDCH):
                                    ii = e.matmul(up_ps[idx % 2][:, 0:QPG],
                                                  wub[:, i * 512 + (j % 4) * 128: i * 512 + ((j % 4) + 1) * 128],
                                                  z1fm[:, i * T + qg * QPG: i * T + (qg + 1) * QPG],
                                                  start=(i == 0), stop=(i == NDCH - 1))
                                C.inc("pe", ii, ("upmm", ob, j, qg))
                    for qg in range(NQG):
                        if qg == 0:
                            C.wait(e, "act", ("upevac", NDCH - 1, NQG - 1))
                        else:
                            C.wait(e, "act", ("gevac", qg - 1))
                        for i in range(NDCH):
                            ii = e.matmul(up_ps[qg % 2][0:NE, 0:QPG], wgT[:, i * NE:(i + 1) * NE],
                                          z1fm[:, i * T + qg * QPG: i * T + (qg + 1) * QPG],
                                          start=(i == 0), stop=(i == NDCH - 1))
                        C.inc("pe", ii, ("gmm", qg))
                    for qgr in range(NQB // 4):
                        C.wait(e, "act", ("gevac", ((qgr * 4 + 3) * 128) // QPG))
                        if qgr >= 1:
                            C.wait(e, "dve", ("gtc", qgr - 1))
                        for qb in range(qgr * 4, qgr * 4 + 4):
                            ii = e.transpose(gt_ps[:, (qb % 4) * NE:((qb % 4) + 1) * NE],
                                             glog[0:NE, qb * 128:(qb + 1) * 128], idb[0:NE, 0:NE])
                        C.inc("pe", ii, ("gt", qgr))
                    for ob in range(NOB):
                        C.wait(e, "dve", ("wcast4", IDX4[("wd", ob, 7)]))
                        for qb in range(NQB):
                            idx = ob * NQB + qb
                            if idx >= 2:
                                C.wait(e, "dve", ("eevac", (idx - 2) // NQB, (idx - 2) % NQB))
                                C.wait(e, "dve", ("s2evac", (idx - 2) // NQB, (idx - 2) % NQB))
                            for i in range(NDCH):
                                ii = e.matmul(ep_ps[idx % 2][:, 0:512],
                                              z1fm[:, i * T + qb * 128: i * T + (qb + 1) * 128],
                                              wslice(web, i, 512), start=(i == 0), stop=(i == NDCH - 1))
                            C.inc("pe", ii, ("emm", ob, qb))
                            for i in range(NDCH):
                                ii = e.matmul(dp_ps[idx % 2][:, 0:512],
                                              hfm[:, i * T + qb * 128: i * T + (qb + 1) * 128],
                                              wslice(wdb, i, 512), start=(i == 0), stop=(i == NDCH - 1))
                            C.inc("pe", ii, ("edmm", ob, qb))

                def p4_act(e, C):
                    for ob in range(NOB):
                        for j in range(qpo):
                            oc = ob * qpo + j
                            for qg in range(NQG):
                                C.wait(e, "pe", ("upmm", ob, j, qg))
                                ii = e.activation(hfm[:, oc * T + qg * QPG: oc * T + (qg + 1) * QPG],
                                                  up_ps[(oc * NQG + qg) % 2][:, 0:QPG], AF.Relu)
                                C.inc("act", ii, ("upevac", oc, qg))
                    for qg in range(NQG):
                        C.wait(e, "pe", ("gmm", qg))
                        ii = e.activation(glog[0:NE, qg * QPG:(qg + 1) * QPG],
                                          up_ps[qg % 2][0:NE, 0:QPG], AF.Copy)
                        C.inc("act", ii, ("gevac", qg))
                    for qb in range(NQB):
                        C.wait(e, "dve", ("dd", qb))
                        C.inc("act", e.activation(sg1[:, :], ddt[:, :], AF.Sigmoid), ("sg1", qb))
                        C.inc("act", e.activation(sg2[:, :], ddt[:, :], AF.Sigmoid, scale=-1.0), ("sg2", qb))
                    for qb in range(NQB):
                        C.wait(e, "dve", ("bnagg4", qb))
                        C.inc("act", e.activation(sd1[:, :], bna[:, 1:2], AF.Sqrt, bias=epst[:, :]), ("sqrt4", qb))

                def p4_dve(e, C):
                    for ob in range(NOB):
                        for hh in range(8):
                            p4_wcast(e, C, ("wu", ob, hh))
                    for qgr in range(NQB // 4):
                        C.wait(e, "pe", ("gt", qgr))
                        C.inc("dve", e.tensor_copy(glogt[:, qgr * 4 * NE:(qgr + 1) * 4 * NE],
                                                   gt_ps[:, :]), ("gtc", qgr))
                    for qb in range(NQB):
                        sl = glogt[:, qb * NE:(qb + 1) * NE]
                        C.wait(e, "dve", ("gtc", qb // 4))
                        if qb > 0:
                            C.wait(e, "dve", ("sfin", qb - 1))
                        C.inc("dve", e.tensor_reduce(l1t[:, :], sl, AX.X, ALU.max), ("l1", qb))
                        C.wait(e, "dve", ("l1", qb))
                        C.inc("dve", e.tensor_scalar(eqm[:, :], sl, l1t[:, :], negbig[:, :],
                                                     ALU.is_equal, ALU.mult), ("eqm", qb))
                        C.wait(e, "dve", ("eqm", qb))
                        C.inc("dve", e.tensor_tensor(mskt[:, :], eqm[:, :], sl, ALU.add), ("msk", qb))
                        C.wait(e, "dve", ("msk", qb))
                        C.inc("dve", e.tensor_reduce(l2t[:, :], mskt[:, :], AX.X, ALU.max), ("l2", qb))
                        C.wait(e, "dve", ("l2", qb))
                        C.inc("dve", e.tensor_tensor(ddt[:, :], l1t[:, :], l2t[:, :], ALU.subtract), ("dd", qb))
                        C.wait(e, "act", ("sg1", qb))
                        C.wait(e, "act", ("sg2", qb))
                        C.inc("dve", e.tensor_tensor(s_all[:, qb:qb + 1], sg1[:, :], sg2[:, :], ALU.add),
                              ("sfin", qb))
                    for ob in range(NOB):
                        for nm in ("we", "wd"):
                            for hh in range(8):
                                p4_wcast(e, C, (nm, ob, hh))
                        for qb in range(NQB):
                            idx = ob * NQB + qb
                            C.wait(e, "pe", ("emm", ob, qb))
                            C.wait(e, "dve", ("sfin", NQB - 1))
                            if idx > 0:
                                C.wait(e, "dve", ("s2evac", (idx - 1) // NQB, (idx - 1) % NQB))
                            C.inc("dve", e.tensor_scalar(t1[:, 0:512], ep_ps[idx % 2][:, 0:512],
                                                         s_all[:, qb:qb + 1], 0.0, ALU.mult, ALU.add),
                                  ("eevac", ob, qb))
                            C.wait(e, "pe", ("edmm", ob, qb))
                            C.wait(e, "dve", ("eevac", ob, qb))
                            if idx >= 2:
                                C.wait(e, "sg%d" % (idx % 2), ("s2out", (idx - 2) // NQB, (idx - 2) % NQB))
                            C.inc("dve", e.tensor_tensor(s2stg[idx % 2][:, 0:OBW], t1[:, 0:512],
                                                         dp_ps[idx % 2][:, 0:512], ALU.add),
                                  ("s2evac", ob, qb))
                            if ob == NOB - 1:
                                C.wait(e, "ss0", ("s2in", qb))
                                C.wait(e, "zs0", ("z1in", qb))
                                if qb > 0:
                                    C.wait(e, "dve", ("outc", qb - 1))
                                C.inc("dve", e.tensor_tensor(s2f[:, :], s2str[:, :],
                                                             z1str[:, :], ALU.add), ("s2f", qb))
                                C.wait(e, "dve", ("s2f", qb))
                                for j in range(NLC):
                                    ii = e.bn_stats(bns[:, 6 * j:6 * (j + 1)], s2f[:, 512 * j:512 * (j + 1)])
                                C.inc("dve", ii, ("bnst4", qb))
                                C.wait(e, "dve", ("bnst4", qb))
                                C.inc("dve", e.bn_aggr(bna[:, :], bns[:, :]), ("bnagg4", qb))
                                C.wait(e, "act", ("sqrt4", qb))
                                C.inc("dve", e.reciprocal(rstd[:, :], sd1[:, :]), ("rstd4", qb))
                                C.wait(e, "dve", ("rstd4", qb))
                                if qb >= 1:
                                    C.wait(e, "fo0", ("final", qb - 1))
                                C.inc("dve", e.tensor_scalar(ostg4[:, :], s2f[:, :], bna[:, 0:1],
                                                             rstd[:, :], ALU.subtract, ALU.mult), ("outc", qb))

                if phases >= 6:
                    run_phase(dict(sync=p4_sync, pe=p4_pe, act=p4_act, dve=p4_dve))

    if phases < 6:
        with nc.sbuf_tensor("stub", [128, D], F32) as stub, nc.semaphore("stubs") as sts, nc.Block() as blk:
            @blk.gpsimd
            def _(gpsimd):
                gpsimd.memset(stub[:, :], 0.0).then_inc(sts, 1)
            @blk.sync
            def _(sync):
                sync.wait_ge(sts, 1)
                for qb in range(NQB):
                    sync.dma_start(out_d[qb * 128:(qb + 1) * 128, :], stub[:, :]).then_inc(sts, 16)
                sync.wait_ge(sts, 1 + 16 * NQB)
    nc.finalize()
    return nc



_BUILD_CACHE = {}


def _get_program(key, cfg):
    if key not in _BUILD_CACHE:
        _BUILD_CACHE[key] = build(cfg)
    return _BUILD_CACHE[key]


def make_inputs(cfg, x, Wq, bq, Bq, Aq, Wk, bk, Bk, Ak, Wv, bv, Bv, Av,
                Wo, bo, Bo, Ao, W_up, W_down, W_gate, W_expert):
    NDCH, NQB, D, T, NH, NKC, NQG, NOB = shapes(cfg)
    f32 = lambda a: np.ascontiguousarray(np.asarray(a), dtype=np.float32)
    ballT = np.zeros((D, 80), np.float32)
    ballT[:, 0:16] = np.asarray(Bq).T
    ballT[:, 32:48] = np.asarray(Bk).T
    ballT[:, 64:80] = np.asarray(Bv).T
    def tile_col(WT):
        return np.ascontiguousarray(
            np.asarray(WT).reshape(NDCH, 128, NDCH, 128).transpose(2, 1, 0, 3).reshape(D, D))

    def tile_slab(Wdo):
        return np.ascontiguousarray(
            np.asarray(Wdo).reshape(NDCH, 128, NOB, 512 if D >= 512 else D)
            .transpose(2, 1, 0, 3).reshape(NOB * 128, NDCH * (512 if D >= 512 else D)))

    shared = dict(
        wqT=tile_col(np.asarray(Wq).T), wkT=tile_col(np.asarray(Wk).T),
        wvT=tile_slab(np.asarray(Wv).T), woT=tile_slab(np.asarray(Wo).T),
        wupT=tile_slab(np.asarray(W_up).T), wdnT=tile_slab(np.asarray(W_down).T),
        wex=tile_slab(W_expert), wgT=f32(np.asarray(W_gate).T), ballT=f32(ballT),
        aqT=f32(np.asarray(Aq).T), akT=f32(np.asarray(Ak).T), avT=f32(np.asarray(Av).T),
        aoT=f32(np.asarray(Ao).T), boT=f32(np.asarray(Bo).T),
        bqr=f32(np.asarray(bq).reshape(NDCH, 128).T), bkr=f32(np.asarray(bk).reshape(NDCH, 128).T),
        bor=f32(np.asarray(bo).reshape(1, D)), bvr=f32(np.asarray(bv).reshape(1, D)),
        idb=np.eye(128, dtype=ml_dtypes.bfloat16),
    )
    in_maps = []
    for c in range(8):
        b, s = c // NR, (c % NR) * T
        xs = np.asarray(x)[b, s:s + T, :]
        m = dict(shared)
        m["xtm"] = f32(xs)
        m["xfm"] = f32(xs.T)
        in_maps.append(m)
    return in_maps


def kernel(**inputs):
    cfg = CFG_FULL
    NDCH, NQB, D, T, NH, NKC, NQG, NOB = shapes(cfg)
    x = np.asarray(inputs["L0_attn_x"], np.float32)
    args = [x]
    for p in ("Wq", "bq", "Bq", "Aq", "Wk", "bk", "Bk", "Ak", "Wv", "bv", "Bv", "Av",
              "Wo", "bo", "Bo", "Ao"):
        args.append(np.asarray(inputs["L0_attn_" + p], np.float32))
    for p in ("W_up", "W_down", "W_gate", "W_expert"):
        args.append(np.asarray(inputs["L0_ffn_" + p], np.float32))
    in_maps = make_inputs(cfg, *args)
    nc = _get_program("full", cfg)
    res = run_bass_kernel_spmd(nc, in_maps, core_ids=list(range(8)))
    out = np.empty((2, 8192, 2048), np.float32)
    for c in range(8):
        b, s = c // NR, (c % NR) * T
        out[b, s:s + T, :] = res.results[c]["out"]
    return out


# version 16
# speedup vs baseline: 1.0243x; 1.0079x over previous
import os as _os
import numpy as np
import ml_dtypes
_P3BL = int(_os.environ.get("P3BL", "9"))
_TRC = _os.environ.get("TRC", "dve")
from contextlib import ExitStack

import concourse.bass as bass
import concourse.mybir as mybir
from concourse.bass_utils import run_bass_kernel_spmd

F32 = mybir.dt.float32
F16 = mybir.dt.float16
BF16 = mybir.dt.bfloat16
AF = mybir.ActivationFunctionType
ALU = mybir.AluOpType
AX = mybir.AxisListType

SCALE = float(128) ** -0.5
EPS = 1e-5
NE = 8
RK = 16
NR = 4

CFG_FULL = dict(NDCH=16, NQB=16)
CFG_MINI = dict(NDCH=4, NQB=4)


def shapes(cfg):
    NDCH, NQB = cfg["NDCH"], cfg["NQB"]
    D = NDCH * 128
    T = NQB * 128
    NH = NDCH
    NKC = NR * NQB
    NQG = max(1, T // 512)
    NOB = max(1, D // 512)
    return NDCH, NQB, D, T, NH, NKC, NQG, NOB



class FakeInstr:
    def then_inc(self, *a, **k):
        return self


class FakeEngine:
    def __getattr__(self, name):
        def f(*a, **k):
            return FakeInstr()
        return f


class Ctx:

    def __init__(self):
        self.emit = False
        self.marks = {}
        self.cnt = {}
        self.handles = {}

    def inc(self, sem, instr=None, key=None, k=1):
        n = self.cnt.get(sem, 0) + k
        self.cnt[sem] = n
        if key is not None:
            kk = (sem, key)
            if self.emit:
                assert self.marks.get(kk) == n, f"mark drift {kk}: {self.marks.get(kk)} vs {n}"
            else:
                self.marks[kk] = n
        if self.emit and instr is not None and not isinstance(instr, FakeInstr):
            instr.then_inc(self.handles[sem], k)
        return n

    def mark(self, sem, key):
        n = self.cnt.get(sem, 0)
        kk = (sem, key)
        if self.emit:
            assert self.marks.get(kk) == n, f"mark drift {kk}"
        else:
            self.marks[kk] = n
        return n

    def wait(self, eng, sem, key):
        kk = (sem, key)
        if not self.emit:
            return
        if kk not in self.marks:
            raise KeyError(f"wait on unknown mark {kk}")
        eng.wait_ge(self.handles[sem], self.marks[kk])

    def wait_val(self, eng, sem, val):
        if self.emit:
            eng.wait_ge(self.handles[sem], val)


def AP3(t, off, *dims):
    return bass.AP(t, off, [list(d) for d in dims])



def build(cfg, phases=99):
    NDCH, NQB, D, T, NH, NKC, NQG, NOB = shapes(cfg)
    HPO = min(4, NH)
    OBW = HPO * 128
    QPG = 512 if T >= 512 else T
    qpo = NDCH // NOB

    nc = bass.Bass("TRN2", target_bir_lowering=False)

    xfm_d = nc.dram_tensor("xfm", [D, T], F32, kind="ExternalInput")
    xtm_d = nc.dram_tensor("xtm", [T, D], F32, kind="ExternalInput")
    wqT_d = nc.dram_tensor("wqT", [D, D], F32, kind="ExternalInput")
    wkT_d = nc.dram_tensor("wkT", [D, D], F32, kind="ExternalInput")
    NOBW = max(1, D // 512) * 128
    wvT_d = nc.dram_tensor("wvT", [NOBW, NDCH * 512], F32, kind="ExternalInput")
    woT_d = nc.dram_tensor("woT", [NOBW, NDCH * 512], F32, kind="ExternalInput")
    wupT_d = nc.dram_tensor("wupT", [NOBW, NDCH * 512], F32, kind="ExternalInput")
    wdnT_d = nc.dram_tensor("wdnT", [NOBW, NDCH * 512], F32, kind="ExternalInput")
    wex_d = nc.dram_tensor("wex", [NOBW, NDCH * 512], F32, kind="ExternalInput")
    wgT_d = nc.dram_tensor("wgT", [D, NE], F32, kind="ExternalInput")
    ballT_d = nc.dram_tensor("ballT", [D, 80], F32, kind="ExternalInput")
    aqT_d = nc.dram_tensor("aqT", [RK, D], F32, kind="ExternalInput")
    akT_d = nc.dram_tensor("akT", [RK, D], F32, kind="ExternalInput")
    avT_d = nc.dram_tensor("avT", [RK, D], F32, kind="ExternalInput")
    aoT_d = nc.dram_tensor("aoT", [RK, D], F32, kind="ExternalInput")
    boT_d = nc.dram_tensor("boT", [D, RK], F32, kind="ExternalInput")
    bqr_d = nc.dram_tensor("bqr", [128, NDCH], F32, kind="ExternalInput")
    bkr_d = nc.dram_tensor("bkr", [128, NDCH], F32, kind="ExternalInput")
    bor_d = nc.dram_tensor("bor", [1, D], F32, kind="ExternalInput")
    bvr_d = nc.dram_tensor("bvr", [1, D], F32, kind="ExternalInput")
    idb_d = nc.dram_tensor("idb", [128, 128], BF16, kind="ExternalInput")
    out_d = nc.dram_tensor("out", [T, D], F32, kind="ExternalOutput")
    Kc_d = nc.dram_tensor("Kc", [D, T], BF16)
    Vc_d = nc.dram_tensor("Vc", [128, NH * T], BF16)
    Qc_d = nc.dram_tensor("Qc", [D, T], BF16)
    Kg_d = nc.dram_tensor("Kg", [NR * D, T], BF16)
    Vg_d = nc.dram_tensor("Vg", [NR * 128, NH * T], BF16)
    opre_d = nc.dram_tensor("opre", [T, D], F16)
    z1tm_d = nc.dram_tensor("z1tm", [T, D], F16)
    s2pre_d = nc.dram_tensor("s2pre", [T, D], F16)

    C = Ctx()

    def run_phase(fns):
        C.emit = False
        snap = dict(C.cnt)
        for f in fns.values():
            f(FakeEngine(), C)
        C.emit = True
        C.cnt = snap
        with nc.Block() as blk:
            if "sync" in fns:
                @blk.sync
                def _(sync, f=fns["sync"]):
                    f(sync, C)
            if "pe" in fns:
                @blk.tensor
                def _(tensor, f=fns["pe"]):
                    f(tensor, C)
            if "act" in fns:
                @blk.scalar
                def _(scalar, f=fns["act"]):
                    f(scalar, C)
            if "dve" in fns:
                @blk.vector
                def _(vector, f=fns["dve"]):
                    f(vector, C)
            if "pool" in fns:
                @blk.gpsimd
                def _(gpsimd, f=fns["pool"]):
                    f(gpsimd, C)
        C.emit = False

    with ExitStack() as ST:
        DMASEMS = ["ms", "cst", "ast", "xs0", "xs1", "wc0", "wc1", "wh1",
                   "ko0", "ko1", "vo0", "vo1", "qo0", "qo1",
                   "qh0", "qh1", "kh0", "kh1", "vh0", "vh1",
                   "wh3", "oo0", "oo1", "op0", "op1", "xr0", "xr1", "z1o",
                   "wh4", "wh1b", "wh3b", "wh4b", "sg0", "sg1", "ss0", "ss1", "zs0", "zs1", "fo0", "fo1"]
        for s in ["pe", "act", "dve", "pool", "ccK", "ccV"] + DMASEMS:
            C.handles[s] = ST.enter_context(nc.semaphore(s))

        idb = ST.enter_context(nc.sbuf_tensor("idb_s", [128, 128], BF16))
        onescol = ST.enter_context(nc.sbuf_tensor("onescol", [128, 1], F32))
        onesrow = ST.enter_context(nc.sbuf_tensor("onesrow", [1, 128], F32))
        bqr = ST.enter_context(nc.sbuf_tensor("bqr_s", [128, NDCH], F32))
        bkr = ST.enter_context(nc.sbuf_tensor("bkr_s", [128, NDCH], F32))
        wgT = ST.enter_context(nc.sbuf_tensor("wgT_s", [128, NDCH * NE], BF16))
        s_all = ST.enter_context(nc.sbuf_tensor("s_all", [128, NQB], F32))
        glogt = ST.enter_context(nc.sbuf_tensor("glogt", [128, NQB * NE], F32))
        bna = ST.enter_context(nc.sbuf_tensor("bna", [128, 2], F32))
        NLC = max(1, D // 512)
        bns = ST.enter_context(nc.sbuf_tensor("bns", [128, 6 * NLC], F32))
        sd1 = ST.enter_context(nc.sbuf_tensor("sd1", [128, 1], F32))
        rstd = ST.enter_context(nc.sbuf_tensor("rstd", [128, 1], F32))
        l1t = ST.enter_context(nc.sbuf_tensor("l1t", [128, 1], F32))
        l2t = ST.enter_context(nc.sbuf_tensor("l2t", [128, 1], F32))
        ddt = ST.enter_context(nc.sbuf_tensor("ddt", [128, 1], F32))
        sg1 = ST.enter_context(nc.sbuf_tensor("sg1", [128, 1], F32))
        sg2 = ST.enter_context(nc.sbuf_tensor("sg2", [128, 1], F32))
        eqm = ST.enter_context(nc.sbuf_tensor("eqm", [128, NE], F32))
        epst = ST.enter_context(nc.sbuf_tensor("epst", [128, 1], F32))
        negbig = ST.enter_context(nc.sbuf_tensor("negbig", [128, 1], F32))
        mskt = ST.enter_context(nc.sbuf_tensor("mskt", [128, NE], F32))

        def wslice(buf, i, w):
            return buf[:, i * w:(i + 1) * w]

        with ExitStack() as SB:
            bvb = SB.enter_context(nc.sbuf_tensor("bvb", [128, D], F32))
            bob = SB.enter_context(nc.sbuf_tensor("bob", [128, D], F32))

            with ExitStack() as SA:
                xstg = [SA.enter_context(nc.sbuf_tensor(f"xstg{i}", [128, T], F32)) for i in range(2)]
                xfm = SA.enter_context(nc.sbuf_tensor("xfm_bf", [128, NDCH * T], BF16))
                wcf = [SA.enter_context(nc.sbuf_tensor(f"wcf{i}", [128, NDCH * 128], F32)) for i in range(2)]
                wcb = [SA.enter_context(nc.sbuf_tensor(f"wcb{i}", [128, NDCH * 128], BF16)) for i in range(2)]
                whs = [SA.enter_context(nc.sbuf_tensor(f"whs{i}", [128, NDCH * 128], F32)) for i in range(2)]
                wsb = SA.enter_context(nc.sbuf_tensor("wsb", [128, NDCH * 512], BF16))
                kstg = [SA.enter_context(nc.sbuf_tensor(f"kstg{i}", [128, T], BF16)) for i in range(2)]
                vstg = [SA.enter_context(nc.sbuf_tensor(f"vstg{i}", [128, 512], BF16)) for i in range(2)]
                cst = SA.enter_context(nc.sbuf_tensor("cst", [128, max(NDCH * 80, D)], F32))
                ast = SA.enter_context(nc.sbuf_tensor("ast", [RK, D], F32))
                ballT = SA.enter_context(nc.sbuf_tensor("ballT_s", [128, NDCH * 80], BF16))
                aT = {}
                for nm in ("aq", "ak", "av"):
                    aT[nm] = SA.enter_context(nc.sbuf_tensor(nm + "T_s", [RK, D], BF16))
                u_q = SA.enter_context(nc.sbuf_tensor("u_q", [RK, T], BF16))
                u_k = SA.enter_context(nc.sbuf_tensor("u_k", [RK, T], BF16))
                u_v = SA.enter_context(nc.sbuf_tensor("u_v", [RK, T], BF16))
                pp = [SA.enter_context(nc.psum_tensor(f"pp{i}", [128, 512], F32)) for i in range(2)]
                kp = [SA.enter_context(nc.psum_tensor(f"kp{i}", [128, 512], F32)) for i in range(2)]
                vp = [SA.enter_context(nc.psum_tensor(f"vp{i}", [128, 512], F32)) for i in range(2)]

                def p0_sync(e, C):
                    CW = max(NDCH * 80, D)
                    for (dst, srct, key) in ((idb, idb_d, "idb"), (bqr, bqr_d, "bqr"),
                                             (bkr, bkr_d, "bkr")):
                        C.inc("ms", e.dma_start(dst[:, :], srct[:, :]), key, 16)
                    C.mark("ms", "ms5")
                    C.inc("cst", e.dma_start(
                        AP3(cst, 0, (CW, 128), (80, NDCH), (1, 80)),
                        AP3(ballT_d, 0, (80, 128), (128 * 80, NDCH), (1, 80))), "ball", 16)
                    for i, nm in enumerate(("aq", "ak", "av")):
                        if i > 0:
                            C.wait(e, "pool", ("acast", i - 1))
                        C.inc("ast", e.dma_start(ast[:, :], {"aq": aqT_d, "ak": akT_d,
                                                             "av": avT_d}[nm][:, :]),
                              ("adma", i), 16)
                    C.wait(e, "pool", "ballcast")
                    C.inc("cst", e.dma_start(
                        AP3(cst, 0, (CW, 128), (NE, NDCH), (1, NE)),
                        AP3(wgT_d, 0, (NE, 128), (128 * NE, NDCH), (1, NE))), "wgdma", 16)
                    C.wait(e, "pool", "wgcast")
                    C.inc("cst", e.dma_start(cst[0:1, 0:D], bvr_d[0:1, :]), "bvrow", 16)
                    C.wait(e, "pe", ("bvmm", NOB - 1))
                    C.inc("cst", e.dma_start(cst[0:1, 0:D], bor_d[0:1, :]), "borow", 16)
                    for i in range(NDCH):
                        if i >= 2:
                            C.wait(e, "pool", ("xcast", i - 2))
                        C.inc("xs%d" % (i % 2), e.dma_start(xstg[i % 2][:, :],
                                                            xfm_d[i * 128:(i + 1) * 128, :]),
                              ("xdma", i), 16)
                    for s in ("ms", "cst", "ast", "xs0", "xs1"):
                        C.mark(s, "p0all")
                        C.wait(e, s, "p0all")

                def p0_pool(e, C):
                    CW = max(NDCH * 80, D)
                    e.memset(onescol[:, :], 1.0)
                    e.memset(epst[:, :], EPS)
                    e.memset(negbig[:, :], -1e30)
                    C.inc("pool", e.memset(onesrow[:, :], 1.0), "ones")
                    C.wait(e, "cst", "ball")
                    C.inc("pool", e.tensor_copy(
                        AP3(ballT, 0, (NDCH * 80, 128), (80, NDCH), (1, 80)),
                        AP3(cst, 0, (CW, 128), (80, NDCH), (1, 80))), "ballcast")
                    for i, nm in enumerate(("aq", "ak", "av")):
                        C.wait(e, "ast", ("adma", i))
                        C.inc("pool", e.tensor_copy(aT[nm][:, :], ast[:, :]), ("acast", i))
                    C.wait(e, "cst", "wgdma")
                    C.inc("pool", e.tensor_copy(
                        AP3(wgT, 0, (NDCH * NE, 128), (NE, NDCH), (1, NE)),
                        AP3(cst, 0, (CW, 128), (NE, NDCH), (1, NE))), "wgcast")
                    for i in range(NDCH):
                        C.wait(e, "xs%d" % (i % 2), ("xdma", i))
                        C.inc("pool", e.tensor_copy(xfm[:, i * T:(i + 1) * T], xstg[i % 2][:, :]),
                              ("xcast", i))

                def p0_pe(e, C):
                    C.wait(e, "pool", "ones")
                    C.wait(e, "ms", "ms5")
                    w = min(512, D)
                    C.wait(e, "cst", "bvrow")
                    for j in range(NOB):
                        C.inc("pe", e.matmul(pp[0][:, 0:w], onesrow[:, :],
                                             cst[0:1, j * w:(j + 1) * w], start=True, stop=True),
                              ("bvmm", j))
                        C.wait(e, "dve", ("bvc", j))
                    C.wait(e, "cst", "borow")
                    for j in range(NOB):
                        C.inc("pe", e.matmul(pp[1][:, 0:w], onesrow[:, :],
                                             cst[0:1, j * w:(j + 1) * w], start=True, stop=True),
                              ("bomm", j))
                        C.wait(e, "dve", ("boc", j))

                def p0_dve(e, C):
                    w = min(512, D)
                    for j in range(NOB):
                        C.wait(e, "pe", ("bvmm", j))
                        C.inc("dve", e.tensor_copy(bvb[:, j * w:(j + 1) * w], pp[0][:, 0:w]), ("bvc", j))
                    for j in range(NOB):
                        C.wait(e, "pe", ("bomm", j))
                        C.inc("dve", e.tensor_copy(bob[:, j * w:(j + 1) * w], pp[1][:, 0:w]), ("boc", j))

                if phases >= 1:
                    run_phase(dict(sync=p0_sync, pool=p0_pool, pe=p0_pe, dve=p0_dve))

                def p1_sync(e, C):
                    for oc in range(NDCH):
                        if oc >= 2:
                            C.wait(e, "pool", ("wkcast", oc - 2))
                        C.inc("wc%d" % (oc % 2), e.dma_start(
                            wcf[oc % 2][:, :], wkT_d[oc * 128:(oc + 1) * 128, :]), ("wkdma", oc), 16)
                        if oc > 0:
                            C.wait(e, "act", ("kevac", oc - 1, NQG - 1))
                            C.inc("ko%d" % ((oc - 1) % 2), e.dma_start(Kc_d[(oc - 1) * 128:oc * 128, :],
                                                      kstg[(oc - 1) % 2][:, :]), ("kout", oc - 1), 16)
                    C.wait(e, "act", ("kevac", NDCH - 1, NQG - 1))
                    C.inc("ko%d" % ((NDCH - 1) % 2), e.dma_start(Kc_d[(NDCH - 1) * 128:NDCH * 128, :],
                                              kstg[(NDCH - 1) % 2][:, :]), ("kout", NDCH - 1), 16)
                    for ob in range(NOB):
                        if ob > 0:
                            for qb in range(NQB):
                                C.wait(e, "dve", ("vevac", ob - 1, qb))
                                h0 = (ob - 1) * HPO
                                C.inc("vo%d" % (((ob - 1) * NQB + qb) % 2), e.dma_start(
                                    AP3(Vc_d, h0 * T + qb * 128, (NH * T, 128), (T, HPO), (1, 128)),
                                    AP3(vstg[((ob - 1) * NQB + qb) % 2], 0, (512, 128), (128, HPO), (1, 128))),
                                    ("vout", ob - 1, qb), 16)
                        for hh in range(4):
                            gi = ob * 4 + hh
                            if gi >= 2:
                                C.wait(e, "dve", ("wvcast", (gi - 2) // 4, (gi - 2) % 4))
                            PW = NDCH * 128
                            C.inc("wh1" if gi % 2 == 0 else "wh1b", e.dma_start(
                                whs[gi % 2][:, :],
                                wvT_d[ob * 128:(ob + 1) * 128, hh * PW:(hh + 1) * PW]),
                                ("wvdma", ob, hh), 16)
                    for qb in range(NQB):
                        C.wait(e, "dve", ("vevac", NOB - 1, qb))
                        h0 = (NOB - 1) * HPO
                        C.inc("vo%d" % (((NOB - 1) * NQB + qb) % 2), e.dma_start(
                            AP3(Vc_d, h0 * T + qb * 128, (NH * T, 128), (T, HPO), (1, 128)),
                            AP3(vstg[((NOB - 1) * NQB + qb) % 2], 0, (512, 128), (128, HPO), (1, 128))),
                            ("vout", NOB - 1, qb), 16)
                    for oc in range(NDCH):
                        if oc < 2:
                            C.wait(e, "pool", ("wkcast", NDCH - 1))
                        else:
                            C.wait(e, "dve", ("wqcast", oc - 2))
                        C.inc("wc%d" % (oc % 2), e.dma_start(
                            wcf[oc % 2][:, :], wqT_d[oc * 128:(oc + 1) * 128, :]), ("wqdma", oc), 16)
                        if oc > 0:
                            C.wait(e, "act", ("qevac", oc - 1, NQG - 1))
                            C.inc("qo%d" % ((oc - 1) % 2), e.dma_start(Qc_d[(oc - 1) * 128:oc * 128, :],
                                                                       kstg[(oc - 1) % 2][:, :]),
                                  ("qout", oc - 1), 16)
                    C.wait(e, "act", ("qevac", NDCH - 1, NQG - 1))
                    C.inc("qo%d" % ((NDCH - 1) % 2), e.dma_start(Qc_d[(NDCH - 1) * 128:NDCH * 128, :],
                                                                 kstg[(NDCH - 1) % 2][:, :]),
                          ("qout", NDCH - 1), 16)
                    for s in ("wc0", "wc1", "wh1", "wh1b", "ko0", "ko1", "vo0", "vo1", "qo0", "qo1"):
                        C.mark(s, "p1all")
                        C.wait(e, s, "p1all")

                def p1_pool(e, C):
                    for oc in range(NDCH):
                        C.wait(e, "wc%d" % (oc % 2), ("wkdma", oc))
                        if oc >= 2:
                            C.wait(e, "pe", ("kmm", oc - 2, NQG - 1))
                        C.inc("pool", e.tensor_copy(wcb[oc % 2][:, :], wcf[oc % 2][:, :]), ("wkcast", oc))
                    C.wait(e, "ko0", ("kout", NDCH - 2))
                    C.wait(e, "ko1", ("kout", NDCH - 1))
                    C.inc("ccK", e.collective_compute(
                        "AllGather", ALU.bypass,
                        replica_groups=[[0, 1, 2, 3], [4, 5, 6, 7]],
                        ins=[Kc_d.ap()], outs=[Kg_d.ap()]), "done", 1)
                    iv0 = (NOB - 1) * NQB + NQB - 1
                    if iv0 >= 1:
                        C.wait(e, "vo%d" % ((iv0 - 1) % 2), ("vout", (iv0 - 1) // NQB, (iv0 - 1) % NQB))
                    C.wait(e, "vo%d" % (iv0 % 2), ("vout", NOB - 1, NQB - 1))
                    C.inc("ccV", e.collective_compute(
                        "AllGather", ALU.bypass,
                        replica_groups=[[0, 1, 2, 3], [4, 5, 6, 7]],
                        ins=[Vc_d.ap()], outs=[Vg_d.ap()]), "done", 1)

                def p1_pe(e, C):
                    for qg in range(NQG):
                        if qg >= 2:
                            C.wait(e, "act", ("uevac", qg - 2))
                        for i in range(NDCH):
                            ii = e.matmul(pp[qg % 2][0:80, 0:QPG], ballT[:, i * 80:(i + 1) * 80],
                                          xfm[:, i * T + qg * QPG: i * T + (qg + 1) * QPG],
                                          start=(i == 0), stop=(i == NDCH - 1))
                        C.inc("pe", ii, ("umm", qg))
                    for oc in range(NDCH):
                        C.wait(e, "pool", ("wkcast", oc))
                        for qg in range(NQG):
                            idx = oc * NQG + qg
                            if idx >= 2:
                                C.wait(e, "act", ("kevac", (idx - 2) // NQG, (idx - 2) % NQG))
                            elif NQG == 1 and oc == 1:
                                pass
                            if idx < 2:
                                C.wait(e, "act", ("uevac", NQG - 1))
                            for i in range(NDCH):
                                e.matmul(kp[idx % 2][:, 0:QPG], wslice(wcb[oc % 2], i, 128),
                                         xfm[:, i * T + qg * QPG: i * T + (qg + 1) * QPG],
                                         start=(i == 0), stop=False)
                            ii = e.matmul(kp[idx % 2][:, 0:QPG], aT["ak"][:, oc * 128:(oc + 1) * 128],
                                          u_k[:, qg * QPG:(qg + 1) * QPG], start=False, stop=True)
                            C.inc("pe", ii, ("kmm", oc, qg))
                    for ob in range(NOB):
                        for qb in range(NQB):
                            idx = ob * NQB + qb
                            if idx >= 2:
                                C.wait(e, "dve", ("vevac", (idx - 2) // NQB, (idx - 2) % NQB))
                            for i in range(NDCH):
                                if qb == 0:
                                    C.wait(e, "dve", ("wvcast", ob, ((i + 1) * 512 - 1) // (NDCH * 128)))
                                e.matmul(vp[idx % 2][:, 0:512], xfm[:, i * T + qb * 128: i * T + (qb + 1) * 128],
                                         wslice(wsb, i, 512), start=(i == 0), stop=False)
                            ii = e.matmul(vp[idx % 2][:, 0:512], u_v[:, qb * 128:(qb + 1) * 128],
                                          aT["av"][:, ob * OBW:(ob + 1) * OBW], start=False, stop=True)
                            C.inc("pe", ii, ("vmm", ob, qb))
                    for oc in range(NDCH):
                        C.wait(e, "dve", ("wqcast", oc))
                        for qg in range(NQG):
                            idx = oc * NQG + qg
                            if idx >= 2:
                                C.wait(e, "act", ("qevac", (idx - 2) // NQG, (idx - 2) % NQG))
                            else:
                                C.wait(e, "act", ("kevac", NDCH - 1, NQG - 1))
                            for i in range(NDCH):
                                e.matmul(kp[idx % 2][:, 0:QPG], wslice(wcb[oc % 2], i, 128),
                                         xfm[:, i * T + qg * QPG: i * T + (qg + 1) * QPG],
                                         start=(i == 0), stop=False)
                            ii = e.matmul(kp[idx % 2][:, 0:QPG], aT["aq"][:, oc * 128:(oc + 1) * 128],
                                          u_q[:, qg * QPG:(qg + 1) * QPG], start=False, stop=True)
                            C.inc("pe", ii, ("qmm", oc, qg))

                def p1_act(e, C):
                    for qg in range(NQG):
                        C.wait(e, "pe", ("umm", qg))
                        e.activation(u_q[:, qg * QPG:(qg + 1) * QPG], pp[qg % 2][0:16, 0:QPG], AF.Copy, scale=0.5)
                        e.activation(u_k[:, qg * QPG:(qg + 1) * QPG], pp[qg % 2][32:48, 0:QPG], AF.Copy, scale=0.5)
                        ii = e.activation(u_v[:, qg * QPG:(qg + 1) * QPG], pp[qg % 2][64:80, 0:QPG], AF.Copy, scale=0.5)
                        C.inc("act", ii, ("uevac", qg))
                    C.mark("act", ("uevac", NQG - 1))
                    for oc in range(NDCH):
                        for qg in range(NQG):
                            C.wait(e, "pe", ("kmm", oc, qg))
                            if qg == 0 and oc >= 2:
                                C.wait(e, "ko%d" % (oc % 2), ("kout", oc - 2))
                            ii = e.activation(kstg[oc % 2][:, qg * QPG:(qg + 1) * QPG],
                                              kp[(oc * NQG + qg) % 2][:, 0:QPG], AF.Identity,
                                              bias=bkr[:, oc:oc + 1])
                            C.inc("act", ii, ("kevac", oc, qg))
                    for oc in range(NDCH):
                        for qg in range(NQG):
                            C.wait(e, "pe", ("qmm", oc, qg))
                            if qg == 0:
                                if oc >= 2:
                                    C.wait(e, "qo%d" % (oc % 2), ("qout", oc - 2))
                                else:
                                    C.wait(e, "ko%d" % (oc % 2), ("kout", NDCH - 2 + (oc % 2)))
                            ii = e.activation(kstg[oc % 2][:, qg * QPG:(qg + 1) * QPG],
                                              kp[(oc * NQG + qg) % 2][:, 0:QPG], AF.Identity,
                                              bias=bqr[:, oc:oc + 1])
                            C.inc("act", ii, ("qevac", oc, qg))

                def p1_dve(e, C):
                    for ob in range(NOB):
                        for hh in range(4):
                            gi = ob * 4 + hh
                            C.wait(e, "wh1" if gi % 2 == 0 else "wh1b", ("wvdma", ob, hh))
                            if ob > 0 and hh == 0:
                                C.wait(e, "pe", ("vmm", ob - 1, NQB - 1))
                            PW = NDCH * 128
                            C.inc("dve", e.tensor_copy(
                                wsb[:, hh * PW:(hh + 1) * PW], whs[gi % 2][:, :]),
                                ("wvcast", ob, hh))
                        for qb in range(NQB):
                            idx = ob * NQB + qb
                            C.wait(e, "pe", ("vmm", ob, qb))
                            if idx >= 2:
                                C.wait(e, "vo%d" % (idx % 2), ("vout", (idx - 2) // NQB, (idx - 2) % NQB))
                            ii = e.tensor_tensor(vstg[idx % 2][:, :], vp[idx % 2][:, 0:512],
                                                 bvb[:, ob * OBW:(ob + 1) * OBW], ALU.add)
                            C.inc("dve", ii, ("vevac", ob, qb))
                    for oc in range(NDCH):
                        C.wait(e, "wc%d" % (oc % 2), ("wqdma", oc))
                        if oc >= 2:
                            C.wait(e, "pe", ("qmm", oc - 2, NQG - 1))
                        else:
                            C.wait(e, "pe", ("kmm", NDCH - 2 + oc, NQG - 1))
                        C.inc("dve", e.tensor_copy(wcb[oc % 2][:, :], wcf[oc % 2][:, :]), ("wqcast", oc))

                if phases >= 2:
                    run_phase(dict(sync=p1_sync, pool=p1_pool, pe=p1_pe, act=p1_act, dve=p1_dve))

            with ExitStack() as SC:
                attn = SC.enter_context(nc.sbuf_tensor("attn", [128, NH * T], BF16))

                with ExitStack() as SD:
                    qh = [SD.enter_context(nc.sbuf_tensor(f"qh{i}", [128, T], BF16)) for i in range(2)]
                    kh = [SD.enter_context(nc.sbuf_tensor(f"kh{i}", [128, NR * T], BF16)) for i in range(2)]
                    vh = [SD.enter_context(nc.sbuf_tensor(f"vh{i}", [128, NR * T], BF16)) for i in range(2)]
                    pt = [SD.enter_context(nc.sbuf_tensor(f"pt{i}", [128, T], BF16)) for i in range(4)]
                    acc = SD.enter_context(nc.sbuf_tensor("acc", [128, T], F32))
                    pairb = SD.enter_context(nc.sbuf_tensor("pairb", [128, T], BF16))
                    rb = SD.enter_context(nc.sbuf_tensor("rb", [128, T], F32))
                    ssum = SD.enter_context(nc.sbuf_tensor("ssum", [1, T], F32))
                    srec = SD.enter_context(nc.sbuf_tensor("srec", [1, T], F32))
                    GRPW = min(1024, T)
                    stp = [SD.enter_context(nc.psum_tensor(f"stp{i}", [128, GRPW], F32)) for i in range(2)]
                    avp = SD.enter_context(nc.psum_tensor("avp", [128, T], F32))

                    GROUPS = []
                    for g in range((NQG + 1) // 2):
                        GROUPS.append((g, [q for q in (2 * g, 2 * g + 1) if q < NQG]))
                    LASTG = GROUPS[-1][0]
                    SPAN = NQG * QPG

                    def p2_sync(e, C):
                        for h in range(NH):
                            if h == 0:
                                C.wait_val(e, "ccK", 1)
                                C.wait_val(e, "ccV", 1)
                            if h >= 2:
                                C.wait(e, "pe", ("hpe", h - 2))
                            C.inc("qh%d" % (h % 2), e.dma_start(qh[h % 2][:, :],
                                                                 Qc_d[h * 128:(h + 1) * 128, :]), ("q", h), 16)
                            for r in range(NR):
                                C.inc("kh%d" % (h % 2), e.dma_start(kh[h % 2][:, r * T:(r + 1) * T],
                                                         Kg_d[r * D + h * 128: r * D + (h + 1) * 128, :]),
                                      ("k", h) if r == NR - 1 else None, 16)
                            for r in range(NR):
                                C.inc("vh%d" % (h % 2), e.dma_start(vh[h % 2][:, r * T:(r + 1) * T],
                                                         Vg_d[r * 128:(r + 1) * 128, h * T:(h + 1) * T]),
                                      ("v", h) if r == NR - 1 else None, 16)
                        for s in ("qh0", "qh1", "kh0", "kh1", "vh0", "vh1"):
                            C.mark(s, "p2all")
                            C.wait(e, s, "p2all")

                    def p2_pe(e, C):
                        for h in range(NH):
                            C.wait(e, "qh%d" % (h % 2), ("q", h))
                            C.wait(e, "kh%d" % (h % 2), ("k", h))
                            C.wait(e, "vh%d" % (h % 2), ("v", h))
                            if h > 0:
                                C.wait(e, "dve", ("rb", h - 1))
                                C.wait(e, "dve", ("norm", h - 1))
                            for kc in range(NKC):
                                for (g, qgs) in GROUPS:
                                    if kc > 0:
                                        C.wait(e, "act", ("exp", h, kc - 1, g))
                                    for j, qg in enumerate(qgs):
                                        ii = e.matmul(stp[g][:, j * QPG:(j + 1) * QPG],
                                                      kh[h % 2][:, kc * 128:(kc + 1) * 128],
                                                      qh[h % 2][:, qg * QPG:(qg + 1) * QPG],
                                                      start=True, stop=True)
                                    C.inc("pe", ii, ("qk", h, kc, g))
                                if kc > 0:
                                    kcp = kc - 1
                                    for qg in range(NQG):
                                        ii = e.matmul(avp[:, qg * QPG:(qg + 1) * QPG],
                                                      vh[h % 2][:, kcp * 128:(kcp + 1) * 128],
                                                      pt[kcp % 4][:, qg * QPG:(qg + 1) * QPG],
                                                      start=(kcp == 0), stop=False)
                                    C.inc("pe", ii, ("av", h, kcp))
                            kcp = NKC - 1
                            C.wait(e, "act", ("exp", h, kcp, LASTG))
                            for qg in range(NQG):
                                ii = e.matmul(avp[:, qg * QPG:(qg + 1) * QPG],
                                              vh[h % 2][:, kcp * 128:(kcp + 1) * 128],
                                              pt[kcp % 4][:, qg * QPG:(qg + 1) * QPG],
                                              start=(kcp == 0), stop=True)
                            C.inc("pe", ii, ("av", h, kcp))
                            C.wait(e, "dve", ("sum", h, NKC - 1))
                            for qg in range(NQG):
                                g, j = qg // 2, qg % 2
                                ii = e.matmul(stp[g][0:1, j * QPG:(j + 1) * QPG], onescol[:, :],
                                              acc[:, qg * QPG:(qg + 1) * QPG], start=True, stop=True)
                            C.inc("pe", ii, ("summ", h))
                            C.wait(e, "act", ("recip", h))
                            for qg in range(NQG):
                                g, j = qg // 2, qg % 2
                                ii = e.matmul(stp[g][:, j * QPG:(j + 1) * QPG], onesrow[:, :],
                                              srec[0:1, qg * QPG:(qg + 1) * QPG], start=True, stop=True)
                            C.inc("pe", ii, ("bcmm", h))
                            C.mark("pe", ("hpe", h))

                    def p2_act(e, C):
                        for h in range(NH):
                            for kc in range(NKC):
                                if kc >= 4:
                                    C.wait(e, "pe", ("av", h, kc - 4))
                                    j = kc - 3 if (kc - 3) % 2 == 1 else kc - 4
                                    if j >= 1:
                                        C.wait(e, "dve", ("sum", h, j))
                                elif h > 0:
                                    C.wait(e, "pe", ("av", h - 1, NKC - 1))
                                    C.wait(e, "dve", ("sum", h - 1, NKC - 1))
                                for (g, qgs) in GROUPS:
                                    C.wait(e, "pe", ("qk", h, kc, g))
                                    span = len(qgs) * QPG
                                    ii = e.activation(pt[kc % 4][:, 2 * g * QPG: 2 * g * QPG + span],
                                                      stp[g][:, 0:span], AF.Exp, scale=SCALE)
                                    C.inc("act", ii, ("exp", h, kc, g))
                            C.wait(e, "pe", ("summ", h))
                            for qg in range(NQG):
                                g, j = qg // 2, qg % 2
                                ii = e.copy(ssum[0:1, qg * QPG:(qg + 1) * QPG],
                                            stp[g][0:1, j * QPG:(j + 1) * QPG])
                            C.inc("act", ii, ("ssum", h))
                            C.wait(e, "act", ("ssum", h))
                            ii = e.activation(srec[0:1, :], ssum[0:1, :], AF.Ln)
                            C.inc("act", ii, ("lns", h))
                            C.wait(e, "act", ("lns", h))
                            ii = e.activation(srec[0:1, :], srec[0:1, :], AF.Exp, scale=-1.0)
                            C.inc("act", ii, ("recip", h))

                    def p2_dve(e, C):
                        sl = slice(0, SPAN)
                        for h in range(NH):
                            for kc in range(1, NKC, 2):
                                C.wait(e, "act", ("exp", h, kc, LASTG))
                                if kc >= 3:
                                    C.wait(e, "dve", ("sum", h, kc - 2))
                                ii = e.tensor_tensor(pairb[:, sl], pt[(kc - 1) % 4][:, sl],
                                                     pt[kc % 4][:, sl], ALU.add)
                                C.inc("dve", ii, ("pair", h, kc))
                                C.wait(e, "dve", ("pair", h, kc))
                                if kc == 1:
                                    ii = e.tensor_copy(acc[:, sl], pairb[:, sl])
                                else:
                                    ii = e.tensor_tensor(acc[:, sl], acc[:, sl], pairb[:, sl], ALU.add)
                                C.inc("dve", ii, ("sum", h, kc))
                            C.wait(e, "pe", ("bcmm", h))
                            for qg in range(NQG):
                                g, j = qg // 2, qg % 2
                                ii = e.tensor_copy(rb[:, qg * QPG:(qg + 1) * QPG],
                                                   stp[g][:, j * QPG:(j + 1) * QPG])
                            C.inc("dve", ii, ("rb", h))
                            C.wait(e, "dve", ("rb", h))
                            C.wait(e, "pe", ("av", h, NKC - 1))
                            for qg in range(NQG):
                                sl = slice(qg * QPG, (qg + 1) * QPG)
                                ii = e.tensor_tensor(attn[:, h * T + qg * QPG: h * T + (qg + 1) * QPG],
                                                     avp[:, sl], rb[:, sl], ALU.mult)
                            C.inc("dve", ii, ("norm", h))

                    if phases >= 3:
                        run_phase(dict(sync=p2_sync, pe=p2_pe, act=p2_act, dve=p2_dve))

                with ExitStack() as SE:
                    whs3 = [SE.enter_context(nc.sbuf_tensor(f"whs3{i}", [128, NDCH * 128], F32)) for i in range(2)]
                    wsb3 = SE.enter_context(nc.sbuf_tensor("wsb3", [128, NDCH * 512], BF16))
                    aoT = SE.enter_context(nc.sbuf_tensor("aoT_s", [RK, D], BF16))
                    boT = SE.enter_context(nc.sbuf_tensor("boT_s", [128, NDCH * RK], BF16))
                    u_o = SE.enter_context(nc.sbuf_tensor("u_o", [RK, T], BF16))
                    ostg = [SE.enter_context(nc.sbuf_tensor(f"ostg{i}", [128, 512], F16)) for i in range(2)]
                    po = [SE.enter_context(nc.psum_tensor(f"po{i}", [128, 512], F32)) for i in range(2)]
                    puo = SE.enter_context(nc.psum_tensor("puo", [128, 512], F32))

                    def p3a_sync(e, C):
                        C.inc("wh3", e.dma_start(whs3[0][0:RK, 0:D], aoT_d[:, :]), "aodma", 16)
                        C.wait(e, "pool", "aocast")
                        C.inc("wh3", e.dma_start(
                            AP3(whs3[0], 0, (NDCH * 128, 128), (128, NDCH), (1, RK)),
                            AP3(boT_d, 0, (RK, 128), (128 * RK, NDCH), (1, RK))), "bodma", 16)
                        for ob in range(NOB):
                            if ob > 0:
                                for qb in range(NQB):
                                    C.wait(e, "dve", ("oevac", ob - 1, qb))
                                    C.inc("oo%d" % (((ob - 1) * NQB + qb) % 2), e.dma_start(
                                        opre_d[qb * 128:(qb + 1) * 128, (ob - 1) * OBW: ob * OBW],
                                        ostg[((ob - 1) * NQB + qb) % 2][:, 0:OBW]), ("oout", ob - 1, qb), 16)
                            for hh in range(4):
                                gi = ob * 4 + hh
                                if gi >= 2:
                                    C.wait(e, "dve", ("wocast", (gi - 2) // 4, (gi - 2) % 4))
                                elif gi == 0:
                                    C.wait(e, "pool", "bocast")
                                PW = NDCH * 128
                                C.inc("wh3" if gi % 2 == 0 else "wh3b", e.dma_start(
                                    whs3[gi % 2][:, :],
                                    woT_d[ob * 128:(ob + 1) * 128, hh * PW:(hh + 1) * PW]),
                                    ("wodma", ob, hh), 16)
                        for qb in range(NQB):
                            C.wait(e, "dve", ("oevac", NOB - 1, qb))
                            C.inc("oo%d" % (((NOB - 1) * NQB + qb) % 2), e.dma_start(
                                opre_d[qb * 128:(qb + 1) * 128, (NOB - 1) * OBW: NOB * OBW],
                                ostg[((NOB - 1) * NQB + qb) % 2][:, 0:OBW]), ("oout", NOB - 1, qb), 16)
                        for s in ("wh3", "wh3b", "oo0", "oo1"):
                            C.mark(s, "p3aall")
                            C.wait(e, s, "p3aall")

                    def p3a_pool(e, C):
                        C.wait(e, "wh3", "aodma")
                        C.inc("pool", e.tensor_copy(aoT[:, :], whs3[0][0:RK, 0:D]), "aocast")
                        C.wait(e, "wh3", "bodma")
                        C.inc("pool", e.tensor_copy(
                            AP3(boT, 0, (NDCH * RK, 128), (RK, NDCH), (1, RK)),
                            AP3(whs3[0], 0, (NDCH * 128, 128), (128, NDCH), (1, RK))), "bocast")

                    def p3a_pe(e, C):
                        C.wait(e, "pool", "bocast")
                        for qg in range(NQG):
                            if qg >= 1:
                                C.wait(e, "act", ("uoevac", qg - 1))
                            for i in range(NDCH):
                                ii = e.matmul(puo[0:RK, 0:QPG], boT[:, i * RK:(i + 1) * RK],
                                              attn[:, i * T + qg * QPG: i * T + (qg + 1) * QPG],
                                              start=(i == 0), stop=(i == NDCH - 1))
                            C.inc("pe", ii, ("uomm", qg))
                        for ob in range(NOB):
                            for qb in range(NQB):
                                idx = ob * NQB + qb
                                if idx >= 2:
                                    C.wait(e, "dve", ("oevac", (idx - 2) // NQB, (idx - 2) % NQB))
                                elif idx == 0:
                                    C.wait(e, "act", ("uoevac", NQG - 1))
                                for i in range(NDCH):
                                    if qb == 0:
                                        C.wait(e, "dve", ("wocast", ob, ((i + 1) * 512 - 1) // (NDCH * 128)))
                                    e.matmul(po[idx % 2][:, 0:512],
                                             attn[:, i * T + qb * 128: i * T + (qb + 1) * 128],
                                             wslice(wsb3, i, 512), start=(i == 0), stop=False)
                                ii = e.matmul(po[idx % 2][:, 0:512], u_o[:, qb * 128:(qb + 1) * 128],
                                              aoT[:, ob * OBW:(ob + 1) * OBW], start=False, stop=True)
                                C.inc("pe", ii, ("omm", ob, qb))

                    def p3a_act(e, C):
                        for qg in range(NQG):
                            C.wait(e, "pe", ("uomm", qg))
                            ii = e.activation(u_o[:, qg * QPG:(qg + 1) * QPG], puo[0:RK, 0:QPG],
                                              AF.Copy, scale=0.5)
                            C.inc("act", ii, ("uoevac", qg))

                    def p3a_dve(e, C):
                        for ob in range(NOB):
                            for hh in range(4):
                                gi = ob * 4 + hh
                                C.wait(e, "wh3" if gi % 2 == 0 else "wh3b", ("wodma", ob, hh))
                                if ob > 0 and hh == 0:
                                    C.wait(e, "pe", ("omm", ob - 1, NQB - 1))
                                PW = NDCH * 128
                                C.inc("dve", e.tensor_copy(
                                    wsb3[:, hh * PW:(hh + 1) * PW], whs3[gi % 2][:, :]),
                                    ("wocast", ob, hh))
                            for qb in range(NQB):
                                idx = ob * NQB + qb
                                C.wait(e, "pe", ("omm", ob, qb))
                                if idx >= 2:
                                    C.wait(e, "oo%d" % (idx % 2), ("oout", (idx - 2) // NQB, (idx - 2) % NQB))
                                ii = e.tensor_tensor(ostg[idx % 2][:, 0:OBW], po[idx % 2][:, 0:512],
                                                     bob[:, ob * OBW:(ob + 1) * OBW], ALU.add)
                                C.inc("dve", ii, ("oevac", ob, qb))

                    if phases >= 4:
                        run_phase(dict(sync=p3a_sync, pool=p3a_pool, pe=p3a_pe, act=p3a_act, dve=p3a_dve))

        with ExitStack() as SF:
            z1fm = SF.enter_context(nc.sbuf_tensor("z1fm", [128, NDCH * T], BF16))
            ROT = 8 if NDCH >= 8 else NDCH

            with ExitStack() as SG:
                opstr = [SG.enter_context(nc.sbuf_tensor(f"opstr{i}", [128, D], F16)) for i in range(2)]
                xstr = [SG.enter_context(nc.sbuf_tensor(f"xstr{i}", [128, D], F32)) for i in range(2)]
                s1 = SG.enter_context(nc.sbuf_tensor("s1", [128, D], F32))
                z1h = SG.enter_context(nc.sbuf_tensor("z1h", [128, D], F16))
                z1b = SG.enter_context(nc.sbuf_tensor("z1b", [128, D], BF16))
                trp = [SG.enter_context(nc.psum_tensor(f"trp{i}", [128, 512], BF16)) for i in range(2)]

                NG4 = NDCH // 4 if NDCH >= 4 else 1

                def tr_slot(qb, i):
                    g = qb * NG4 + i // 4
                    return trp[g % 2][:, (i % 4) * 128:((i % 4) + 1) * 128]

                def p3b_sync(e, C):
                    for qb in range(NQB):
                        if qb >= 2 and _P3BL >= 2:
                            C.wait(e, "dve", ("s1c", qb - 2))
                        C.inc("op%d" % (qb % 2), e.dma_start(opstr[qb % 2][:, :],
                                                              opre_d[qb * 128:(qb + 1) * 128, :]),
                              ("opin", qb), 16)
                        C.inc("xr%d" % (qb % 2), e.dma_start(xstr[qb % 2][:, :],
                                                             xtm_d[qb * 128:(qb + 1) * 128, :]),
                              ("xin", qb), 16)
                        if _P3BL >= 3:
                            C.wait(e, "dve", ("z1hc", qb))
                            C.inc("z1o", e.dma_start(z1tm_d[qb * 128:(qb + 1) * 128, :], z1h[:, :]),
                                  ("z1out", qb), 16)
                    for s in (("op0", "op1", "xr0", "xr1", "z1o") if _P3BL >= 3 else ("op0", "op1", "xr0", "xr1")):
                        C.mark(s, "p3ball")
                        C.wait(e, s, "p3ball")

                def p3b_act(e, C):
                    if _P3BL < 2:
                        return
                    for qb in range(NQB):
                        C.wait(e, "dve", ("bnagg", qb))
                        C.inc("act", e.activation(sd1[:, :], bna[:, 1:2], AF.Sqrt, bias=epst[:, :]), ("sqrt", qb))
                        C.wait(e, "dve", ("z1hc", qb))
                        if qb > 0 and _P3BL >= 4:
                            C.wait(e, "pe", ("trg", qb - 1, NDCH // 4 - 1 if NDCH >= 4 else 0))
                        C.inc("act", e.activation(z1b[:, :], z1h[:, :], AF.Copy), ("z1bc", qb))
                        if _P3BL < 5:
                            continue
                        for i in range(NDCH):
                            if i % 2 == 0 and _TRC != "act":
                                continue
                            if i % 2 == 1 and _TRC == "dve":
                                continue
                            C.wait(e, "pe", ("trg", qb, i // 4))
                            C.inc("act", e.copy(z1fm[:, i * T + qb * 128: i * T + (qb + 1) * 128],
                                                tr_slot(qb, i)), ("trc", qb, i))

                def p3b_dve(e, C):
                    if _P3BL < 2:
                        return
                    for qb in range(NQB):
                        C.wait(e, "op%d" % (qb % 2), ("opin", qb))
                        C.wait(e, "xr%d" % (qb % 2), ("xin", qb))
                        if qb > 0:
                            C.wait(e, "dve", ("z1hc", qb - 1))
                        C.inc("dve", e.tensor_tensor(s1[:, :], opstr[qb % 2][:, :], xstr[qb % 2][:, :],
                                                     ALU.add), ("s1c", qb))
                        C.wait(e, "dve", ("s1c", qb))
                        for j in range(NLC):
                            ii = e.bn_stats(bns[:, 6 * j:6 * (j + 1)], s1[:, 512 * j:512 * (j + 1)])
                        C.inc("dve", ii, ("bnst", qb))
                        C.wait(e, "dve", ("bnst", qb))
                        C.inc("dve", e.bn_aggr(bna[:, :], bns[:, :]), ("bnagg", qb))
                        C.wait(e, "act", ("sqrt", qb))
                        C.inc("dve", e.reciprocal(rstd[:, :], sd1[:, :]), ("rstd", qb))
                        C.wait(e, "dve", ("rstd", qb))
                        if qb > 0 and _P3BL >= 3:
                            C.wait(e, "z1o", ("z1out", qb - 1))
                        if qb > 0:
                            C.wait(e, "act", ("z1bc", qb - 1))
                        C.inc("dve", e.tensor_scalar(z1h[:, :], s1[:, :], bna[:, 0:1], rstd[:, :],
                                                     ALU.subtract, ALU.mult), ("z1hc", qb))
                        if _P3BL < 5:
                            continue
                        for i in range(NDCH):
                            if i % 2 == 1 and _TRC != "dve":
                                continue
                            if i % 2 == 0 and _TRC == "act":
                                continue
                            C.wait(e, "pe", ("trg", qb, i // 4))
                            C.inc("dve", e.tensor_copy(z1fm[:, i * T + qb * 128: i * T + (qb + 1) * 128],
                                                       tr_slot(qb, i)), ("trc", qb, i))

                def p3b_pe(e, C):
                    if _P3BL < 4:
                        return
                    for qb in range(NQB):
                        C.wait(e, "act", ("z1bc", qb))
                        for g4 in range(NG4):
                            pg = qb * NG4 + g4 - 1
                            if pg >= 0 and _P3BL >= 5:
                                pqb, pg4 = divmod(pg, NG4)
                                for i in range(pg4 * 4, pg4 * 4 + 4):
                                    eng = "act" if (_TRC == "act" or (_TRC == "mix" and i % 2 == 1)) else "dve"
                                    C.wait(e, eng, ("trc", pqb, i))
                            for i in range(g4 * 4, g4 * 4 + 4):
                                ii = e.transpose(tr_slot(qb, i), z1b[:, i * 128:(i + 1) * 128], idb[:, :])
                            C.inc("pe", ii, ("trg", qb, g4))

                if phases >= 5:
                    run_phase(dict(sync=p3b_sync, act=p3b_act, dve=p3b_dve, pe=p3b_pe))

            with ExitStack() as SH:
                hfm = SH.enter_context(nc.sbuf_tensor("hfm", [128, NDCH * T], BF16))
                whs4 = [SH.enter_context(nc.sbuf_tensor(f"whs4{i}", [128, NDCH * 64], F32)) for i in range(2)]
                web = SH.enter_context(nc.sbuf_tensor("web", [128, NDCH * 512], BF16))
                wdb = SH.enter_context(nc.sbuf_tensor("wdb", [128, NDCH * 512], BF16))
                wub = web
                glog = SH.enter_context(nc.sbuf_tensor("glog", [NE, T], BF16))
                t1 = SH.enter_context(nc.sbuf_tensor("t1", [128, 512], F32))
                s2stg = [SH.enter_context(nc.sbuf_tensor(f"s2stg{i}", [128, 512], F16)) for i in range(2)]
                s2str = SH.enter_context(nc.sbuf_tensor("s2str", [128, D], F16))
                z1str = SH.enter_context(nc.sbuf_tensor("z1str", [128, D], F16))
                s2f = SH.enter_context(nc.sbuf_tensor("s2f", [128, D], F32))
                ostg4 = SH.enter_context(nc.sbuf_tensor("ostg4", [128, D], F32))
                up_ps = [SH.enter_context(nc.psum_tensor(f"up_ps{i}", [128, 512], F32)) for i in range(2)]
                ep_ps = [SH.enter_context(nc.psum_tensor(f"ep_ps{i}", [128, 512], F32)) for i in range(2)]
                dp_ps = [SH.enter_context(nc.psum_tensor(f"dp_ps{i}", [128, 512], F32)) for i in range(2)]
                gt_ps = SH.enter_context(nc.psum_tensor("gt_ps", [128, NE * 4], BF16))

                SEQ4 = [("wu", ob, hh) for ob in range(NOB) for hh in range(8)] + \
                       [(nm, ob, hh) for ob in range(NOB) for nm in ("we", "wd") for hh in range(8)]
                SRC4 = {"wu": wupT_d, "we": wex_d, "wd": wdnT_d}
                DST4 = {"wu": wub, "we": web, "wd": wdb}
                IDX4 = {k: i for i, k in enumerate(SEQ4)}

                def p4_sync(e, C):
                    def wpiece(key):
                        i = IDX4[key]
                        nm, ob, hh = key
                        if i >= 2:
                            C.wait(e, "dve", ("wcast4", i - 2))
                        PW = NDCH * 64
                        C.inc("wh4" if i % 2 == 0 else "wh4b", e.dma_start(
                            whs4[i % 2][:, :],
                            SRC4[nm][ob * 128:(ob + 1) * 128, hh * PW:(hh + 1) * PW]),
                            ("wdma4", i), 16)
                    for ob in range(NOB):
                        for hh in range(8):
                            wpiece(("wu", ob, hh))
                    for ob in range(NOB):
                        if ob > 0:
                            for qb in range(NQB):
                                C.wait(e, "dve", ("s2evac", ob - 1, qb))
                                C.inc("sg%d" % (((ob - 1) * NQB + qb) % 2), e.dma_start(
                                    s2pre_d[qb * 128:(qb + 1) * 128, (ob - 1) * OBW: ob * OBW],
                                    s2stg[((ob - 1) * NQB + qb) % 2][:, 0:OBW]), ("s2out", ob - 1, qb), 16)
                        for nm in ("we", "wd"):
                            for hh in range(8):
                                wpiece((nm, ob, hh))
                    for qb in range(NQB):
                        C.wait(e, "dve", ("s2evac", NOB - 1, qb))
                        C.inc("sg%d" % (((NOB - 1) * NQB + qb) % 2), e.dma_start(
                            s2pre_d[qb * 128:(qb + 1) * 128, (NOB - 1) * OBW: NOB * OBW],
                            s2stg[((NOB - 1) * NQB + qb) % 2][:, 0:OBW]), ("s2out", NOB - 1, qb), 16)
                        if qb >= 1:
                            C.wait(e, "dve", ("s2f", qb - 1))
                        C.wait(e, "sg%d" % (((NOB - 1) * NQB + qb) % 2), ("s2out", NOB - 1, qb))
                        C.inc("ss0", e.dma_start(s2str[:, :],
                                                 s2pre_d[qb * 128:(qb + 1) * 128, :]),
                              ("s2in", qb), 16)
                        C.inc("zs0", e.dma_start(z1str[:, :],
                                                 z1tm_d[qb * 128:(qb + 1) * 128, :]),
                              ("z1in", qb), 16)
                        if qb >= 1:
                            C.wait(e, "dve", ("outc", qb - 1))
                            C.inc("fo0", e.dma_start(out_d[(qb - 1) * 128:qb * 128, :],
                                                     ostg4[:, :]), ("final", qb - 1), 16)
                    C.wait(e, "dve", ("outc", NQB - 1))
                    C.inc("fo0", e.dma_start(out_d[(NQB - 1) * 128:NQB * 128, :],
                                             ostg4[:, :]), ("final", NQB - 1), 16)
                    for s in ("wh4", "wh4b", "ss0", "zs0", "fo0"):
                        C.mark(s, "p4all")
                        C.wait(e, s, "p4all")

                def p4_wcast(e, C, key):
                    i = IDX4[key]
                    nm, ob, hh = key
                    C.wait(e, "wh4" if i % 2 == 0 else "wh4b", ("wdma4", i))
                    if nm == "wu" and ob > 0 and hh == 0:
                        C.wait(e, "pe", ("upmm", ob - 1, qpo - 1, NQG - 1))
                    if nm == "we" and ob == 0 and hh == 0:
                        C.wait(e, "pe", ("upmm", NOB - 1, qpo - 1, NQG - 1))
                    if nm == "we" and ob > 0 and hh == 0:
                        C.wait(e, "pe", ("edmm", ob - 1, NQB - 1))
                    if nm == "wd" and ob > 0 and hh == 0:
                        C.wait(e, "pe", ("edmm", ob - 1, NQB - 1))
                    PW = NDCH * 64
                    C.inc("dve", e.tensor_copy(
                        DST4[nm][:, hh * PW:(hh + 1) * PW], whs4[i % 2][:, :]),
                        ("wcast4", i))

                def p4_pe(e, C):
                    for ob in range(NOB):
                        C.wait(e, "dve", ("wcast4", IDX4[("wu", ob, 7)]))
                        for j in range(qpo):
                            oc = ob * qpo + j
                            for qg in range(NQG):
                                idx = oc * NQG + qg
                                if idx >= 2:
                                    C.wait(e, "act", ("upevac", ((idx - 2) // NQG), (idx - 2) % NQG))
                                for i in range(NDCH):
                                    ii = e.matmul(up_ps[idx % 2][:, 0:QPG],
                                                  wub[:, i * 512 + (j % 4) * 128: i * 512 + ((j % 4) + 1) * 128],
                                                  z1fm[:, i * T + qg * QPG: i * T + (qg + 1) * QPG],
                                                  start=(i == 0), stop=(i == NDCH - 1))
                                C.inc("pe", ii, ("upmm", ob, j, qg))
                    for qg in range(NQG):
                        if qg == 0:
                            C.wait(e, "act", ("upevac", NDCH - 1, NQG - 1))
                        else:
                            C.wait(e, "act", ("gevac", qg - 1))
                        for i in range(NDCH):
                            ii = e.matmul(up_ps[qg % 2][0:NE, 0:QPG], wgT[:, i * NE:(i + 1) * NE],
                                          z1fm[:, i * T + qg * QPG: i * T + (qg + 1) * QPG],
                                          start=(i == 0), stop=(i == NDCH - 1))
                        C.inc("pe", ii, ("gmm", qg))
                    for qgr in range(NQB // 4):
                        C.wait(e, "act", ("gevac", ((qgr * 4 + 3) * 128) // QPG))
                        if qgr >= 1:
                            C.wait(e, "dve", ("gtc", qgr - 1))
                        for qb in range(qgr * 4, qgr * 4 + 4):
                            ii = e.transpose(gt_ps[:, (qb % 4) * NE:((qb % 4) + 1) * NE],
                                             glog[0:NE, qb * 128:(qb + 1) * 128], idb[0:NE, 0:NE])
                        C.inc("pe", ii, ("gt", qgr))
                    for ob in range(NOB):
                        for qb in range(NQB):
                            idx = ob * NQB + qb
                            if idx >= 2:
                                C.wait(e, "dve", ("eevac", (idx - 2) // NQB, (idx - 2) % NQB))
                                C.wait(e, "dve", ("s2evac", (idx - 2) // NQB, (idx - 2) % NQB))
                            for i in range(NDCH):
                                if qb == 0:
                                    C.wait(e, "dve", ("wcast4", IDX4[("we", ob, ((i + 1) * 512 - 1) // (NDCH * 64))]))
                                ii = e.matmul(ep_ps[idx % 2][:, 0:512],
                                              z1fm[:, i * T + qb * 128: i * T + (qb + 1) * 128],
                                              wslice(web, i, 512), start=(i == 0), stop=(i == NDCH - 1))
                            C.inc("pe", ii, ("emm", ob, qb))
                            for i in range(NDCH):
                                if qb == 0:
                                    C.wait(e, "dve", ("wcast4", IDX4[("wd", ob, ((i + 1) * 512 - 1) // (NDCH * 64))]))
                                ii = e.matmul(dp_ps[idx % 2][:, 0:512],
                                              hfm[:, i * T + qb * 128: i * T + (qb + 1) * 128],
                                              wslice(wdb, i, 512), start=(i == 0), stop=(i == NDCH - 1))
                            C.inc("pe", ii, ("edmm", ob, qb))

                def p4_act(e, C):
                    for ob in range(NOB):
                        for j in range(qpo):
                            oc = ob * qpo + j
                            for qg in range(NQG):
                                C.wait(e, "pe", ("upmm", ob, j, qg))
                                ii = e.activation(hfm[:, oc * T + qg * QPG: oc * T + (qg + 1) * QPG],
                                                  up_ps[(oc * NQG + qg) % 2][:, 0:QPG], AF.Relu)
                                C.inc("act", ii, ("upevac", oc, qg))
                    for qg in range(NQG):
                        C.wait(e, "pe", ("gmm", qg))
                        ii = e.activation(glog[0:NE, qg * QPG:(qg + 1) * QPG],
                                          up_ps[qg % 2][0:NE, 0:QPG], AF.Copy)
                        C.inc("act", ii, ("gevac", qg))
                    for qb in range(NQB):
                        C.wait(e, "dve", ("dd", qb))
                        C.inc("act", e.activation(sg1[:, :], ddt[:, :], AF.Sigmoid), ("sg1", qb))
                        C.inc("act", e.activation(sg2[:, :], ddt[:, :], AF.Sigmoid, scale=-1.0), ("sg2", qb))
                    for qb in range(NQB):
                        C.wait(e, "dve", ("bnagg4", qb))
                        C.inc("act", e.activation(sd1[:, :], bna[:, 1:2], AF.Sqrt, bias=epst[:, :]), ("sqrt4", qb))

                def p4_dve(e, C):
                    for ob in range(NOB):
                        for hh in range(8):
                            p4_wcast(e, C, ("wu", ob, hh))
                    for qgr in range(NQB // 4):
                        C.wait(e, "pe", ("gt", qgr))
                        C.inc("dve", e.tensor_copy(glogt[:, qgr * 4 * NE:(qgr + 1) * 4 * NE],
                                                   gt_ps[:, :]), ("gtc", qgr))
                    for qb in range(NQB):
                        sl = glogt[:, qb * NE:(qb + 1) * NE]
                        C.wait(e, "dve", ("gtc", qb // 4))
                        if qb > 0:
                            C.wait(e, "dve", ("sfin", qb - 1))
                        C.inc("dve", e.tensor_reduce(l1t[:, :], sl, AX.X, ALU.max), ("l1", qb))
                        C.wait(e, "dve", ("l1", qb))
                        C.inc("dve", e.tensor_scalar(eqm[:, :], sl, l1t[:, :], negbig[:, :],
                                                     ALU.is_equal, ALU.mult), ("eqm", qb))
                        C.wait(e, "dve", ("eqm", qb))
                        C.inc("dve", e.tensor_tensor(mskt[:, :], eqm[:, :], sl, ALU.add), ("msk", qb))
                        C.wait(e, "dve", ("msk", qb))
                        C.inc("dve", e.tensor_reduce(l2t[:, :], mskt[:, :], AX.X, ALU.max), ("l2", qb))
                        C.wait(e, "dve", ("l2", qb))
                        C.inc("dve", e.tensor_tensor(ddt[:, :], l1t[:, :], l2t[:, :], ALU.subtract), ("dd", qb))
                        C.wait(e, "act", ("sg1", qb))
                        C.wait(e, "act", ("sg2", qb))
                        C.inc("dve", e.tensor_tensor(s_all[:, qb:qb + 1], sg1[:, :], sg2[:, :], ALU.add),
                              ("sfin", qb))
                    for ob in range(NOB):
                        for nm in ("we", "wd"):
                            for hh in range(8):
                                p4_wcast(e, C, (nm, ob, hh))
                        for qb in range(NQB):
                            idx = ob * NQB + qb
                            C.wait(e, "pe", ("emm", ob, qb))
                            C.wait(e, "dve", ("sfin", NQB - 1))
                            if idx > 0:
                                C.wait(e, "dve", ("s2evac", (idx - 1) // NQB, (idx - 1) % NQB))
                            C.inc("dve", e.tensor_scalar(t1[:, 0:512], ep_ps[idx % 2][:, 0:512],
                                                         s_all[:, qb:qb + 1], 0.0, ALU.mult, ALU.add),
                                  ("eevac", ob, qb))
                            C.wait(e, "pe", ("edmm", ob, qb))
                            C.wait(e, "dve", ("eevac", ob, qb))
                            if idx >= 2:
                                C.wait(e, "sg%d" % (idx % 2), ("s2out", (idx - 2) // NQB, (idx - 2) % NQB))
                            C.inc("dve", e.tensor_tensor(s2stg[idx % 2][:, 0:OBW], t1[:, 0:512],
                                                         dp_ps[idx % 2][:, 0:512], ALU.add),
                                  ("s2evac", ob, qb))
                            if ob == NOB - 1:
                                C.wait(e, "ss0", ("s2in", qb))
                                C.wait(e, "zs0", ("z1in", qb))
                                if qb > 0:
                                    C.wait(e, "dve", ("outc", qb - 1))
                                C.inc("dve", e.tensor_tensor(s2f[:, :], s2str[:, :],
                                                             z1str[:, :], ALU.add), ("s2f", qb))
                                C.wait(e, "dve", ("s2f", qb))
                                for j in range(NLC):
                                    ii = e.bn_stats(bns[:, 6 * j:6 * (j + 1)], s2f[:, 512 * j:512 * (j + 1)])
                                C.inc("dve", ii, ("bnst4", qb))
                                C.wait(e, "dve", ("bnst4", qb))
                                C.inc("dve", e.bn_aggr(bna[:, :], bns[:, :]), ("bnagg4", qb))
                                C.wait(e, "act", ("sqrt4", qb))
                                C.inc("dve", e.reciprocal(rstd[:, :], sd1[:, :]), ("rstd4", qb))
                                C.wait(e, "dve", ("rstd4", qb))
                                if qb >= 1:
                                    C.wait(e, "fo0", ("final", qb - 1))
                                C.inc("dve", e.tensor_scalar(ostg4[:, :], s2f[:, :], bna[:, 0:1],
                                                             rstd[:, :], ALU.subtract, ALU.mult), ("outc", qb))

                if phases >= 6:
                    run_phase(dict(sync=p4_sync, pe=p4_pe, act=p4_act, dve=p4_dve))

    if phases < 6:
        with nc.sbuf_tensor("stub", [128, D], F32) as stub, nc.semaphore("stubs") as sts, nc.Block() as blk:
            @blk.gpsimd
            def _(gpsimd):
                gpsimd.memset(stub[:, :], 0.0).then_inc(sts, 1)
            @blk.sync
            def _(sync):
                sync.wait_ge(sts, 1)
                for qb in range(NQB):
                    sync.dma_start(out_d[qb * 128:(qb + 1) * 128, :], stub[:, :]).then_inc(sts, 16)
                sync.wait_ge(sts, 1 + 16 * NQB)
    nc.finalize()
    return nc



_BUILD_CACHE = {}


def _get_program(key, cfg):
    if key not in _BUILD_CACHE:
        _BUILD_CACHE[key] = build(cfg)
    return _BUILD_CACHE[key]


def make_inputs(cfg, x, Wq, bq, Bq, Aq, Wk, bk, Bk, Ak, Wv, bv, Bv, Av,
                Wo, bo, Bo, Ao, W_up, W_down, W_gate, W_expert):
    NDCH, NQB, D, T, NH, NKC, NQG, NOB = shapes(cfg)
    f32 = lambda a: np.ascontiguousarray(np.asarray(a), dtype=np.float32)
    ballT = np.zeros((D, 80), np.float32)
    ballT[:, 0:16] = np.asarray(Bq).T
    ballT[:, 32:48] = np.asarray(Bk).T
    ballT[:, 64:80] = np.asarray(Bv).T
    def tile_col(WT):
        return np.ascontiguousarray(
            np.asarray(WT).reshape(NDCH, 128, NDCH, 128).transpose(2, 1, 0, 3).reshape(D, D))

    def tile_slab(Wdo):
        return np.ascontiguousarray(
            np.asarray(Wdo).reshape(NDCH, 128, NOB, 512 if D >= 512 else D)
            .transpose(2, 1, 0, 3).reshape(NOB * 128, NDCH * (512 if D >= 512 else D)))

    shared = dict(
        wqT=tile_col(np.asarray(Wq).T), wkT=tile_col(np.asarray(Wk).T),
        wvT=tile_slab(np.asarray(Wv).T), woT=tile_slab(np.asarray(Wo).T),
        wupT=tile_slab(np.asarray(W_up).T), wdnT=tile_slab(np.asarray(W_down).T),
        wex=tile_slab(W_expert), wgT=f32(np.asarray(W_gate).T), ballT=f32(ballT),
        aqT=f32(np.asarray(Aq).T), akT=f32(np.asarray(Ak).T), avT=f32(np.asarray(Av).T),
        aoT=f32(np.asarray(Ao).T), boT=f32(np.asarray(Bo).T),
        bqr=f32(np.asarray(bq).reshape(NDCH, 128).T), bkr=f32(np.asarray(bk).reshape(NDCH, 128).T),
        bor=f32(np.asarray(bo).reshape(1, D)), bvr=f32(np.asarray(bv).reshape(1, D)),
        idb=np.eye(128, dtype=ml_dtypes.bfloat16),
    )
    in_maps = []
    for c in range(8):
        b, s = c // NR, (c % NR) * T
        xs = np.asarray(x)[b, s:s + T, :]
        m = dict(shared)
        m["xtm"] = f32(xs)
        m["xfm"] = f32(xs.T)
        in_maps.append(m)
    return in_maps


def kernel(**inputs):
    cfg = CFG_FULL
    NDCH, NQB, D, T, NH, NKC, NQG, NOB = shapes(cfg)
    x = np.asarray(inputs["L0_attn_x"], np.float32)
    args = [x]
    for p in ("Wq", "bq", "Bq", "Aq", "Wk", "bk", "Bk", "Ak", "Wv", "bv", "Bv", "Av",
              "Wo", "bo", "Bo", "Ao"):
        args.append(np.asarray(inputs["L0_attn_" + p], np.float32))
    for p in ("W_up", "W_down", "W_gate", "W_expert"):
        args.append(np.asarray(inputs["L0_ffn_" + p], np.float32))
    in_maps = make_inputs(cfg, *args)
    nc = _get_program("full", cfg)
    res = run_bass_kernel_spmd(nc, in_maps, core_ids=list(range(8)))
    out = np.empty((2, 8192, 2048), np.float32)
    for c in range(8):
        b, s = c // NR, (c % NR) * T
        out[b, s:s + T, :] = res.results[c]["out"]
    return out
